# version 39
# baseline (speedup 1.0000x reference)
import numpy as np
import concourse.bass as bass
import concourse.mybir as mybir
from concourse.bass_utils import run_bass_kernel_spmd

F32 = mybir.dt.float32
BF16 = mybir.dt.bfloat16
I32 = mybir.dt.int32
AF = mybir.ActivationFunctionType
OP = mybir.AluOpType
AX = mybir.AxisListType

T, D, N, H, R = 4096, 1024, 16, 4096, 64
LC = 256
NCH = T // LC
TB = LC // 128
DT = D // 128
HT = H // 128
NU = N * DT
EPS = 1e-5
MAGIC = 12582912.0
INV_D = 1.0 / D


class Sem:
    def __init__(self, nc, es, name):
        self.h = es.enter_context(nc.semaphore(name))
        self.n = 0


class Emitter:
    ENGS = ("sync", "tensor", "vector", "scalar", "gpsimd")

    def __init__(self, self_sync_sems, use_drain=False):
        self.ops = {e: [] for e in self.ENGS}
        self.waited = {e: {} for e in self.ENGS}
        self.seq = 0
        self.self_sync = self_sync_sems
        self.use_drain = use_drain

    def op(self, eng, body, waits=(), incs=(), self_sync=True):
        own = self.self_sync.get(eng) if self_sync else None
        wl = []
        drain = False
        if own is not None and own.n > 0:
            if self.use_drain and eng != "tensor":
                drain = True
            else:
                wl.append((own.h, own.n))
                self.waited[eng][id(own)] = max(
                    self.waited[eng].get(id(own), 0), own.n)
        for s, v in waits:
            if v is None or v <= 0:
                continue
            key = id(s)
            if v > self.waited[eng].get(key, 0):
                wl.append((s.h, v))
                self.waited[eng][key] = v
        il = []
        own_inced = False
        for s, k in incs:
            s.n += k
            il.append((s.h, k))
            if s is own:
                own_inced = True
        if own is not None and not own_inced:
            own.n += 1
            il.append((own.h, 1))
        self.ops[eng].append((wl, body, il, self.seq, drain))
        self.seq += 1

    def flush(self, block, max_seq=None):
        def mk(lst):
            def run(eng):
                for wl, body, il, sq_, drain in lst:
                    if max_seq is not None and sq_ > max_seq:
                        continue
                    if drain:
                        eng.drain()
                    for h, v in wl:
                        eng.wait_ge(h, v)
                    ins = body(eng)
                    for h, k in il:
                        ins.then_inc(h, k)
            return run
        block.sync(mk(self.ops["sync"]))
        block.tensor(mk(self.ops["tensor"]))
        block.vector(mk(self.ops["vector"]))
        block.scalar(mk(self.ops["scalar"]))
        block.gpsimd(mk(self.ops["gpsimd"]))


def build_nc(full_sync=False, nch=NCH, max_seq=None):
    nc = bass.Bass("TRN2", target_bir_lowering=False,
                   detect_race_conditions=full_sync)

    u_d = nc.dram_tensor("u", [T, D], F32, kind="ExternalInput")
    o_d = nc.dram_tensor("o", [T, D], F32, kind="ExternalOutput")
    wxT_d = nc.dram_tensor("wxT", [D, 96], F32, kind="ExternalInput")
    wdtTb_d = nc.dram_tensor("wdtTb", [R + 1, D], F32, kind="ExternalInput")
    logA_d = nc.dram_tensor("logA_r", [128, DT * N], F32, kind="ExternalInput")
    Dp_d = nc.dram_tensor("D_r", [128, DT], F32, kind="ExternalInput")
    gam_d = nc.dram_tensor("gamma_r", [128, DT], F32, kind="ExternalInput")
    bet_d = nc.dram_tensor("beta_r", [128, DT], F32, kind="ExternalInput")
    b2_d = nc.dram_tensor("b2_r", [128, DT], F32, kind="ExternalInput")
    b1_d = nc.dram_tensor("b1_r", [128, HT], F32, kind="ExternalInput")
    w1T_d = nc.dram_tensor("w1T", [D, H], F32, kind="ExternalInput")
    w2T_d = nc.dram_tensor("w2T", [H, D], F32, kind="ExternalInput")
    eye_d = nc.dram_tensor("eye", [128, 128], F32, kind="ExternalInput")
    onr_d = nc.dram_tensor("ones_row", [1, 128], F32, kind="ExternalInput")
    onc_d = nc.dram_tensor("ones_col", [128, 1], F32, kind="ExternalInput")
    sgn_d = nc.dram_tensor("sgnm", [128, 1], I32, kind="ExternalInput")
    cst_d = nc.dram_tensor("cst", [128, 2], F32, kind="ExternalInput")

    wq1_d = nc.dram_tensor("wq1d", [HT * 128 * 1024], BF16, kind="Internal")
    bc_d = nc.dram_tensor("bcd", [2 * 32 * LC], F32, kind="Internal")
    scl_d = nc.dram_tensor("scld", [8], F32, kind="Internal")

    from contextlib import ExitStack
    es = ExitStack()
    with es:
        sb = nc.sbuf_tensor
        wq2 = es.enter_context(sb("wq2", [128, HT * 1024], BF16))
        g = es.enter_context(sb("g", [128, HT * LC], F32))
        uT = es.enter_context(sb("uT", [128, DT * LC], F32))
        xnT = es.enter_context(sb("xnT", [128, DT * LC], F32))
        dl = es.enter_context(sb("dl", [128, DT * LC], F32))
        wt = es.enter_context(sb("wt", [128, DT * LC], F32))
        ya = es.enter_context(sb("ya", [128, DT * LC], F32))
        xqT = es.enter_context(sb("xqT", [128, DT * LC], BF16))
        utok = es.enter_context(sb("utok", [128, D], F32))
        outst = es.enter_context(sb("outst", [128, D], F32))
        wq1s = es.enter_context(sb("wq1s", [128, 2 * 1024], BF16))
        bcB = es.enter_context(sb("bcB", [128, 2 * 512], F32))
        bcC = es.enter_context(sb("bcC", [128, 2 * 512], F32))
        a_t = es.enter_context(sb("a_t", [128, 2 * 514], F32))
        b_t = es.enter_context(sb("b_t", [128, 2 * 514], F32))
        h_t = es.enter_context(sb("h_t", [128, 514], F32))
        ptree = es.enter_context(sb("ptree", [128, 256], F32))
        dsb = es.enter_context(sb("dsb", [128, LC], F32))
        bcsb = es.enter_context(sb("bcsb", [32, LC], F32))
        sq = es.enter_context(sb("sq", [128, 2 * LC], F32))
        sp1 = es.enter_context(sb("sp1", [128, 1024], F32))
        sp2 = es.enter_context(sb("sp2", [128, 1024], F32))
        pre = es.enter_context(sb("pre", [128, 2 * LC], F32))
        mrun = es.enter_context(sb("mrun", [128, LC], F32))
        mnrun = es.enter_context(sb("mnrun", [128, LC], F32))
        c1s = es.enter_context(sb("c1s", [128, LC], F32))
        scs = es.enter_context(sb("scs", [128, LC], F32))
        c2s = es.enter_context(sb("c2s", [128, LC], F32))
        f1 = es.enter_context(sb("f1", [128, LC], F32))
        f2 = es.enter_context(sb("f2", [128, LC], F32))
        stt_sb = es.enter_context(sb("stt", [128, NU], F32))
        wxT = es.enter_context(sb("wxTs", [128, DT * 96], F32))
        wdt = es.enter_context(sb("wdts", [R + 1, D], F32))
        A_sb = es.enter_context(sb("A_sb", [128, DT * N], F32))
        Dg = es.enter_context(sb("Dg", [128, DT], F32))
        gam = es.enter_context(sb("gam", [128, DT], F32))
        bet = es.enter_context(sb("bet", [128, DT], F32))
        b2c = es.enter_context(sb("b2c", [128, DT], F32))
        b1c = es.enter_context(sb("b1c", [128, HT], F32))
        eye = es.enter_context(sb("eyes", [128, 128], F32))
        onr = es.enter_context(sb("onr", [1, 128], F32))
        onc = es.enter_context(sb("onc", [128, 1], F32))
        sgn = es.enter_context(sb("sgn", [128, 1], I32))
        mwc = es.enter_context(sb("mwc", [128, 4], F32))
        acc = es.enter_context(sb("acc", [128, 4], F32))
        cst = es.enter_context(sb("cst_s", [128, 2], F32))
        rows = es.enter_context(sb("rows", [1, 5 * LC], F32))
        RW = lambda i: rows[0:1, i * LC:(i + 1) * LC]

        ps = nc.psum_tensor
        ps_tr = es.enter_context(ps("ps_tr", [128, 512], F32))
        ps_xd = es.enter_context(ps("ps_xd", [128, 512], F32))
        ps_d = es.enter_context(ps("ps_d", [128, 512], F32))
        ps_b1 = es.enter_context(ps("ps_b1", [128, 512], F32))
        ps_b2 = es.enter_context(ps("ps_b2", [128, 512], F32))
        ps_b3 = es.enter_context(ps("ps_b3", [128, 512], F32))
        ps_m1 = es.enter_context(ps("ps_m1", [128, 512], F32))
        ps_m2 = es.enter_context(ps("ps_m2", [128, 512], F32))

        sems = {k: Sem(nc, es, "s_" + k) for k in
                ("ld", "du", "dw", "dwq", "dbc", "dbw", "dsc", "do",
                 "pe", "act", "dve", "pool", "dbc1", "dw1", "am0", "am1")}
        s_am0, s_am1 = sems["am0"], sems["am1"]
        s_ld, s_du, s_dw, s_dwq = (sems[k] for k in ("ld", "du", "dw", "dwq"))
        s_dbc, s_dbw, s_dsc, s_do = (sems[k] for k in ("dbc", "dbw", "dsc", "do"))
        s_dbc1, s_dw1 = sems["dbc1"], sems["dw1"]
        s_pe, s_act, s_dve, s_pool = (sems[k] for k in ("pe", "act", "dve", "pool"))

        nc.const_aps.register(F32, MAGIC, cst[:, 0:1])
        nc.const_aps.register(F32, EPS, cst[:, 1:2])

        self_sync = {"vector": s_dve, "scalar": s_act, "gpsimd": s_pool}
        if full_sync:
            self_sync["tensor"] = s_pe
        em = Emitter(self_sync, use_drain=not full_sync)
        E = em.op

        uT3 = uT[:].rearrange("p (dt l) -> p dt l", dt=DT)
        tr3 = ps_tr[:].rearrange("p (j l) -> p j l", j=4)

        for dst, src in ((wxT[:], bass.AP(wxT_d, 0, [[96, 128], [96 * 128, DT], [1, 96]])),
                         (wdt[:], wdtTb_d.ap()),
                         (A_sb[:], logA_d.ap()),
                         (Dg[:], Dp_d.ap()), (gam[:], gam_d.ap()),
                         (bet[:], bet_d.ap()), (b2c[:], b2_d.ap()),
                         (b1c[:], b1_d.ap()), (eye[:], eye_d.ap()),
                         (onr[:], onr_d.ap()), (onc[:], onc_d.ap()),
                         (sgn[:], sgn_d.ap()),
                         (cst[:], cst_d.ap())):
            E("sync", lambda e, d_=dst, s_=src: e.dma_start(out=d_, in_=s_),
              incs=[(s_ld, 16)])
        ld_all = s_ld.n

        E("scalar", lambda e: e.activation(sp1[:, 0:DT * N], A_sb[:], AF.Exp),
          waits=[(s_ld, ld_all)], incs=[(s_act, 1)])
        E("vector", lambda e: e.tensor_scalar_mul(A_sb[:], sp1[:, 0:DT * N], -1.0),
          waits=[(s_act, s_act.n)], incs=[(s_dve, 1)])
        E("vector", lambda e: e.memset(dsb[64:65, :], 1.0), incs=[(s_dve, 1)])
        E("vector", lambda e: e.memset(a_t[:], 0.0), incs=[(s_dve, 1)])
        E("vector", lambda e: e.memset(stt_sb[:], 0.0), incs=[(s_dve, 1)])

        stg = g[:, 4096:8192]
        stg2 = g[:, 0:4096]
        wq_bf = g[:].bitcast(BF16)[0:128, 8192:12288]

        def wblk_ap(is_w1, blk):
            if is_w1:
                return bass.AP(w1T_d, blk * 512, [[H, 128], [H * 128, DT], [1, 512]])
            return bass.AP(w2T_d, blk * 4 * 128 * D, [[D, 128], [D * 128, 4], [1, D]])

        def w_half_ap(is_w1, hb):
            if is_w1:
                return bass.AP(w1T_d,
                               (hb // 2) * 512 + (hb % 2) * (H * 128 * 4),
                               [[H, 128], [H * 128, 4], [1, 512]])
            return bass.AP(w2T_d, hb * (D * 128 * 2),
                           [[D, 128], [D * 128, 2], [1, D]])

        def w_absmean_dual():
            am1 = [0, 0]
            am2 = [0, 0]
            sems1 = (s_dw, s_dw1)
            sems2 = (s_am0, s_am1)
            for hb in range(16):
                sl = hb % 2
                b1 = stg[:, sl * 2048:(sl + 1) * 2048]
                E("sync", lambda e, a=w_half_ap(True, hb), b_=b1: e.dma_start(
                    out=b_, in_=a),
                  waits=[(s_dve, am1[sl])], incs=[(sems1[sl], 16)])
                d1 = sems1[sl].n
                E("vector", lambda e, b_=b1: e.tensor_reduce(
                    acc[:, 1:2], b_, AX.X, OP.add, apply_absolute_value=True),
                  waits=[(sems1[sl], d1)])
                if hb == 0:
                    E("vector", lambda e: e.tensor_copy(acc[:, 0:1],
                                                        acc[:, 1:2]),
                      incs=[(s_dve, 1)])
                else:
                    E("vector", lambda e: e.tensor_tensor(
                        acc[:, 0:1], acc[:, 0:1], acc[:, 1:2], OP.add),
                      incs=[(s_dve, 1)])
                am1[sl] = s_dve.n
                b2_ = uT[:] if sl == 0 else xnT[:]
                E("gpsimd", lambda e, a=w_half_ap(False, hb), b_=b2_:
                  e.dma_start(out=b_, in_=a),
                  waits=[(s_dve, am2[sl])],
                  incs=[(sems2[sl], 16)], self_sync=False)
                d2 = sems2[sl].n
                E("vector", lambda e, b_=b2_: e.tensor_reduce(
                    acc[:, 3:4], b_, AX.X, OP.add, apply_absolute_value=True),
                  waits=[(sems2[sl], d2)])
                if hb == 0:
                    E("vector", lambda e: e.tensor_copy(acc[:, 2:3],
                                                        acc[:, 3:4]),
                      incs=[(s_dve, 1)])
                else:
                    E("vector", lambda e: e.tensor_tensor(
                        acc[:, 2:3], acc[:, 2:3], acc[:, 3:4], OP.add),
                      incs=[(s_dve, 1)])
                am2[sl] = s_dve.n
            return am1, max(am2)

        def w_scale_finish(col, acol):
            E("tensor", lambda e, ac=acol: e.matmul(
                ps_tr[0:1, 0:1], onc[:], acc[:, ac:ac + 1],
                start=True, stop=True),
              waits=[(s_dve, s_dve.n), (s_ld, ld_all)], incs=[(s_pe, 1)])
            pev = s_pe.n
            E("vector", lambda e: e.tensor_scalar(
                RW(3)[:, 0:1], ps_tr[0:1, 0:1], 1.0 / (D * H), EPS, OP.mult, OP.max),
              waits=[(s_pe, pev), (s_dsc, s_dsc.n)])
            E("vector", lambda e: e.reciprocal(RW(3)[:, 1:2], RW(3)[:, 0:1]),
              incs=[(s_dve, 1)])
            dvv = s_dve.n
            E("sync", lambda e, c=col: e.dma_start(
                out=bass.AP(scl_d, 2 * c, [[1, 1], [1, 2]]), in_=RW(3)[0:1, 0:2]),
              waits=[(s_dve, dvv), (s_dsc, s_dsc.n)], incs=[(s_dsc, 16)])
            dscv = s_dsc.n
            E("sync", lambda e, c=col: e.dma_start(
                out=mwc[:, 2 * c:2 * c + 2],
                in_=bass.AP(scl_d, 2 * c, [[0, 128], [1, 2]])),
              waits=[(s_dsc, dscv)], incs=[(s_dsc, 16)])

        def emit_quant(am1_slots):
            sw1_col = mwc[:, 1:2]
            sw2_col = mwc[:, 3:4]
            mwrdy = s_dsc.n
            yabf = ya[:].bitcast(BF16)
            qsems = (s_dw, s_dw1)
            actm = [0, 0]
            dvem = [0, 0]
            inm = list(am1_slots)
            dwqm = s_dwq.n
            for idx in range(32):
                is_w1 = idx < 16
                hb = idx % 16
                blk, h2 = hb // 2, hb % 2
                sl = idx % 2
                sin = stg[:, sl * 2048:(sl + 1) * 2048]
                ssc = stg2[:, sl * 2048:(sl + 1) * 2048]
                E("sync", lambda e, a=w_half_ap(is_w1, hb), b_=sin:
                  e.dma_start(out=b_, in_=a),
                  waits=[(s_act, actm[sl]), (s_dve, inm[sl])],
                  incs=[(qsems[sl], 16)])
                dw = qsems[sl].n
                E("scalar", lambda e, i_=sin, o_=ssc,
                  sc=(sw1_col if is_w1 else sw2_col): e.activation(
                    o_, i_, AF.Identity, bias=MAGIC, scale=sc),
                  waits=[(qsems[sl], dw), (s_dve, dvem[sl]), (s_dsc, mwrdy)],
                  incs=[(s_act, 1)])
                actm[sl] = s_act.n
                E("vector", lambda e, o_=ssc: e.tensor_scalar(
                    o_, o_, MAGIC, 1.0, OP.subtract, OP.min),
                  waits=[(s_act, actm[sl])])
                if is_w1:
                    out = bass.AP(yabf.tensor, yabf.offset + h2 * 512,
                                  [[4096, 128], [128, 4], [1024, 4], [1, 128]])
                    E("vector", lambda e, o_=out, i_=ssc:
                      e.tensor_single_scalar(o_, i_, -1.0, OP.max),
                      waits=[(s_dwq, dwqm)], incs=[(s_dve, 1)])
                else:
                    o2 = wq2[:, blk * 4096 + h2 * 2048:
                             blk * 4096 + h2 * 2048 + 2048]
                    E("vector", lambda e, o_=o2, i_=ssc:
                      e.tensor_single_scalar(o_, i_, -1.0, OP.max),
                      incs=[(s_dve, 1)])
                dvem[sl] = s_dve.n
                if is_w1 and h2 == 1:
                    E("sync", lambda e, b=blk, dv=dvem[sl]: e.dma_start(
                        out=bass.AP(wq1_d, b * 4 * 131072,
                                    [[1024, 128], [131072, 4], [1, 1024]]),
                        in_=yabf.rearrange("p (hb f) -> p hb f", hb=4)),
                      waits=[(s_dve, dvem[sl])], incs=[(s_dwq, 16)])
                    dwqm = s_dwq.n

        am1_done, am2_done = w_absmean_dual()
        w_scale_finish(0, 0)
        w_scale_finish(1, 2)
        mw_ready = s_dsc.n
        wq_done_dve = 0
        wq1_written = 0

        NUP = 64
        unit_dve = [0] * NUP
        unit_pool = [0] * NUP

        def stp_slice(stp, i_):
            return bass.AP(stp.tensor, stp.offset + i_, [list(stp.ap[0]), [1, 2]])

        atp = a_t[:]
        btp = b_t[:]
        htp = h_t[:]
        wtp = wt[:]
        bBp = bcB[:]
        bCp = bcC[:]
        stp = stt_sb[:]
        sqa = sq[:]
        rep8_0 = bass.AP(sqa.tensor, sqa.offset, [list(sqa.ap[0]), [0, 8], [1, LC]])
        rep8_1 = bass.AP(sqa.tensor, sqa.offset + LC,
                         [list(sqa.ap[0]), [0, 8], [1, LC]])
        xnT3d = xnT[:].rearrange("p (r f) -> p r f", r=8)
        uT3d = uT[:].rearrange("p (r f) -> p r f", r=8)
        scsa = scs[:]
        screp8 = bass.AP(scsa.tensor, scsa.offset, [list(scsa.ap[0]), [0, 8], [1, LC]])
        screp4 = bass.AP(scsa.tensor, scsa.offset, [list(scsa.ap[0]), [0, 4], [1, LC]])
        hq = g[:].bitcast(BF16)
        ZB = 4096

        roll = {
            "outdma": 0, "bcw": (s_dbw, 0), "wln_dve": wq_done_dve,
            "mm1_pe": 0,
            "sq_pe": 0, "c2s_act": 0, "evxd_act": 0, "pre2": [0, 0],
            "psd2": [0, 0], "ln2m": [0, 0], "round_all": 0, "outtr_pe": 0,
            "uevac_act": 0, "tailevac_dve": 0, "mm1war2": [0, 0],
            "prep": None, "xq_dve": 0, "stage2_done": 0,
            "z_dve": 0, "xn_dve": am2_done,
        }

        def scale_rows(src_dve_mark, mw_off, sp_ln_all):
            dvv_ = src_dve_mark
            for k in range(TB):
                E("tensor", lambda e, kk=k: e.transpose(
                    ps_tr[:, kk * 128:(kk + 1) * 128],
                    mrun[:, kk * 128:(kk + 1) * 128], eye[:]),
                  waits=[(s_dve, dvv_),
                         (s_act, max(sp_ln_all, roll["uevac_act"]))])
                E("tensor", lambda e, kk=k: e.transpose(
                    ps_tr[:, 256 + kk * 128:256 + (kk + 1) * 128],
                    mnrun[:, kk * 128:(kk + 1) * 128], eye[:]),
                  incs=[(s_pe, 1)] if k == TB - 1 else ())
            for k in range(TB):
                E("vector", lambda e, kk=k: e.tensor_reduce(
                    acc[:, 1:2], ps_tr[:, kk * 128:(kk + 1) * 128], AX.X, OP.max),
                  waits=[(s_pe, s_pe.n)])
                E("vector", lambda e, kk=k: e.tensor_reduce(
                    acc[:, 0:1], ps_tr[:, 256 + kk * 128:256 + (kk + 1) * 128],
                    AX.X, OP.max))
                E("vector", lambda e: e.tensor_tensor(
                    acc[:, 1:2], acc[:, 1:2], acc[:, 0:1], OP.max))
                E("vector", lambda e: e.tensor_scalar_max(acc[:, 1:2], acc[:, 1:2],
                                                          EPS))
                E("vector", lambda e: e.reciprocal(acc[:, 0:1], acc[:, 1:2]))
                E("vector", lambda e: e.tensor_scalar_mul(acc[:, 2:3], acc[:, 0:1],
                                                          127.0))
                E("vector", lambda e, mo=mw_off: e.tensor_scalar(
                    acc[:, 3:4], acc[:, 1:2], mwc[:, mo:mo + 1], 1.0 / 127.0,
                    OP.mult, OP.mult),
                  waits=[(s_dsc, mw_ready)], incs=[(s_dve, 1)])
                dv2 = s_dve.n
                E("tensor", lambda e, kk=k: e.transpose(
                    ps_tr[0:1, kk * 128:(kk + 1) * 128], acc[:, 2:3], eye[:]),
                  waits=[(s_dve, dv2)])
                E("tensor", lambda e, kk=k: e.transpose(
                    ps_tr[0:1, 256 + kk * 128:256 + (kk + 1) * 128],
                    acc[:, 3:4], eye[:]),
                  incs=[(s_pe, 1)])
            pe_mark = s_pe.n
            E("vector", lambda e: e.tensor_copy(RW(3)[:], ps_tr[0:1, 0:LC]),
              waits=[(s_pe, pe_mark)])
            E("vector", lambda e: e.tensor_copy(RW(4)[:], ps_tr[0:1, 256:512]),
              incs=[(s_dve, 1)])
            return s_dve.n, pe_mark

        def prepA_steps(c):
            A = {"uevac_act": 0, "stats1_pe": 0, "pev": 0,
                 "sqm": [roll["sq_pe"], roll["sq_pe"]]}

            def mk_k(k):
                def fn():
                    E("sync", lambda e, kk=k, cc=c: e.dma_start(
                        out=utok[:],
                        in_=bass.AP(u_d, (cc * LC + kk * 128) * D,
                                    [[D, 128], [1, D]])),
                      waits=[(s_pe, A["pev"]),
                             (s_dve, roll["tailevac_dve"])],
                      incs=[(s_du, 16)])
                    duv = s_du.n
                    for half in range(2):
                        for j in range(4):
                            dt_ = half * 4 + j
                            E("tensor", lambda e, d_=dt_, jj=j: e.transpose(
                                ps_tr[:, jj * 128:(jj + 1) * 128],
                                utok[:, d_ * 128:(d_ + 1) * 128], eye[:]),
                              waits=[(s_du, duv), (s_ld, ld_all),
                                     (s_act, s_act.n), (s_dve, s_dve.n)],
                              incs=[(s_pe, 1)] if j == 3 else ())
                        pev = s_pe.n
                        E("scalar", lambda e, kk=k, hh=half: e.activation(
                            uT3[:, hh * 4:(hh + 1) * 4, kk * 128:(kk + 1) * 128],
                            tr3[:, :, :], AF.Identity),
                          waits=[(s_pe, pev), (s_dve, roll["xn_dve"])],
                          incs=[(s_act, 1)])
                    A["pev"] = s_pe.n
                    A["uevac_act"] = s_act.n
                    roll["uevac_act"] = s_act.n
                return fn

            def mk_stats(dlo, dhi):
                def fn():
                    sqm = A["sqm"]
                    for dt_ in range(dlo, dhi):
                        E("scalar", lambda e, d_=dt_: e.activation(
                            sq[:, (d_ % 2) * LC:(d_ % 2 + 1) * LC],
                            uT3[:, d_:d_ + 1, :], AF.Square),
                          waits=[(s_act, A["uevac_act"]), (s_pe, sqm[dt_ % 2]),
                                 (s_dve, roll["z_dve"])],
                          incs=[(s_act, 1)])
                        acv = s_act.n
                        E("tensor", lambda e, d_=dt_: e.matmul(
                            ps_b2[0:1, 0:LC], onc[:], uT3[:, d_:d_ + 1, :],
                            start=(d_ == 0), stop=(d_ == DT - 1)),
                          waits=[(s_act, A["uevac_act"]),
                                 (s_dve, roll["wln_dve"])])
                        E("tensor", lambda e, d_=dt_: e.matmul(
                            ps_b2[32:33, 0:LC], onc[:],
                            sq[:, (d_ % 2) * LC:(d_ % 2 + 1) * LC],
                            start=(d_ == 0), stop=(d_ == DT - 1)),
                          waits=[(s_act, acv)], incs=[(s_pe, 1)])
                        sqm[dt_ % 2] = s_pe.n
                    A["stats1_pe"] = s_pe.n
                return fn

            return A, [mk_k(0), mk_k(1), mk_stats(0, 4), mk_stats(4, DT)]

        def emit_prep(c, A, tailcb=None):
            cmod = c % 2
            M = {}
            cb = tailcb if tailcb is not None else (lambda: None)
            ut_evac_act = A["uevac_act"]
            M["uevac_act"] = ut_evac_act
            stats1_pe = A["stats1_pe"]
            cb()
            E("vector", lambda e: e.tensor_scalar_mul(RW(0)[:], ps_b2[0:1, 0:LC],
                                                      INV_D),
              waits=[(s_pe, stats1_pe)], incs=[(s_dve, 1)])
            rw0_dve = s_dve.n
            E("scalar", lambda e: e.activation(RW(3)[:], RW(0)[:], AF.Square),
              waits=[(s_pe, stats1_pe), (s_dve, rw0_dve if c == 0 else s_dve.n),
                     (s_dsc, mw_ready if c == 0 else 0)],
              incs=[(s_act, 1)])
            acv = s_act.n
            E("vector", lambda e: e.scalar_tensor_tensor(
                RW(3)[:], ps_b2[32:33, 0:LC], INV_D, RW(3)[:], OP.mult, OP.subtract),
              waits=[(s_act, acv)], incs=[(s_dve, 1)])
            dvv = s_dve.n
            E("scalar", lambda e: e.activation(RW(3)[:], RW(3)[:], AF.Ln, bias=EPS),
              waits=[(s_dve, dvv)])
            E("scalar", lambda e: e.activation(RW(1)[:], RW(3)[:], AF.Exp, scale=-0.5),
              incs=[(s_act, 1)])
            acv = s_act.n
            E("vector", lambda e: e.scalar_tensor_tensor(
                RW(2)[:], RW(0)[:], -1.0, RW(1)[:], OP.mult, OP.mult),
              waits=[(s_act, acv)], incs=[(s_dve, 1)])
            rows1_dve = s_dve.n
            E("tensor", lambda e: e.matmul(ps_b1[:, 0:LC], onr[:], RW(1)[:],
                                           start=True, stop=True),
              waits=[(s_dve, rows1_dve)])
            E("tensor", lambda e: e.matmul(ps_b1[:, 256:512], onr[:], RW(2)[:],
                                           start=True, stop=True),
              incs=[(s_pe, 1)])
            bc1_pe = s_pe.n
            cb()
            E("scalar", lambda e: e.activation(sq[:, 0:LC], ps_b1[:, 0:LC],
                                               AF.Identity),
              waits=[(s_pe, bc1_pe), (s_dve, 0 if c == 0 else s_dve.n)])
            E("scalar", lambda e: e.activation(sq[:, LC:2 * LC], ps_b1[:, 256:512],
                                               AF.Identity),
              incs=[(s_act, 1)])
            acv = s_act.n
            E("vector", lambda e: e.tensor_tensor(xnT3d, uT3d, rep8_0, OP.mult),
              waits=[(s_act, acv), (s_act, ut_evac_act)])
            E("vector", lambda e: e.tensor_tensor(xnT3d, xnT3d, rep8_1, OP.add),
              incs=[(s_dve, 1)])
            xn_dve = s_dve.n
            roll["xn_dve"] = xn_dve
            cb()
            for dt_ in range(DT):
                E("tensor", lambda e, d_=dt_: e.matmul(
                    ps_xd[0:96, 0:LC], wxT[:, d_ * 96:(d_ + 1) * 96],
                    xnT[:, d_ * LC:(d_ + 1) * LC],
                    start=(d_ == 0), stop=(d_ == DT - 1)),
                  waits=[(s_dve, xn_dve), (s_act, roll["evxd_act"])],
                  incs=[(s_pe, 1)] if dt_ == DT - 1 else ())
            xd_pe = s_pe.n
            E("scalar", lambda e: e.activation(dsb[0:64, :], ps_xd[0:64, 0:LC],
                                               AF.Identity),
              waits=[(s_pe, xd_pe), (s_dve, 0 if c == 0 else s_dve.n)],
              incs=[(s_act, 1)])
            E("scalar", lambda e: e.activation(bcsb[0:32, :], ps_xd[64:96, 0:LC],
                                               AF.Identity),
              waits=[roll["bcw"]], incs=[(s_act, 1)])
            evac_xd_act = s_act.n
            roll["evxd_act"] = evac_xd_act
            bwsem = s_am0 if c == 0 else s_dbw
            E("gpsimd" if c == 0 else "sync", lambda e, cm=cmod: e.dma_start(
                out=bass.AP(bc_d, cm * 32 * LC, [[LC, 32], [1, LC]]), in_=bcsb[:]),
              waits=[(s_act, evac_xd_act), (s_dbc, s_dbc.n), (s_dbc1, s_dbc1.n)],
              incs=[(bwsem, 16)], self_sync=False)
            roll["bcw"] = (bwsem, bwsem.n)
            M["bcw_now"] = (bwsem, bwsem.n)
            cb()
            psd2 = list(roll["psd2"])
            ln2m = list(roll["ln2m"])
            for dt_ in range(DT):
                dps = ps_d[:, 0:LC] if dt_ % 2 == 0 else ps_xd[:, 256:256 + LC]
                so = (dt_ % 2) * LC
                E("tensor", lambda e, d_=dt_, pp=dps: e.matmul(
                    pp, wdt[:, d_ * 128:(d_ + 1) * 128],
                    dsb[0:65, :], start=True, stop=True),
                  waits=[(s_act, evac_xd_act), (s_dve, psd2[dt_ % 2])],
                  incs=[(s_pe, 1)])
                pev = s_pe.n
                E("vector", lambda e, pp=dps, s_=so: e.tensor_single_scalar(
                    sp1[:, s_:s_ + LC].bitcast(I32), pp.bitcast(I32),
                    sgn[:, 0:1], OP.bitwise_or),
                  waits=[(s_pe, pev), (s_act, ln2m[dt_ % 2])],
                  incs=[(s_dve, 1)])
                dvv = s_dve.n
                E("scalar", lambda e, s_=so: e.activation(
                    sp2[:, s_:s_ + LC], sp1[:, s_:s_ + LC], AF.Exp),
                  waits=[(s_dve, dvv)])
                E("scalar", lambda e, s_=so: e.activation(
                    sp1[:, s_:s_ + LC], sp2[:, s_:s_ + LC], AF.Ln, bias=1.0),
                  incs=[(s_act, 1)])
                ln2m[dt_ % 2] = s_act.n
                acv = s_act.n
                E("vector", lambda e, d_=dt_, pp=dps, s_=so: e.scalar_tensor_tensor(
                    dl[:, d_ * LC:(d_ + 1) * LC], pp, 0.0,
                    sp1[:, s_:s_ + LC], OP.max, OP.add),
                  waits=[(s_act, acv)], incs=[(s_dve, 1)])
                psd2[dt_ % 2] = s_dve.n
                if dt_ % 2 == 1:
                    cb()
            roll["psd2"] = psd2
            roll["ln2m"] = ln2m
            M["sp_ln_all"] = max(ln2m)
            M["psd2"] = list(psd2)
            E("vector", lambda e: e.tensor_tensor(wt[:], dl[:], xnT[:], OP.mult),
              incs=[(s_dve, 1)])
            M["dl_dve"] = s_dve.n
            M["wt_dve"] = s_dve.n
            sqm2 = [stats1_pe, stats1_pe]
            for dt_ in range(DT):
                E("scalar", lambda e, d_=dt_: e.activation(
                    sq[:, (d_ % 2) * LC:(d_ % 2 + 1) * LC],
                    xnT[:, d_ * LC:(d_ + 1) * LC], AF.Square),
                  waits=[(s_pe, sqm2[dt_ % 2]), (s_dve, xn_dve)],
                  incs=[(s_act, 1)])
                acv = s_act.n
                E("tensor", lambda e, d_=dt_: e.matmul(
                    ps_b2[64:65, 0:LC], onc[:], xnT[:, d_ * LC:(d_ + 1) * LC],
                    start=(d_ == 0), stop=(d_ == DT - 1)),
                  waits=[(s_dve, xn_dve)])
                E("tensor", lambda e, d_=dt_: e.matmul(
                    ps_b2[0:1, 256:512], onc[:],
                    sq[:, (d_ % 2) * LC:(d_ % 2 + 1) * LC],
                    start=(d_ == 0), stop=(d_ == DT - 1)),
                  waits=[(s_act, acv)], incs=[(s_pe, 1)])
                sqm2[dt_ % 2] = s_pe.n
            stats2_pe = s_pe.n
            roll["sq_pe"] = stats2_pe
            E("vector", lambda e: e.tensor_scalar_mul(RW(0)[:], ps_b2[64:65, 0:LC],
                                                      INV_D),
              waits=[(s_pe, stats2_pe)])
            E("scalar", lambda e: e.activation(RW(3)[:], RW(0)[:], AF.Square),
              waits=[(s_pe, stats2_pe), (s_dve, s_dve.n)], incs=[(s_act, 1)])
            acv = s_act.n
            E("vector", lambda e: e.scalar_tensor_tensor(
                RW(3)[:], ps_b2[0:1, 256:512], INV_D, RW(3)[:], OP.mult, OP.subtract),
              waits=[(s_act, acv)], incs=[(s_dve, 1)])
            dvv = s_dve.n
            E("scalar", lambda e: e.activation(RW(3)[:], RW(3)[:], AF.Ln, bias=EPS),
              waits=[(s_dve, dvv)])
            E("scalar", lambda e: e.activation(RW(1)[:], RW(3)[:], AF.Exp, scale=-0.5),
              waits=[(s_pe, bc1_pe)], incs=[(s_act, 1)])
            acv = s_act.n
            E("vector", lambda e: e.tensor_tensor(RW(2)[:], RW(0)[:], RW(1)[:],
                                                  OP.mult),
              waits=[(s_act, acv), (s_pe, bc1_pe)], incs=[(s_dve, 1)])
            rows2_dve = s_dve.n
            E("tensor", lambda e: e.matmul(ps_b2[:, 0:LC], onr[:], RW(1)[:],
                                           start=True, stop=True),
              waits=[(s_dve, rows2_dve)])
            E("tensor", lambda e: e.matmul(ps_b2[:, 256:512], onr[:], RW(2)[:],
                                           start=True, stop=True),
              incs=[(s_pe, 1)])
            bc2_pe = s_pe.n
            E("scalar", lambda e: e.activation(sq[:, 0:LC], ps_b2[:, 0:LC],
                                               AF.Identity),
              waits=[(s_pe, bc2_pe), (s_dve, s_dve.n)])
            E("scalar", lambda e: e.activation(sq[:, LC:2 * LC], ps_b2[:, 256:512],
                                               AF.Identity),
              incs=[(s_act, 1)])
            acv = s_act.n
            zb = g[:, ZB:ZB + 2048].rearrange("p (r f) -> p r f", r=8)
            E("vector", lambda e: e.tensor_tensor(zb, xnT3d, rep8_0, OP.mult),
              waits=[(s_act, acv), (s_dve, wq_done_dve if c == 0 else 0),
                     (s_dwq, wq1_written if c == 0 else 0)])
            E("vector", lambda e: e.tensor_tensor(zb, zb, rep8_1, OP.subtract),
              incs=[(s_dve, 1)])
            z_dve = s_dve.n
            roll["z_dve"] = z_dve
            wlb = ZB + 2048
            for dt_ in range(DT):
                E("scalar", lambda e, d_=dt_: e.activation(
                    g[:, wlb + d_ * LC:wlb + (d_ + 1) * LC],
                    g[:, ZB + d_ * LC:ZB + (d_ + 1) * LC], AF.Identity,
                    bias=bet[:, d_:d_ + 1], scale=gam[:, d_:d_ + 1]),
                  waits=[(s_dve, z_dve)],
                  incs=[(s_act, 1)])
            wlnact = s_act.n
            E("vector", lambda e: e.tensor_tensor(
                sp1[:], g[:, wlb:wlb + 1024], g[:, wlb + 1024:wlb + 2048], OP.max),
              waits=[(s_act, wlnact)])
            E("vector", lambda e: e.tensor_tensor(
                sp1[:, 0:512], sp1[:, 0:512], sp1[:, 512:1024], OP.max))
            E("vector", lambda e: e.tensor_tensor(
                mrun[:], sp1[:, 0:256], sp1[:, 256:512], OP.max))
            E("vector", lambda e: e.tensor_tensor(
                sp2[:], g[:, wlb:wlb + 1024], g[:, wlb + 1024:wlb + 2048], OP.min))
            E("vector", lambda e: e.tensor_tensor(
                sp2[:, 0:512], sp2[:, 0:512], sp2[:, 512:1024], OP.min))
            E("vector", lambda e: e.tensor_tensor(
                sp2[:, 0:256], sp2[:, 0:256], sp2[:, 256:512], OP.min))
            E("vector", lambda e: e.tensor_scalar_mul(mnrun[:], sp2[:, 0:256], -1.0),
              incs=[(s_dve, 1)])
            wln_dve = s_dve.n
            roll["wln_dve"] = wln_dve
            q1rows_dve, _ = scale_rows(wln_dve, 0, M["sp_ln_all"])
            E("tensor", lambda e: e.matmul(ps_b3[:, 0:LC], onr[:], RW(3)[:],
                                           start=True, stop=True),
              waits=[(s_dve, q1rows_dve), (s_act, roll["c2s_act"])])
            E("tensor", lambda e: e.matmul(ps_b3[:, 256:512], onr[:], RW(4)[:],
                                           start=True, stop=True),
              incs=[(s_pe, 1)])
            bc3_pe = s_pe.n
            E("scalar", lambda e: e.activation(c1s[:], ps_b3[:, 256:512],
                                               AF.Identity),
              waits=[(s_pe, bc3_pe), (s_dve, max(roll["mm1war2"]))],
              incs=[(s_act, 1)])
            E("scalar", lambda e: e.activation(scs[:], ps_b3[:, 0:LC], AF.Identity),
              waits=[(s_dve, roll["round_all"])], incs=[(s_act, 1)])
            c1s_act = s_act.n
            M["c1s_act"] = c1s_act
            E("vector", lambda e: e.tensor_tensor(
                g[:, ZB:ZB + 2048].rearrange("p (r f) -> p r f", r=8),
                g[:, wlb:wlb + 2048].rearrange("p (r f) -> p r f", r=8),
                screp8, OP.mult),
              waits=[(s_act, c1s_act), (s_pe, roll["mm1_pe"])])
            E("vector", lambda e: e.tensor_scalar(
                xqT[:], g[:, ZB:ZB + 2048], MAGIC, MAGIC, OP.add, OP.subtract),
              incs=[(s_dve, 1)])
            M["xq_dve"] = s_dve.n
            M["bc_done"] = {}

            def bc_dma(pr):
                war = 0
                un = ((pr - 2) % 8) * DT + DT - 1
                if c > 0 or pr >= 2:
                    war = unit_dve[un]
                if c == 0 and pr < 2:
                    sem = s_am0 if pr == 0 else s_am1
                else:
                    sem = s_dbc if pr % 2 == 0 else s_dbc1
                for which, buf in ((0, bcB), (1, bcC)):
                    off = cmod * 32 * LC + (which * 16 + pr * 2) * LC
                    E("gpsimd" if (c == 0 and pr < 2) else "sync",
                      lambda e, o_=off, b_=buf, p_=pr: e.dma_start(
                        out=b_[:, (p_ % 2) * 512:(p_ % 2) * 512 + 512],
                        in_=bass.AP(bc_d, o_, [[0, 128], [LC, 2], [1, LC]])),
                      waits=[M["bcw_now"], (s_dve, war)],
                      incs=[(sem, 16)], self_sync=False)
                M["bc_done"][pr] = (sem, sem.n)

            M["bc_dma"] = bc_dma
            bc_dma(0)
            bc_dma(1)
            return M

        def emit_pd(c, M, pr, db):
            pd = pr * DT + db
            sl = pd % 2
            so = sl * 514
            bo = (pr % 2) * 512
            idx = db * N + pr * 2
            aw = unit_dve[(pd - 2) % NUP] if (c > 0 or pd >= 2) else 0
            aw = max(aw, roll.get("azero_dve", 0))
            pw = unit_pool[(pd - 2) % NUP] if (c > 0 or pd >= 2) else 0
            for j in range(2):
                E("scalar", lambda e, d_=db, i_=idx, j_=j, o_=so: e.activation(
                    bass.AP(atp.tensor, atp.offset + o_ + j_ * 257 + 1,
                            [list(atp.ap[0]), [1, 256]]),
                    dl[:, d_ * LC:(d_ + 1) * LC],
                    AF.Exp, scale=A_sb[:, i_ + j_:i_ + j_ + 1]),
                  waits=[(s_dve, max(aw, M["dl_dve"]))],
                  incs=[(s_act, 1)] if j == 1 else ())
            E("scalar", lambda e, o_=so, i_=idx: e.activation(
                bass.AP(btp.tensor, btp.offset + o_, [list(btp.ap[0]), [257, 2]]),
                stp_slice(stp, i_), AF.Identity),
              waits=[(s_pool, pw), (s_pe, roll["outtr_pe"]),
                     (s_dve, aw if (c > 0 or pd >= 2) else M["dl_dve"])],
              incs=[(s_act, 1)])
            a_act = s_act.n
            E("vector", lambda e, d_=db, o_=so, b_=bo: e.tensor_tensor(
                bass.AP(btp.tensor, btp.offset + o_ + 1,
                        [list(btp.ap[0]), [257, 2], [1, 256]]),
                bass.AP(wtp.tensor, wtp.offset + d_ * LC,
                        [list(wtp.ap[0]), [0, 2], [1, 256]]),
                bass.AP(bBp.tensor, bBp.offset + b_,
                        [list(bBp.ap[0]), [256, 2], [1, 256]]),
                OP.mult),
              waits=[M["bc_done"][pr], (s_pool, pw), (s_dve, M["wt_dve"]),
                     (s_pe, roll["outtr_pe"])])
            E("vector", lambda e, o_=so: e.tensor_tensor_scan(
                htp[0:128, 0:514],
                bass.AP(atp.tensor, atp.offset + o_, [list(atp.ap[0]), [1, 514]]),
                bass.AP(btp.tensor, btp.offset + o_, [list(btp.ap[0]), [1, 514]]),
                0.0, OP.mult, OP.add),
              waits=[(s_act, a_act)])
            E("vector", lambda e, i_=idx: e.tensor_copy(
                stp_slice(stp, i_),
                bass.AP(htp.tensor, htp.offset + 256, [list(htp.ap[0]), [257, 2]])))
            E("vector", lambda e, o_=so, b_=bo: e.tensor_tensor(
                bass.AP(btp.tensor, btp.offset + o_ + 1,
                        [list(btp.ap[0]), [257, 2], [1, 256]]),
                bass.AP(htp.tensor, htp.offset + 1,
                        [list(htp.ap[0]), [257, 2], [1, 256]]),
                bass.AP(bCp.tensor, bCp.offset + b_,
                        [list(bCp.ap[0]), [256, 2], [1, 256]]),
                OP.mult),
              incs=[(s_dve, 1)])
            unit_dve[pd] = s_dve.n
            E("gpsimd", lambda e, o_=so: e.tensor_tensor(
                ptree[:],
                bass.AP(btp.tensor, btp.offset + o_ + 1, [list(btp.ap[0]), [1, 256]]),
                bass.AP(btp.tensor, btp.offset + o_ + 258,
                        [list(btp.ap[0]), [1, 256]]),
                OP.add),
              waits=[(s_dve, unit_dve[pd]),
                     (s_act, M["yainit_act"][db]) if pr == 0 else (s_act, 0)])
            E("gpsimd", lambda e, d_=db: e.tensor_tensor(
                ya[:, d_ * LC:(d_ + 1) * LC],
                ya[:, d_ * LC:(d_ + 1) * LC], ptree[:], OP.add),
              incs=[(s_pool, 1)])
            unit_pool[pd] = s_pool.n

        def emit_stage2A(c, M, ht):
            sl = ht % 2
            wsem = s_dw if ht % 2 == 0 else s_dw1
            E("sync", lambda e, h_=ht, s_=sl: e.dma_start(
                out=wq1s[:, s_ * 1024:(s_ + 1) * 1024],
                in_=bass.AP(wq1_d, h_ * 131072, [[1024, 128], [1, 1024]])),
              waits=[(s_dwq, wq1_written),
                     (s_pe, M["mm1_marks"][ht - 2] if ht >= 2 else roll["mm1_pe"])],
              incs=[(wsem, 16)])
            dwv = (wsem, wsem.n)
            m1ps = (ps_m1 if ht % 2 == 0 else ps_m2)[:, 0:LC]
            for dt_ in range(DT):
                E("tensor", lambda e, d_=dt_, s_=sl, pp=m1ps: e.matmul(
                    pp,
                    wq1s[:, s_ * 1024 + d_ * 128:s_ * 1024 + (d_ + 1) * 128],
                    xqT[:, d_ * LC:(d_ + 1) * LC],
                    start=(d_ == 0), stop=(d_ == DT - 1)),
                  waits=[dwv, (s_dve, M["xq_dve"]),
                         (s_dve, M["pre_marks"][ht - 2] if ht >= 2
                          else roll["mm1war2"][sl])],
                  incs=[(s_pe, 1)] if dt_ == DT - 1 else ())
            M["mm1_marks"][ht] = s_pe.n
            E("vector", lambda e, s_=sl, pp=m1ps: e.tensor_tensor(
                pre[:, s_ * LC:(s_ + 1) * LC], pp, c1s[:], OP.mult),
              waits=[(s_pe, M["mm1_marks"][ht]), (s_act, M["c1s_act"]),
                     (s_act, M["gelu_marks"][ht - 2] if ht >= 2 else 0)],
              incs=[(s_dve, 1)])
            M["pre_marks"][ht] = s_dve.n

        def emit_stage2B(c, M, ht):
            sl = ht % 2
            E("scalar", lambda e, h_=ht, s_=sl: e.activation(
                g[:, h_ * LC:(h_ + 1) * LC], pre[:, s_ * LC:(s_ + 1) * LC],
                AF.Gelu, bias=b1c[:, h_:h_ + 1]),
              waits=[(s_dve, M["pre_marks"][ht])],
              incs=[(s_act, 1)])
            M["gelu_marks"][ht] = s_act.n
            if ht % 4 == 3:
                qb = (ht - 3) * LC
                if ht == 3:
                    E("vector", lambda e, q_=qb: e.tensor_copy(
                        sp1[:], g[:, q_:q_ + 1024]),
                      waits=[(s_act, M["gelu_marks"][ht]),
                             (s_pe, roll["outtr_pe"])],
                      incs=[(s_dve, 1)])
                    E("vector", lambda e, q_=qb: e.tensor_scalar_mul(
                        sp2[:], g[:, q_:q_ + 1024], -1.0),
                      incs=[(s_dve, 1)])
                else:
                    E("vector", lambda e, q_=qb: e.tensor_tensor(
                        sp1[:], sp1[:], g[:, q_:q_ + 1024], OP.max),
                      waits=[(s_act, M["gelu_marks"][ht])], incs=[(s_dve, 1)])
                    E("vector", lambda e, q_=qb: e.scalar_tensor_tensor(
                        sp2[:], g[:, q_:q_ + 1024], -1.0, sp2[:],
                        OP.mult, OP.max),
                      incs=[(s_dve, 1)])

        def emit_q2_round(c, M):
            roll["mm1_pe"] = M["mm1_marks"][HT - 1]
            roll["mm1war2"] = [M["pre_marks"][HT - 2], M["pre_marks"][HT - 1]]
            E("vector", lambda e: e.tensor_tensor(
                sp1[:, 0:512], sp1[:, 0:512], sp1[:, 512:1024], OP.max))
            E("vector", lambda e: e.tensor_tensor(
                mrun[:], sp1[:, 0:256], sp1[:, 256:512], OP.max))
            E("vector", lambda e: e.tensor_tensor(
                sp2[:, 0:512], sp2[:, 0:512], sp2[:, 512:1024], OP.max))
            E("vector", lambda e: e.tensor_tensor(
                mnrun[:], sp2[:, 0:256], sp2[:, 256:512], OP.max),
              incs=[(s_dve, 1)])
            q2max_dve = s_dve.n
            q2rows_dve, _ = scale_rows(q2max_dve, 2, M["sp_ln_all"])
            E("tensor", lambda e: e.matmul(ps_b3[:, 0:LC], onr[:], RW(3)[:],
                                           start=True, stop=True),
              waits=[(s_dve, q2rows_dve), (s_act, M["c1s_act"])])
            E("tensor", lambda e: e.matmul(ps_b3[:, 256:512], onr[:], RW(4)[:],
                                           start=True, stop=True),
              incs=[(s_pe, 1)])
            bc32_pe = s_pe.n
            E("scalar", lambda e: e.activation(c2s[:], ps_b3[:, 256:512],
                                               AF.Identity),
              waits=[(s_pe, bc32_pe), (s_dve, s_dve.n)], incs=[(s_act, 1)])
            E("scalar", lambda e: e.activation(scs[:], ps_b3[:, 0:LC], AF.Identity),
              waits=[(s_dve, M["xq_dve"])], incs=[(s_act, 1)])
            c2s_act = s_act.n
            roll["c2s_act"] = c2s_act
            M["c2s_act"] = c2s_act
            round_marks = [0] * HT
            for q_ in range(HT // 4):
                E("vector", lambda e, qq=q_: e.tensor_tensor(
                    sp1[:].rearrange("p (r f) -> p r f", r=4),
                    g[:, qq * 1024:(qq + 1) * 1024].rearrange(
                        "p (r f) -> p r f", r=4),
                    screp4, OP.mult),
                  waits=[(s_act, c2s_act), (s_pe, roll["outtr_pe"])])
                E("vector", lambda e, qq=q_: e.tensor_scalar(
                    hq[:, qq * 1024:(qq + 1) * 1024], sp1[:],
                    MAGIC, MAGIC, OP.add, OP.subtract),
                  incs=[(s_dve, 1)])
                for j in range(4):
                    round_marks[q_ * 4 + j] = s_dve.n
            M["round_marks"] = round_marks
            roll["round_all"] = s_dve.n

        def _mm2_group(c, M, dt_, war):
            m2ps = (ps_m1 if dt_ % 2 == 0 else ps_m2)[:, 256:256 + LC]
            for ht in range(HT):
                E("tensor", lambda e, d_=dt_, h_=ht, pp=m2ps: e.matmul(
                    pp,
                    wq2[:, h_ * 1024 + d_ * 128:h_ * 1024 + (d_ + 1) * 128],
                    hq[:, h_ * LC:(h_ + 1) * LC],
                    start=(h_ == 0), stop=(h_ == HT - 1)),
                  waits=[(s_dve, M["round_marks"][ht]), (s_dve, war[dt_ % 2])],
                  incs=[(s_pe, 1)] if ht == HT - 1 else ())
            M["mm2_pe"][dt_] = s_pe.n

        def emit_mm2(c, M):
            M["mm2_pe"] = [0] * DT
            M["m2war"] = list(roll.get("m2war2", [0, 0]))
            _mm2_group(c, M, 0, M["m2war"])
            _mm2_group(c, M, 1, M["m2war"])

        def make_tailcb(c, M):
            state = {"dt": 0}
            M["e3_marks"] = [0] * DT
            war = M["m2war"]

            def cb():
                dt_ = state["dt"]
                if dt_ >= DT:
                    return
                state["dt"] += 1
                m2ps = (ps_m1 if dt_ % 2 == 0 else ps_m2)[:, 256:256 + LC]
                eo = (atp if dt_ < 4 else btp)
                E("vector", lambda e, pp=m2ps: e.tensor_tensor(
                    f1[:], pp, c2s[:], OP.mult),
                  waits=[(s_pe, M["mm2_pe"][dt_]), (s_act, M["c2s_act"])],
                  incs=[(s_dve, 1)])
                war[dt_ % 2] = s_dve.n
                E("vector", lambda e, d_=dt_, b_=eo: e.tensor_tensor(
                    bass.AP(b_.tensor, b_.offset + (d_ % 4) * LC,
                            [list(b_.ap[0]), [1, LC]]),
                    f1[:], ya[:, d_ * LC:(d_ + 1) * LC], OP.add),
                  waits=[(s_pool, M["mamba_pool"])],
                  incs=[(s_dve, 1)])
                M["e3_marks"][dt_] = s_dve.n
                if dt_ + 2 < DT:
                    _mm2_group(c, M, dt_ + 2, war)
                if dt_ == DT - 1:
                    roll["m2war2"] = war

            return cb

        def emit_out(c, M):
            for k in range(TB):
                E("sync", lambda e, kk=k, c2=c: e.dma_start(
                    out=utok[:],
                    in_=bass.AP(u_d, (c2 * LC + kk * 128) * D, [[D, 128], [1, D]])),
                  waits=[(s_pe, s_pe.n), (s_dve, roll["tailevac_dve"])],
                  incs=[(s_du, 16)])
                duv = s_du.n
                for half in range(2):
                    for j in range(4):
                        dt_ = half * 4 + j
                        src = (atp if dt_ < 4 else btp)
                        E("tensor", lambda e, d_=dt_, jj=j, kk=k, ss=src:
                          e.transpose(
                            ps_tr[:, jj * 128:(jj + 1) * 128],
                            bass.AP(ss.tensor,
                                    ss.offset + (d_ % 4) * LC + kk * 128,
                                    [list(ss.ap[0]), [1, 128]]),
                            eye[:]),
                          waits=[(s_dve, M["e3_marks"][dt_]), (s_dve, s_dve.n),
                                 (s_act, s_act.n)],
                          incs=[(s_pe, 1)] if j == 3 else ())
                    pev = s_pe.n
                    E("vector", lambda e, hh=half: e.tensor_tensor(
                        outst[:, hh * 512:(hh + 1) * 512], ps_tr[:, 0:512],
                        utok[:, hh * 512:(hh + 1) * 512], OP.add),
                      waits=[(s_pe, pev), (s_du, duv), (s_do, roll["outdma"])],
                      incs=[(s_dve, 1)])
                roll["tailevac_dve"] = s_dve.n
                roll["outtr_pe"] = s_pe.n
                dvv = s_dve.n
                E("sync", lambda e, kk=k, c2=c: e.dma_start(
                    out=bass.AP(o_d, (c2 * LC + kk * 128) * D, [[D, 128], [1, D]]),
                    in_=outst[:]),
                  waits=[(s_dve, dvv)], incs=[(s_do, 16)])
                roll["outdma"] = s_do.n
            E("vector", lambda e: e.memset(
                bass.AP(atp.tensor, atp.offset, [list(atp.ap[0]), [257, 4]]), 0.0),
              waits=[(s_pe, roll["outtr_pe"])], incs=[(s_dve, 1)])
            roll["azero_dve"] = s_dve.n

        def emit_yainit(c, M):
            M["yainit_act"] = [0] * DT
            for dt_ in range(DT):
                E("scalar", lambda e, d_=dt_: e.activation(
                    ya[:, d_ * LC:(d_ + 1) * LC], xnT[:, d_ * LC:(d_ + 1) * LC],
                    AF.Identity, bias=b2c[:, d_:d_ + 1], scale=Dg[:, d_:d_ + 1]),
                  waits=[(s_dve, max(M["ya_free_dve"][dt_], M["xq_dve"]))],
                  incs=[(s_act, 1)])
                M["yainit_act"][dt_] = s_act.n

        def emit_main_rest(c, M, start_pr=0, delay=0, lag=0, hoist=None):
            M["mm1_marks"] = [0] * HT
            M["pre_marks"] = [0] * HT
            M["gelu_marks"] = [0] * HT
            hsteps = dict(zip((16, 24, 32, 40), hoist)) if hoist else {}
            npds = (8 - start_pr) * DT
            doneA = 0
            doneB = 0
            i = 0
            for pr in range(start_pr, 8):
                for db in range(DT):
                    emit_pd(c, M, pr, db)
                    i += 1
                    if i in hsteps:
                        hsteps[i]()
                    if i > delay:
                        want = min(HT, ((i - delay) * HT) // (npds - delay - 4) + 1)
                        while doneA < want:
                            while doneB < doneA - 1:
                                emit_stage2B(c, M, doneB)
                                doneB += 1
                            emit_stage2A(c, M, doneA)
                            doneA += 1
                        while doneB < max(0, doneA - lag):
                            emit_stage2B(c, M, doneB)
                            doneB += 1
                if pr + 2 < 8:
                    M["bc_dma"](pr + 2)
            while doneA < HT:
                emit_stage2A(c, M, doneA)
                doneA += 1
            while doneB < HT:
                emit_stage2B(c, M, doneB)
                doneB += 1
            M["mamba_pool"] = s_pool.n

        A, asteps = prepA_steps(0)
        for fn in asteps:
            fn()
        emit_quant(am1_done)
        wq_done_dve = s_dve.n
        wq1_written = s_dwq.n
        M = emit_prep(0, A)
        M["ya_free_dve"] = [0] * DT
        emit_yainit(0, M)
        An, asteps = prepA_steps(1)
        emit_main_rest(0, M, 0, hoist=asteps)
        for c in range(NCH):
            emit_q2_round(c, M)
            emit_mm2(c, M)
            cb = make_tailcb(c, M)
            if c + 1 < NCH:
                Mn = emit_prep(c + 1, An, tailcb=cb)
                while M["e3_marks"][DT - 1] == 0:
                    cb()
                emit_out(c, M)
                Mn["ya_free_dve"] = list(M["e3_marks"])
                emit_yainit(c + 1, Mn)
                if c + 2 < NCH:
                    An, asteps = prepA_steps(c + 2)
                else:
                    asteps = None
                emit_main_rest(c + 1, Mn, 0, hoist=asteps)
                M = Mn
            else:
                for _ in range(DT):
                    cb()
                emit_out(c, M)

        E("gpsimd", lambda e: e.engine_nop(), waits=[(s_do, s_do.n)])

        import os
        if os.environ.get("BASS_PRINT_SBUF"):
            print("sbuf remaining/partition:", nc.sbuf_bytes_remaining,
                  "psum banks remaining:", nc.psum_banks_remaining)
        with nc.Block() as block:
            em.flush(block, max_seq=max_seq)

    return nc


def kernel(L0_attn_u, L0_attn_W_x, L0_attn_W_dt, L0_attn_b_dt, L0_attn_log_A,
           L0_attn_D_param, L0_ffn_w1, L0_ffn_b1, L0_ffn_w2, L0_ffn_b2,
           L0_ffn_gamma, L0_ffn_beta):
    u = np.asarray(L0_attn_u, dtype=np.float32)
    wx = np.asarray(L0_attn_W_x, dtype=np.float32)
    wdt = np.asarray(L0_attn_W_dt, dtype=np.float32)
    bdt = np.asarray(L0_attn_b_dt, dtype=np.float32)
    logA = np.asarray(L0_attn_log_A, dtype=np.float32)
    Dp = np.asarray(L0_attn_D_param, dtype=np.float32)
    w1 = np.asarray(L0_ffn_w1, dtype=np.float32)
    b1 = np.asarray(L0_ffn_b1, dtype=np.float32)
    w2 = np.asarray(L0_ffn_w2, dtype=np.float32)
    b2 = np.asarray(L0_ffn_b2, dtype=np.float32)
    gam = np.asarray(L0_ffn_gamma, dtype=np.float32)
    bet = np.asarray(L0_ffn_beta, dtype=np.float32)

    wxT = np.ascontiguousarray(wx.T)
    wdtTb = np.ascontiguousarray(np.concatenate([wdt.T, bdt[None, :]], axis=0))
    logA_r = np.ascontiguousarray(
        logA.reshape(DT, 128, N).transpose(1, 0, 2).reshape(128, DT * N))
    r8 = lambda v: np.ascontiguousarray(v.reshape(DT, 128).T)
    base = dict(
        wxT=wxT, wdtTb=wdtTb, logA_r=logA_r,
        D_r=r8(Dp), gamma_r=r8(gam), beta_r=r8(bet), b2_r=r8(b2),
        b1_r=np.ascontiguousarray(b1.reshape(HT, 128).T),
        w1T=np.ascontiguousarray(w1.T), w2T=np.ascontiguousarray(w2.T),
        eye=np.eye(128, dtype=np.float32),
        ones_row=np.ones((1, 128), dtype=np.float32),
        ones_col=np.ones((128, 1), dtype=np.float32),
        sgnm=np.full((128, 1), np.int32(-2147483648), dtype=np.int32),
        cst=np.broadcast_to(np.array([MAGIC, EPS], dtype=np.float32),
                            (128, 2)).copy(),
    )
    nc = build_nc()
    in_maps = [dict(base, u=np.ascontiguousarray(u[i])) for i in range(8)]
    res = run_bass_kernel_spmd(nc, in_maps, core_ids=list(range(8)))
    return np.stack([res.results[i]["o"] for i in range(8)], axis=0)


# version 40
# speedup vs baseline: 1.1722x; 1.1722x over previous
import numpy as np
import concourse.bass as bass
import concourse.mybir as mybir
from concourse.bass_utils import run_bass_kernel_spmd

F32 = mybir.dt.float32
BF16 = mybir.dt.bfloat16
I32 = mybir.dt.int32
AF = mybir.ActivationFunctionType
OP = mybir.AluOpType
AX = mybir.AxisListType

T, D, N, H, R = 4096, 1024, 16, 4096, 64
LC = 256
NCH = T // LC
TB = LC // 128
DT = D // 128
HT = H // 128
NU = N * DT
EPS = 1e-5
MAGIC = 12582912.0
INV_D = 1.0 / D


class Sem:
    def __init__(self, nc, es, name):
        self.h = es.enter_context(nc.semaphore(name))
        self.n = 0


class Emitter:
    ENGS = ("sync", "tensor", "vector", "scalar", "gpsimd")

    def __init__(self, self_sync_sems, use_drain=False):
        self.ops = {e: [] for e in self.ENGS}
        self.waited = {e: {} for e in self.ENGS}
        self.seq = 0
        self.self_sync = self_sync_sems
        self.use_drain = use_drain

    def op(self, eng, body, waits=(), incs=(), self_sync=True):
        own = self.self_sync.get(eng) if self_sync else None
        wl = []
        drain = False
        if own is not None and own.n > 0:
            if self.use_drain and eng != "tensor":
                drain = True
            else:
                wl.append((own.h, own.n))
                self.waited[eng][id(own)] = max(
                    self.waited[eng].get(id(own), 0), own.n)
        for s, v in waits:
            if v is None or v <= 0:
                continue
            key = id(s)
            if v > self.waited[eng].get(key, 0):
                wl.append((s.h, v))
                self.waited[eng][key] = v
        il = []
        own_inced = False
        for s, k in incs:
            s.n += k
            il.append((s.h, k))
            if s is own:
                own_inced = True
        if own is not None and not own_inced:
            own.n += 1
            il.append((own.h, 1))
        self.ops[eng].append((wl, body, il, self.seq, drain))
        self.seq += 1

    def flush(self, block, max_seq=None):
        def mk(lst):
            def run(eng):
                for wl, body, il, sq_, drain in lst:
                    if max_seq is not None and sq_ > max_seq:
                        continue
                    if drain:
                        eng.drain()
                    for h, v in wl:
                        eng.wait_ge(h, v)
                    ins = body(eng)
                    for h, k in il:
                        ins.then_inc(h, k)
            return run
        block.sync(mk(self.ops["sync"]))
        block.tensor(mk(self.ops["tensor"]))
        block.vector(mk(self.ops["vector"]))
        block.scalar(mk(self.ops["scalar"]))
        block.gpsimd(mk(self.ops["gpsimd"]))


def build_nc(full_sync=False, nch=NCH, max_seq=None):
    nc = bass.Bass("TRN2", target_bir_lowering=False,
                   detect_race_conditions=full_sync)

    u_d = nc.dram_tensor("u", [T, D], F32, kind="ExternalInput")
    o_d = nc.dram_tensor("o", [T, D], F32, kind="ExternalOutput")
    wxT_d = nc.dram_tensor("wxT", [D, 96], F32, kind="ExternalInput")
    wdtTb_d = nc.dram_tensor("wdtTb", [R + 1, D], F32, kind="ExternalInput")
    logA_d = nc.dram_tensor("logA_r", [128, DT * N], F32, kind="ExternalInput")
    Dp_d = nc.dram_tensor("D_r", [128, DT], F32, kind="ExternalInput")
    gam_d = nc.dram_tensor("gamma_r", [128, DT], F32, kind="ExternalInput")
    bet_d = nc.dram_tensor("beta_r", [128, DT], F32, kind="ExternalInput")
    b2_d = nc.dram_tensor("b2_r", [128, DT], F32, kind="ExternalInput")
    b1_d = nc.dram_tensor("b1_r", [128, HT], F32, kind="ExternalInput")
    w1T_d = nc.dram_tensor("w1T", [D, H], F32, kind="ExternalInput")
    w2T_d = nc.dram_tensor("w2T", [H, D], F32, kind="ExternalInput")
    eye_d = nc.dram_tensor("eye", [128, 128], F32, kind="ExternalInput")
    onr_d = nc.dram_tensor("ones_row", [1, 128], F32, kind="ExternalInput")
    onc_d = nc.dram_tensor("ones_col", [128, 1], F32, kind="ExternalInput")
    sgn_d = nc.dram_tensor("sgnm", [128, 1], I32, kind="ExternalInput")
    cst_d = nc.dram_tensor("cst", [128, 2], F32, kind="ExternalInput")

    wq1_d = nc.dram_tensor("wq1d", [HT * 128 * 1024], BF16, kind="Internal")
    bc_d = nc.dram_tensor("bcd", [2 * 32 * LC], F32, kind="Internal")
    scl_d = nc.dram_tensor("scld", [8], F32, kind="Internal")

    from contextlib import ExitStack
    es = ExitStack()
    with es:
        sb = nc.sbuf_tensor
        wq2 = es.enter_context(sb("wq2", [128, HT * 1024], BF16))
        g = es.enter_context(sb("g", [128, HT * LC], F32))
        uT = es.enter_context(sb("uT", [128, DT * LC], F32))
        xnT = es.enter_context(sb("xnT", [128, DT * LC], F32))
        dl = es.enter_context(sb("dl", [128, DT * LC], F32))
        wt = es.enter_context(sb("wt", [128, DT * LC], F32))
        ya = es.enter_context(sb("ya", [128, DT * LC], F32))
        xqT = es.enter_context(sb("xqT", [128, DT * LC], BF16))
        utok = es.enter_context(sb("utok", [128, D], F32))
        outst = es.enter_context(sb("outst", [128, D], F32))
        wq1s = es.enter_context(sb("wq1s", [128, 2 * 1024], BF16))
        bcB = es.enter_context(sb("bcB", [128, 2 * 512], F32))
        bcC = es.enter_context(sb("bcC", [128, 2 * 512], F32))
        a_t = es.enter_context(sb("a_t", [128, 2 * 514], F32))
        b_t = es.enter_context(sb("b_t", [128, 2 * 514], F32))
        h_t = es.enter_context(sb("h_t", [128, 514], F32))
        ptree = es.enter_context(sb("ptree", [128, 256], F32))
        dsb = es.enter_context(sb("dsb", [128, LC], F32))
        bcsb = es.enter_context(sb("bcsb", [32, LC], F32))
        sq = es.enter_context(sb("sq", [128, 2 * LC], F32))
        sp1 = es.enter_context(sb("sp1", [128, 1024], F32))
        sp2 = es.enter_context(sb("sp2", [128, 1024], F32))
        pre = es.enter_context(sb("pre", [128, 2 * LC], F32))
        mrun = es.enter_context(sb("mrun", [128, LC], F32))
        mnrun = es.enter_context(sb("mnrun", [128, LC], F32))
        c1s = es.enter_context(sb("c1s", [128, LC], F32))
        scs = es.enter_context(sb("scs", [128, LC], F32))
        c2s = es.enter_context(sb("c2s", [128, LC], F32))
        f1 = es.enter_context(sb("f1", [128, LC], F32))
        f2 = es.enter_context(sb("f2", [128, LC], F32))
        stt_sb = es.enter_context(sb("stt", [128, NU], F32))
        wxT = es.enter_context(sb("wxTs", [128, DT * 96], F32))
        wdt = es.enter_context(sb("wdts", [R + 1, D], F32))
        A_sb = es.enter_context(sb("A_sb", [128, DT * N], F32))
        Dg = es.enter_context(sb("Dg", [128, DT], F32))
        gam = es.enter_context(sb("gam", [128, DT], F32))
        bet = es.enter_context(sb("bet", [128, DT], F32))
        b2c = es.enter_context(sb("b2c", [128, DT], F32))
        b1c = es.enter_context(sb("b1c", [128, HT], F32))
        eye = es.enter_context(sb("eyes", [128, 128], F32))
        onr = es.enter_context(sb("onr", [1, 128], F32))
        onc = es.enter_context(sb("onc", [128, 1], F32))
        sgn = es.enter_context(sb("sgn", [128, 1], I32))
        mwc = es.enter_context(sb("mwc", [128, 4], F32))
        acc = es.enter_context(sb("acc", [128, 4], F32))
        cst = es.enter_context(sb("cst_s", [128, 2], F32))
        rows = es.enter_context(sb("rows", [1, 5 * LC], F32))
        RW = lambda i: rows[0:1, i * LC:(i + 1) * LC]

        ps = nc.psum_tensor
        ps_tr = es.enter_context(ps("ps_tr", [128, 512], F32))
        ps_xd = es.enter_context(ps("ps_xd", [128, 512], F32))
        ps_d = es.enter_context(ps("ps_d", [128, 512], F32))
        ps_b1 = es.enter_context(ps("ps_b1", [128, 512], F32))
        ps_b2 = es.enter_context(ps("ps_b2", [128, 512], F32))
        ps_b3 = es.enter_context(ps("ps_b3", [128, 512], F32))
        ps_m1 = es.enter_context(ps("ps_m1", [128, 512], F32))
        ps_m2 = es.enter_context(ps("ps_m2", [128, 512], F32))

        sems = {k: Sem(nc, es, "s_" + k) for k in
                ("ld", "du", "dw", "dwq", "dbc", "dbw", "dsc", "do",
                 "pe", "act", "dve", "pool", "dbc1", "dw1", "am0", "am1")}
        s_am0, s_am1 = sems["am0"], sems["am1"]
        s_ld, s_du, s_dw, s_dwq = (sems[k] for k in ("ld", "du", "dw", "dwq"))
        s_dbc, s_dbw, s_dsc, s_do = (sems[k] for k in ("dbc", "dbw", "dsc", "do"))
        s_dbc1, s_dw1 = sems["dbc1"], sems["dw1"]
        s_pe, s_act, s_dve, s_pool = (sems[k] for k in ("pe", "act", "dve", "pool"))

        nc.const_aps.register(F32, MAGIC, cst[:, 0:1])
        nc.const_aps.register(F32, EPS, cst[:, 1:2])

        self_sync = {"vector": s_dve, "scalar": s_act, "gpsimd": s_pool}
        if full_sync:
            self_sync["tensor"] = s_pe
        em = Emitter(self_sync, use_drain=not full_sync)
        E = em.op

        uT3 = uT[:].rearrange("p (dt l) -> p dt l", dt=DT)
        tr3 = ps_tr[:].rearrange("p (j l) -> p j l", j=4)

        for dst, src in ((wxT[:], bass.AP(wxT_d, 0, [[96, 128], [96 * 128, DT], [1, 96]])),
                         (wdt[:], wdtTb_d.ap()),
                         (A_sb[:], logA_d.ap()),
                         (Dg[:], Dp_d.ap()), (gam[:], gam_d.ap()),
                         (bet[:], bet_d.ap()), (b2c[:], b2_d.ap()),
                         (b1c[:], b1_d.ap()), (eye[:], eye_d.ap()),
                         (onr[:], onr_d.ap()), (onc[:], onc_d.ap()),
                         (sgn[:], sgn_d.ap()),
                         (cst[:], cst_d.ap())):
            E("sync", lambda e, d_=dst, s_=src: e.dma_start(out=d_, in_=s_),
              incs=[(s_ld, 16)])
        ld_all = s_ld.n

        E("scalar", lambda e: e.activation(sp1[:, 0:DT * N], A_sb[:], AF.Exp),
          waits=[(s_ld, ld_all)], incs=[(s_act, 1)])
        E("vector", lambda e: e.tensor_scalar_mul(A_sb[:], sp1[:, 0:DT * N], -1.0),
          waits=[(s_act, s_act.n)], incs=[(s_dve, 1)])
        E("vector", lambda e: e.memset(dsb[64:65, :], 1.0), incs=[(s_dve, 1)])
        E("vector", lambda e: e.memset(a_t[:], 0.0), incs=[(s_dve, 1)])
        E("vector", lambda e: e.memset(stt_sb[:], 0.0), incs=[(s_dve, 1)])

        stg = g[:, 4096:8192]
        stg2 = g[:, 0:4096]
        wq_bf = g[:].bitcast(BF16)[0:128, 8192:12288]

        def wblk_ap(is_w1, blk):
            if is_w1:
                return bass.AP(w1T_d, blk * 512, [[H, 128], [H * 128, DT], [1, 512]])
            return bass.AP(w2T_d, blk * 4 * 128 * D, [[D, 128], [D * 128, 4], [1, D]])

        def w_half_ap(is_w1, hb):
            if is_w1:
                return bass.AP(w1T_d,
                               (hb // 2) * 512 + (hb % 2) * (H * 128 * 4),
                               [[H, 128], [H * 128, 4], [1, 512]])
            return bass.AP(w2T_d, hb * (D * 128 * 2),
                           [[D, 128], [D * 128, 2], [1, D]])

        def w_absmean_dual():
            am1 = [0, 0]
            am2 = [0, 0]
            sems1 = (s_dw, s_dw1)
            sems2 = (s_am0, s_am1)
            for hb in range(16):
                sl = hb % 2
                b1 = stg[:, sl * 2048:(sl + 1) * 2048]
                E("sync", lambda e, a=w_half_ap(True, hb), b_=b1: e.dma_start(
                    out=b_, in_=a),
                  waits=[(s_dve, am1[sl])], incs=[(sems1[sl], 16)])
                d1 = sems1[sl].n
                E("vector", lambda e, b_=b1: e.tensor_reduce(
                    acc[:, 1:2], b_, AX.X, OP.add, apply_absolute_value=True),
                  waits=[(sems1[sl], d1)])
                if hb == 0:
                    E("vector", lambda e: e.tensor_copy(acc[:, 0:1],
                                                        acc[:, 1:2]),
                      incs=[(s_dve, 1)])
                else:
                    E("vector", lambda e: e.tensor_tensor(
                        acc[:, 0:1], acc[:, 0:1], acc[:, 1:2], OP.add),
                      incs=[(s_dve, 1)])
                am1[sl] = s_dve.n
                b2_ = uT[:] if sl == 0 else xnT[:]
                E("gpsimd", lambda e, a=w_half_ap(False, hb), b_=b2_:
                  e.dma_start(out=b_, in_=a),
                  waits=[(s_dve, am2[sl])],
                  incs=[(sems2[sl], 16)], self_sync=False)
                d2 = sems2[sl].n
                E("vector", lambda e, b_=b2_: e.tensor_reduce(
                    acc[:, 3:4], b_, AX.X, OP.add, apply_absolute_value=True),
                  waits=[(sems2[sl], d2)])
                if hb == 0:
                    E("vector", lambda e: e.tensor_copy(acc[:, 2:3],
                                                        acc[:, 3:4]),
                      incs=[(s_dve, 1)])
                else:
                    E("vector", lambda e: e.tensor_tensor(
                        acc[:, 2:3], acc[:, 2:3], acc[:, 3:4], OP.add),
                      incs=[(s_dve, 1)])
                am2[sl] = s_dve.n
            return am1, max(am2)

        def w_scale_finish(col, acol):
            E("tensor", lambda e, ac=acol: e.matmul(
                ps_tr[0:1, 0:1], onc[:], acc[:, ac:ac + 1],
                start=True, stop=True),
              waits=[(s_dve, s_dve.n), (s_ld, ld_all)], incs=[(s_pe, 1)])
            pev = s_pe.n
            E("vector", lambda e: e.tensor_scalar(
                RW(3)[:, 0:1], ps_tr[0:1, 0:1], 1.0 / (D * H), EPS, OP.mult, OP.max),
              waits=[(s_pe, pev), (s_dsc, s_dsc.n)])
            E("vector", lambda e: e.reciprocal(RW(3)[:, 1:2], RW(3)[:, 0:1]),
              incs=[(s_dve, 1)])
            dvv = s_dve.n
            E("sync", lambda e, c=col: e.dma_start(
                out=bass.AP(scl_d, 2 * c, [[1, 1], [1, 2]]), in_=RW(3)[0:1, 0:2]),
              waits=[(s_dve, dvv), (s_dsc, s_dsc.n)], incs=[(s_dsc, 16)])
            dscv = s_dsc.n
            E("sync", lambda e, c=col: e.dma_start(
                out=mwc[:, 2 * c:2 * c + 2],
                in_=bass.AP(scl_d, 2 * c, [[0, 128], [1, 2]])),
              waits=[(s_dsc, dscv)], incs=[(s_dsc, 16)])

        def emit_quant(am1_slots):
            sw1_col = mwc[:, 1:2]
            sw2_col = mwc[:, 3:4]
            mwrdy = s_dsc.n
            yabf = ya[:].bitcast(BF16)
            qsems = (s_dw, s_dw1)
            actm = [0, 0]
            dvem = [0, 0]
            inm = list(am1_slots)
            dwqm = s_dwq.n
            for idx in range(32):
                is_w1 = idx < 16
                hb = idx % 16
                blk, h2 = hb // 2, hb % 2
                sl = idx % 2
                sin = stg[:, sl * 2048:(sl + 1) * 2048]
                ssc = stg2[:, sl * 2048:(sl + 1) * 2048]
                E("sync", lambda e, a=w_half_ap(is_w1, hb), b_=sin:
                  e.dma_start(out=b_, in_=a),
                  waits=[(s_act, actm[sl]), (s_dve, inm[sl])],
                  incs=[(qsems[sl], 16)])
                dw = qsems[sl].n
                E("scalar", lambda e, i_=sin, o_=ssc,
                  sc=(sw1_col if is_w1 else sw2_col): e.activation(
                    o_, i_, AF.Identity, bias=MAGIC, scale=sc),
                  waits=[(qsems[sl], dw), (s_dve, dvem[sl]), (s_dsc, mwrdy)],
                  incs=[(s_act, 1)])
                actm[sl] = s_act.n
                E("vector", lambda e, o_=ssc: e.tensor_scalar(
                    o_, o_, MAGIC, 1.0, OP.subtract, OP.min),
                  waits=[(s_act, actm[sl])])
                if is_w1:
                    out = bass.AP(yabf.tensor, yabf.offset + h2 * 512,
                                  [[4096, 128], [128, 4], [1024, 4], [1, 128]])
                    E("vector", lambda e, o_=out, i_=ssc:
                      e.tensor_single_scalar(o_, i_, -1.0, OP.max),
                      waits=[(s_dwq, dwqm)], incs=[(s_dve, 1)])
                else:
                    o2 = wq2[:, blk * 4096 + h2 * 2048:
                             blk * 4096 + h2 * 2048 + 2048]
                    E("vector", lambda e, o_=o2, i_=ssc:
                      e.tensor_single_scalar(o_, i_, -1.0, OP.max),
                      incs=[(s_dve, 1)])
                dvem[sl] = s_dve.n
                if is_w1 and h2 == 1:
                    E("sync", lambda e, b=blk, dv=dvem[sl]: e.dma_start(
                        out=bass.AP(wq1_d, b * 4 * 131072,
                                    [[1024, 128], [131072, 4], [1, 1024]]),
                        in_=yabf.rearrange("p (hb f) -> p hb f", hb=4)),
                      waits=[(s_dve, dvem[sl])], incs=[(s_dwq, 16)])
                    dwqm = s_dwq.n

        am1_done, am2_done = w_absmean_dual()
        w_scale_finish(0, 0)
        w_scale_finish(1, 2)
        mw_ready = s_dsc.n
        wq_done_dve = 0
        wq1_written = 0

        NUP = 64
        unit_dve = [0] * NUP
        unit_pool = [0] * NUP

        def stp_slice(stp, i_):
            return bass.AP(stp.tensor, stp.offset + i_, [list(stp.ap[0]), [1, 2]])

        atp = a_t[:]
        btp = b_t[:]
        htp = h_t[:]
        wtp = wt[:]
        bBp = bcB[:]
        bCp = bcC[:]
        stp = stt_sb[:]
        sqa = sq[:]
        rep8_0 = bass.AP(sqa.tensor, sqa.offset, [list(sqa.ap[0]), [0, 8], [1, LC]])
        rep8_1 = bass.AP(sqa.tensor, sqa.offset + LC,
                         [list(sqa.ap[0]), [0, 8], [1, LC]])
        xnT3d = xnT[:].rearrange("p (r f) -> p r f", r=8)
        uT3d = uT[:].rearrange("p (r f) -> p r f", r=8)
        scsa = scs[:]
        screp8 = bass.AP(scsa.tensor, scsa.offset, [list(scsa.ap[0]), [0, 8], [1, LC]])
        screp4 = bass.AP(scsa.tensor, scsa.offset, [list(scsa.ap[0]), [0, 4], [1, LC]])
        hq = g[:].bitcast(BF16)
        ZB = 4096

        roll = {
            "outdma": 0, "bcw": (s_dbw, 0), "wln_dve": wq_done_dve,
            "mm1_pe": 0,
            "sq_pe": 0, "c2s_act": 0, "evxd_act": 0, "pre2": [0, 0],
            "psd2": [0, 0], "ln2m": [0, 0], "round_all": 0, "outtr_pe": 0,
            "uevac_act": 0, "tailevac_dve": 0, "mm1war2": [0, 0],
            "prep": None, "xq_dve": 0, "stage2_done": 0,
            "z_dve": 0, "xn_dve": am2_done,
        }

        def scale_rows(src_dve_mark, mw_off, sp_ln_all):
            dvv_ = src_dve_mark
            for k in range(TB):
                E("tensor", lambda e, kk=k: e.transpose(
                    ps_tr[:, kk * 128:(kk + 1) * 128],
                    mrun[:, kk * 128:(kk + 1) * 128], eye[:]),
                  waits=[(s_dve, dvv_),
                         (s_act, max(sp_ln_all, roll["uevac_act"]))])
                E("tensor", lambda e, kk=k: e.transpose(
                    ps_tr[:, 256 + kk * 128:256 + (kk + 1) * 128],
                    mnrun[:, kk * 128:(kk + 1) * 128], eye[:]),
                  incs=[(s_pe, 1)] if k == TB - 1 else ())
            for k in range(TB):
                E("vector", lambda e, kk=k: e.tensor_reduce(
                    acc[:, 1:2], ps_tr[:, kk * 128:(kk + 1) * 128], AX.X, OP.max),
                  waits=[(s_pe, s_pe.n)])
                E("vector", lambda e, kk=k: e.tensor_reduce(
                    acc[:, 0:1], ps_tr[:, 256 + kk * 128:256 + (kk + 1) * 128],
                    AX.X, OP.max))
                E("vector", lambda e: e.tensor_tensor(
                    acc[:, 1:2], acc[:, 1:2], acc[:, 0:1], OP.max))
                E("vector", lambda e: e.tensor_scalar_max(acc[:, 1:2], acc[:, 1:2],
                                                          EPS))
                E("vector", lambda e: e.reciprocal(acc[:, 0:1], acc[:, 1:2]))
                E("vector", lambda e: e.tensor_scalar_mul(acc[:, 2:3], acc[:, 0:1],
                                                          127.0))
                E("vector", lambda e, mo=mw_off: e.tensor_scalar(
                    acc[:, 3:4], acc[:, 1:2], mwc[:, mo:mo + 1], 1.0 / 127.0,
                    OP.mult, OP.mult),
                  waits=[(s_dsc, mw_ready)], incs=[(s_dve, 1)])
                dv2 = s_dve.n
                E("tensor", lambda e, kk=k: e.transpose(
                    ps_tr[0:1, kk * 128:(kk + 1) * 128], acc[:, 2:3], eye[:]),
                  waits=[(s_dve, dv2)])
                E("tensor", lambda e, kk=k: e.transpose(
                    ps_tr[0:1, 256 + kk * 128:256 + (kk + 1) * 128],
                    acc[:, 3:4], eye[:]),
                  incs=[(s_pe, 1)])
            pe_mark = s_pe.n
            E("vector", lambda e: e.tensor_copy(RW(3)[:], ps_tr[0:1, 0:LC]),
              waits=[(s_pe, pe_mark)])
            E("vector", lambda e: e.tensor_copy(RW(4)[:], ps_tr[0:1, 256:512]),
              incs=[(s_dve, 1)])
            return s_dve.n, pe_mark

        def prepA_steps(c):
            A = {"uevac_act": 0, "stats1_pe": 0, "pev": 0,
                 "sqm": [roll["sq_pe"], roll["sq_pe"]]}

            def mk_k(k):
                def fn():
                    E("sync", lambda e, kk=k, cc=c: e.dma_start(
                        out=utok[:],
                        in_=bass.AP(u_d, (cc * LC + kk * 128) * D,
                                    [[D, 128], [1, D]])),
                      waits=[(s_pe, A["pev"]),
                             (s_dve, roll["tailevac_dve"])],
                      incs=[(s_du, 16)])
                    duv = s_du.n
                    for half in range(2):
                        for j in range(4):
                            dt_ = half * 4 + j
                            E("tensor", lambda e, d_=dt_, jj=j: e.transpose(
                                ps_tr[:, jj * 128:(jj + 1) * 128],
                                utok[:, d_ * 128:(d_ + 1) * 128], eye[:]),
                              waits=[(s_du, duv), (s_ld, ld_all),
                                     (s_act, s_act.n), (s_dve, s_dve.n)],
                              incs=[(s_pe, 1)] if j == 3 else ())
                        pev = s_pe.n
                        E("scalar", lambda e, kk=k, hh=half: e.activation(
                            uT3[:, hh * 4:(hh + 1) * 4, kk * 128:(kk + 1) * 128],
                            tr3[:, :, :], AF.Identity),
                          waits=[(s_pe, pev), (s_dve, roll["xn_dve"])],
                          incs=[(s_act, 1)])
                    A["pev"] = s_pe.n
                    A["uevac_act"] = s_act.n
                    roll["uevac_act"] = s_act.n
                return fn

            def mk_stats(dlo, dhi):
                def fn():
                    sqm = A["sqm"]
                    for dt_ in range(dlo, dhi):
                        E("scalar", lambda e, d_=dt_: e.activation(
                            sq[:, (d_ % 2) * LC:(d_ % 2 + 1) * LC],
                            uT3[:, d_:d_ + 1, :], AF.Square),
                          waits=[(s_act, A["uevac_act"]), (s_pe, sqm[dt_ % 2]),
                                 (s_dve, roll["z_dve"])],
                          incs=[(s_act, 1)])
                        acv = s_act.n
                        E("tensor", lambda e, d_=dt_: e.matmul(
                            ps_b2[0:1, 0:LC], onc[:], uT3[:, d_:d_ + 1, :],
                            start=(d_ == 0), stop=(d_ == DT - 1)),
                          waits=[(s_act, A["uevac_act"]),
                                 (s_dve, roll["wln_dve"])])
                        E("tensor", lambda e, d_=dt_: e.matmul(
                            ps_b2[32:33, 0:LC], onc[:],
                            sq[:, (d_ % 2) * LC:(d_ % 2 + 1) * LC],
                            start=(d_ == 0), stop=(d_ == DT - 1)),
                          waits=[(s_act, acv)], incs=[(s_pe, 1)])
                        sqm[dt_ % 2] = s_pe.n
                    A["stats1_pe"] = s_pe.n
                return fn

            return A, [mk_k(0), mk_k(1), mk_stats(0, 4), mk_stats(4, DT)]

        def emit_prep(c, A, tailcb=None):
            cmod = c % 2
            M = {}
            cb = tailcb if tailcb is not None else (lambda: None)
            ut_evac_act = A["uevac_act"]
            M["uevac_act"] = ut_evac_act
            stats1_pe = A["stats1_pe"]
            cb()
            E("vector", lambda e: e.tensor_scalar_mul(RW(0)[:], ps_b2[0:1, 0:LC],
                                                      INV_D),
              waits=[(s_pe, stats1_pe)], incs=[(s_dve, 1)])
            rw0_dve = s_dve.n
            E("scalar", lambda e: e.activation(RW(3)[:], RW(0)[:], AF.Square),
              waits=[(s_pe, stats1_pe), (s_dve, rw0_dve if c == 0 else s_dve.n),
                     (s_dsc, mw_ready if c == 0 else 0)],
              incs=[(s_act, 1)])
            acv = s_act.n
            E("vector", lambda e: e.scalar_tensor_tensor(
                RW(3)[:], ps_b2[32:33, 0:LC], INV_D, RW(3)[:], OP.mult, OP.subtract),
              waits=[(s_act, acv)], incs=[(s_dve, 1)])
            dvv = s_dve.n
            E("scalar", lambda e: e.activation(RW(3)[:], RW(3)[:], AF.Ln, bias=EPS),
              waits=[(s_dve, dvv)])
            E("scalar", lambda e: e.activation(RW(1)[:], RW(3)[:], AF.Exp, scale=-0.5),
              incs=[(s_act, 1)])
            acv = s_act.n
            E("vector", lambda e: e.scalar_tensor_tensor(
                RW(2)[:], RW(0)[:], -1.0, RW(1)[:], OP.mult, OP.mult),
              waits=[(s_act, acv)], incs=[(s_dve, 1)])
            rows1_dve = s_dve.n
            E("tensor", lambda e: e.matmul(ps_b1[:, 0:LC], onr[:], RW(1)[:],
                                           start=True, stop=True),
              waits=[(s_dve, rows1_dve)])
            E("tensor", lambda e: e.matmul(ps_b1[:, 256:512], onr[:], RW(2)[:],
                                           start=True, stop=True),
              incs=[(s_pe, 1)])
            bc1_pe = s_pe.n
            cb()
            E("scalar", lambda e: e.activation(sq[:, 0:LC], ps_b1[:, 0:LC],
                                               AF.Identity),
              waits=[(s_pe, bc1_pe), (s_dve, 0 if c == 0 else s_dve.n)])
            E("scalar", lambda e: e.activation(sq[:, LC:2 * LC], ps_b1[:, 256:512],
                                               AF.Identity),
              incs=[(s_act, 1)])
            acv = s_act.n
            E("vector", lambda e: e.tensor_tensor(xnT3d, uT3d, rep8_0, OP.mult),
              waits=[(s_act, acv), (s_act, ut_evac_act)])
            E("vector", lambda e: e.tensor_tensor(xnT3d, xnT3d, rep8_1, OP.add),
              incs=[(s_dve, 1)])
            xn_dve = s_dve.n
            roll["xn_dve"] = xn_dve
            cb()
            for dt_ in range(DT):
                E("tensor", lambda e, d_=dt_: e.matmul(
                    ps_xd[0:96, 0:LC], wxT[:, d_ * 96:(d_ + 1) * 96],
                    xnT[:, d_ * LC:(d_ + 1) * LC],
                    start=(d_ == 0), stop=(d_ == DT - 1)),
                  waits=[(s_dve, xn_dve), (s_act, roll["evxd_act"])],
                  incs=[(s_pe, 1)] if dt_ == DT - 1 else ())
            xd_pe = s_pe.n
            E("scalar", lambda e: e.activation(dsb[0:64, :], ps_xd[0:64, 0:LC],
                                               AF.Identity),
              waits=[(s_pe, xd_pe), (s_dve, 0 if c == 0 else s_dve.n)],
              incs=[(s_act, 1)])
            E("scalar", lambda e: e.activation(bcsb[0:32, :], ps_xd[64:96, 0:LC],
                                               AF.Identity),
              waits=[roll["bcw"]], incs=[(s_act, 1)])
            evac_xd_act = s_act.n
            roll["evxd_act"] = evac_xd_act
            bwsem = s_am0 if c == 0 else s_dbw
            E("gpsimd" if c == 0 else "sync", lambda e, cm=cmod: e.dma_start(
                out=bass.AP(bc_d, cm * 32 * LC, [[LC, 32], [1, LC]]), in_=bcsb[:]),
              waits=[(s_act, evac_xd_act), (s_dbc, s_dbc.n), (s_dbc1, s_dbc1.n)],
              incs=[(bwsem, 16)], self_sync=False)
            roll["bcw"] = (bwsem, bwsem.n)
            M["bcw_now"] = (bwsem, bwsem.n)
            cb()
            psd2 = list(roll["psd2"])
            ln2m = list(roll["ln2m"])
            for dt_ in range(DT):
                dps = ps_d[:, 0:LC] if dt_ % 2 == 0 else ps_xd[:, 256:256 + LC]
                so = (dt_ % 2) * LC
                E("tensor", lambda e, d_=dt_, pp=dps: e.matmul(
                    pp, wdt[:, d_ * 128:(d_ + 1) * 128],
                    dsb[0:65, :], start=True, stop=True),
                  waits=[(s_act, evac_xd_act), (s_dve, psd2[dt_ % 2])],
                  incs=[(s_pe, 1)])
                pev = s_pe.n
                E("vector", lambda e, pp=dps, s_=so: e.tensor_single_scalar(
                    sp1[:, s_:s_ + LC].bitcast(I32), pp.bitcast(I32),
                    sgn[:, 0:1], OP.bitwise_or),
                  waits=[(s_pe, pev), (s_act, ln2m[dt_ % 2])],
                  incs=[(s_dve, 1)])
                dvv = s_dve.n
                E("scalar", lambda e, s_=so: e.activation(
                    sp2[:, s_:s_ + LC], sp1[:, s_:s_ + LC], AF.Exp),
                  waits=[(s_dve, dvv)])
                E("scalar", lambda e, s_=so: e.activation(
                    sp1[:, s_:s_ + LC], sp2[:, s_:s_ + LC], AF.Ln, bias=1.0),
                  incs=[(s_act, 1)])
                ln2m[dt_ % 2] = s_act.n
                acv = s_act.n
                E("vector", lambda e, d_=dt_, pp=dps, s_=so: e.scalar_tensor_tensor(
                    dl[:, d_ * LC:(d_ + 1) * LC], pp, 0.0,
                    sp1[:, s_:s_ + LC], OP.max, OP.add),
                  waits=[(s_act, acv)], incs=[(s_dve, 1)])
                psd2[dt_ % 2] = s_dve.n
                if dt_ % 2 == 1:
                    cb()
            roll["psd2"] = psd2
            roll["ln2m"] = ln2m
            M["sp_ln_all"] = max(ln2m)
            M["psd2"] = list(psd2)
            E("vector", lambda e: e.tensor_tensor(wt[:], dl[:], xnT[:], OP.mult),
              incs=[(s_dve, 1)])
            M["dl_dve"] = s_dve.n
            M["wt_dve"] = s_dve.n
            sqm2 = [stats1_pe, stats1_pe]
            for dt_ in range(DT):
                E("scalar", lambda e, d_=dt_: e.activation(
                    sq[:, (d_ % 2) * LC:(d_ % 2 + 1) * LC],
                    xnT[:, d_ * LC:(d_ + 1) * LC], AF.Square),
                  waits=[(s_pe, sqm2[dt_ % 2]), (s_dve, xn_dve)],
                  incs=[(s_act, 1)])
                acv = s_act.n
                E("tensor", lambda e, d_=dt_: e.matmul(
                    ps_b2[64:65, 0:LC], onc[:], xnT[:, d_ * LC:(d_ + 1) * LC],
                    start=(d_ == 0), stop=(d_ == DT - 1)),
                  waits=[(s_dve, xn_dve)])
                E("tensor", lambda e, d_=dt_: e.matmul(
                    ps_b2[0:1, 256:512], onc[:],
                    sq[:, (d_ % 2) * LC:(d_ % 2 + 1) * LC],
                    start=(d_ == 0), stop=(d_ == DT - 1)),
                  waits=[(s_act, acv)], incs=[(s_pe, 1)])
                sqm2[dt_ % 2] = s_pe.n
            stats2_pe = s_pe.n
            roll["sq_pe"] = stats2_pe
            E("vector", lambda e: e.tensor_scalar_mul(RW(0)[:], ps_b2[64:65, 0:LC],
                                                      INV_D),
              waits=[(s_pe, stats2_pe)])
            E("scalar", lambda e: e.activation(RW(3)[:], RW(0)[:], AF.Square),
              waits=[(s_pe, stats2_pe), (s_dve, s_dve.n)], incs=[(s_act, 1)])
            acv = s_act.n
            E("vector", lambda e: e.scalar_tensor_tensor(
                RW(3)[:], ps_b2[0:1, 256:512], INV_D, RW(3)[:], OP.mult, OP.subtract),
              waits=[(s_act, acv)], incs=[(s_dve, 1)])
            dvv = s_dve.n
            E("scalar", lambda e: e.activation(RW(3)[:], RW(3)[:], AF.Ln, bias=EPS),
              waits=[(s_dve, dvv)])
            E("scalar", lambda e: e.activation(RW(1)[:], RW(3)[:], AF.Exp, scale=-0.5),
              waits=[(s_pe, bc1_pe)], incs=[(s_act, 1)])
            acv = s_act.n
            E("vector", lambda e: e.tensor_tensor(RW(2)[:], RW(0)[:], RW(1)[:],
                                                  OP.mult),
              waits=[(s_act, acv), (s_pe, bc1_pe)], incs=[(s_dve, 1)])
            rows2_dve = s_dve.n
            E("tensor", lambda e: e.matmul(ps_b2[:, 0:LC], onr[:], RW(1)[:],
                                           start=True, stop=True),
              waits=[(s_dve, rows2_dve)])
            E("tensor", lambda e: e.matmul(ps_b2[:, 256:512], onr[:], RW(2)[:],
                                           start=True, stop=True),
              incs=[(s_pe, 1)])
            bc2_pe = s_pe.n
            E("scalar", lambda e: e.activation(sq[:, 0:LC], ps_b2[:, 0:LC],
                                               AF.Identity),
              waits=[(s_pe, bc2_pe), (s_dve, s_dve.n)])
            E("scalar", lambda e: e.activation(sq[:, LC:2 * LC], ps_b2[:, 256:512],
                                               AF.Identity),
              incs=[(s_act, 1)])
            acv = s_act.n
            zb = g[:, ZB:ZB + 2048].rearrange("p (r f) -> p r f", r=8)
            E("vector", lambda e: e.tensor_tensor(zb, xnT3d, rep8_0, OP.mult),
              waits=[(s_act, acv), (s_dve, wq_done_dve if c == 0 else 0),
                     (s_dwq, wq1_written if c == 0 else 0)])
            E("vector", lambda e: e.tensor_tensor(zb, zb, rep8_1, OP.subtract),
              incs=[(s_dve, 1)])
            z_dve = s_dve.n
            roll["z_dve"] = z_dve
            wlb = ZB + 2048
            for dt_ in range(DT):
                E("scalar", lambda e, d_=dt_: e.activation(
                    g[:, wlb + d_ * LC:wlb + (d_ + 1) * LC],
                    g[:, ZB + d_ * LC:ZB + (d_ + 1) * LC], AF.Identity,
                    bias=bet[:, d_:d_ + 1], scale=gam[:, d_:d_ + 1]),
                  waits=[(s_dve, z_dve)],
                  incs=[(s_act, 1)])
            wlnact = s_act.n
            E("vector", lambda e: e.tensor_tensor(
                sp1[:], g[:, wlb:wlb + 1024], g[:, wlb + 1024:wlb + 2048], OP.max),
              waits=[(s_act, wlnact)])
            E("vector", lambda e: e.tensor_tensor(
                sp1[:, 0:512], sp1[:, 0:512], sp1[:, 512:1024], OP.max))
            E("vector", lambda e: e.tensor_tensor(
                mrun[:], sp1[:, 0:256], sp1[:, 256:512], OP.max))
            E("vector", lambda e: e.tensor_tensor(
                sp2[:], g[:, wlb:wlb + 1024], g[:, wlb + 1024:wlb + 2048], OP.min))
            E("vector", lambda e: e.tensor_tensor(
                sp2[:, 0:512], sp2[:, 0:512], sp2[:, 512:1024], OP.min))
            E("vector", lambda e: e.tensor_tensor(
                sp2[:, 0:256], sp2[:, 0:256], sp2[:, 256:512], OP.min))
            E("vector", lambda e: e.tensor_scalar_mul(mnrun[:], sp2[:, 0:256], -1.0),
              incs=[(s_dve, 1)])
            wln_dve = s_dve.n
            roll["wln_dve"] = wln_dve
            q1rows_dve, _ = scale_rows(wln_dve, 0, M["sp_ln_all"])
            E("tensor", lambda e: e.matmul(ps_b3[:, 0:LC], onr[:], RW(3)[:],
                                           start=True, stop=True),
              waits=[(s_dve, q1rows_dve), (s_act, roll["c2s_act"])])
            E("tensor", lambda e: e.matmul(ps_b3[:, 256:512], onr[:], RW(4)[:],
                                           start=True, stop=True),
              incs=[(s_pe, 1)])
            bc3_pe = s_pe.n
            E("scalar", lambda e: e.activation(c1s[:], ps_b3[:, 256:512],
                                               AF.Identity),
              waits=[(s_pe, bc3_pe), (s_dve, max(roll["mm1war2"]))],
              incs=[(s_act, 1)])
            E("scalar", lambda e: e.activation(scs[:], ps_b3[:, 0:LC], AF.Identity),
              waits=[(s_dve, roll["round_all"])], incs=[(s_act, 1)])
            c1s_act = s_act.n
            M["c1s_act"] = c1s_act
            E("vector", lambda e: e.tensor_tensor(
                g[:, ZB:ZB + 2048].rearrange("p (r f) -> p r f", r=8),
                g[:, wlb:wlb + 2048].rearrange("p (r f) -> p r f", r=8),
                screp8, OP.mult),
              waits=[(s_act, c1s_act), (s_pe, roll["mm1_pe"])])
            E("vector", lambda e: e.tensor_scalar(
                xqT[:], g[:, ZB:ZB + 2048], MAGIC, MAGIC, OP.add, OP.subtract),
              incs=[(s_dve, 1)])
            M["xq_dve"] = s_dve.n
            M["bc_done"] = {}

            def bc_dma(pr):
                war = 0
                un = ((pr - 2) % 8) * DT + DT - 1
                if c > 0 or pr >= 2:
                    war = unit_dve[un]
                if c == 0 and pr < 2:
                    sem = s_am0 if pr == 0 else s_am1
                else:
                    sem = s_dbc if pr % 2 == 0 else s_dbc1
                for which, buf in ((0, bcB), (1, bcC)):
                    off = cmod * 32 * LC + (which * 16 + pr * 2) * LC
                    E("gpsimd" if (c == 0 and pr < 2) else "sync",
                      lambda e, o_=off, b_=buf, p_=pr: e.dma_start(
                        out=b_[:, (p_ % 2) * 512:(p_ % 2) * 512 + 512],
                        in_=bass.AP(bc_d, o_, [[0, 128], [LC, 2], [1, LC]])),
                      waits=[M["bcw_now"], (s_dve, war)],
                      incs=[(sem, 16)], self_sync=False)
                M["bc_done"][pr] = (sem, sem.n)

            M["bc_dma"] = bc_dma
            bc_dma(0)
            bc_dma(1)
            return M

        def emit_pd(c, M, pr, db):
            pd = pr * DT + db
            sl = pd % 2
            so = sl * 514
            bo = (pr % 2) * 512
            idx = db * N + pr * 2
            aw = unit_dve[(pd - 2) % NUP] if (c > 0 or pd >= 2) else 0
            aw = max(aw, roll.get("azero_dve", 0))
            pw = unit_pool[(pd - 2) % NUP] if (c > 0 or pd >= 2) else 0
            for j in range(2):
                E("scalar", lambda e, d_=db, i_=idx, j_=j, o_=so: e.activation(
                    bass.AP(atp.tensor, atp.offset + o_ + j_ * 257 + 1,
                            [list(atp.ap[0]), [1, 256]]),
                    dl[:, d_ * LC:(d_ + 1) * LC],
                    AF.Exp, scale=A_sb[:, i_ + j_:i_ + j_ + 1]),
                  waits=[(s_dve, max(aw, M["dl_dve"]))],
                  incs=[(s_act, 1)] if j == 1 else ())
            E("scalar", lambda e, o_=so, i_=idx: e.activation(
                bass.AP(btp.tensor, btp.offset + o_, [list(btp.ap[0]), [257, 2]]),
                stp_slice(stp, i_), AF.Identity),
              waits=[(s_pool, pw), (s_pe, roll["outtr_pe"]),
                     (s_dve, aw if (c > 0 or pd >= 2) else M["dl_dve"])],
              incs=[(s_act, 1)])
            a_act = s_act.n
            E("vector", lambda e, d_=db, o_=so, b_=bo: e.tensor_tensor(
                bass.AP(btp.tensor, btp.offset + o_ + 1,
                        [list(btp.ap[0]), [257, 2], [1, 256]]),
                bass.AP(wtp.tensor, wtp.offset + d_ * LC,
                        [list(wtp.ap[0]), [0, 2], [1, 256]]),
                bass.AP(bBp.tensor, bBp.offset + b_,
                        [list(bBp.ap[0]), [256, 2], [1, 256]]),
                OP.mult),
              waits=[M["bc_done"][pr], (s_pool, pw), (s_dve, M["wt_dve"]),
                     (s_pe, roll["outtr_pe"])])
            E("vector", lambda e, o_=so: e.tensor_tensor_scan(
                htp[0:128, 0:514],
                bass.AP(atp.tensor, atp.offset + o_, [list(atp.ap[0]), [1, 514]]),
                bass.AP(btp.tensor, btp.offset + o_, [list(btp.ap[0]), [1, 514]]),
                0.0, OP.mult, OP.add),
              waits=[(s_act, a_act)])
            E("vector", lambda e, i_=idx: e.tensor_copy(
                stp_slice(stp, i_),
                bass.AP(htp.tensor, htp.offset + 256, [list(htp.ap[0]), [257, 2]])))
            E("vector", lambda e, o_=so, b_=bo: e.tensor_tensor(
                bass.AP(btp.tensor, btp.offset + o_ + 1,
                        [list(btp.ap[0]), [257, 2], [1, 256]]),
                bass.AP(htp.tensor, htp.offset + 1,
                        [list(htp.ap[0]), [257, 2], [1, 256]]),
                bass.AP(bCp.tensor, bCp.offset + b_,
                        [list(bCp.ap[0]), [256, 2], [1, 256]]),
                OP.mult),
              incs=[(s_dve, 1)])
            unit_dve[pd] = s_dve.n
            E("gpsimd", lambda e, o_=so: e.tensor_tensor(
                ptree[:],
                bass.AP(btp.tensor, btp.offset + o_ + 1, [list(btp.ap[0]), [1, 256]]),
                bass.AP(btp.tensor, btp.offset + o_ + 258,
                        [list(btp.ap[0]), [1, 256]]),
                OP.add),
              waits=[(s_dve, unit_dve[pd]),
                     (s_act, M["yainit_act"][db]) if pr == 0 else (s_act, 0)])
            E("gpsimd", lambda e, d_=db: e.tensor_tensor(
                ya[:, d_ * LC:(d_ + 1) * LC],
                ya[:, d_ * LC:(d_ + 1) * LC], ptree[:], OP.add),
              incs=[(s_pool, 1)])
            unit_pool[pd] = s_pool.n

        def emit_stage2A(c, M, ht):
            sl = ht % 2
            wsem = s_dw if ht % 2 == 0 else s_dw1
            E("sync", lambda e, h_=ht, s_=sl: e.dma_start(
                out=wq1s[:, s_ * 1024:(s_ + 1) * 1024],
                in_=bass.AP(wq1_d, h_ * 131072, [[1024, 128], [1, 1024]])),
              waits=[(s_dwq, wq1_written),
                     (s_pe, M["mm1_marks"][ht - 2] if ht >= 2 else roll["mm1_pe"])],
              incs=[(wsem, 16)])
            dwv = (wsem, wsem.n)
            m1ps = (ps_m1 if ht % 2 == 0 else ps_m2)[:, 0:LC]
            for dt_ in range(DT):
                E("tensor", lambda e, d_=dt_, s_=sl, pp=m1ps: e.matmul(
                    pp,
                    wq1s[:, s_ * 1024 + d_ * 128:s_ * 1024 + (d_ + 1) * 128],
                    xqT[:, d_ * LC:(d_ + 1) * LC],
                    start=(d_ == 0), stop=(d_ == DT - 1)),
                  waits=[dwv, (s_dve, M["xq_dve"]),
                         (s_dve, M["pre_marks"][ht - 2] if ht >= 2
                          else roll["mm1war2"][sl])],
                  incs=[(s_pe, 1)] if dt_ == DT - 1 else ())
            M["mm1_marks"][ht] = s_pe.n
            E("vector", lambda e, s_=sl, pp=m1ps: e.tensor_tensor(
                pre[:, s_ * LC:(s_ + 1) * LC], pp, c1s[:], OP.mult),
              waits=[(s_pe, M["mm1_marks"][ht]), (s_act, M["c1s_act"]),
                     (s_act, M["gelu_marks"][ht - 2] if ht >= 2 else 0)],
              incs=[(s_dve, 1)])
            M["pre_marks"][ht] = s_dve.n

        def emit_stage2B(c, M, ht):
            sl = ht % 2
            E("scalar", lambda e, h_=ht, s_=sl: e.activation(
                g[:, h_ * LC:(h_ + 1) * LC], pre[:, s_ * LC:(s_ + 1) * LC],
                AF.Gelu, bias=b1c[:, h_:h_ + 1]),
              waits=[(s_dve, M["pre_marks"][ht])],
              incs=[(s_act, 1)])
            M["gelu_marks"][ht] = s_act.n
            if ht % 4 == 3:
                qb = (ht - 3) * LC
                if ht == 3:
                    E("vector", lambda e, q_=qb: e.tensor_copy(
                        sp1[:], g[:, q_:q_ + 1024]),
                      waits=[(s_act, M["gelu_marks"][ht]),
                             (s_pe, roll["outtr_pe"])],
                      incs=[(s_dve, 1)])
                    E("vector", lambda e, q_=qb: e.tensor_scalar_mul(
                        sp2[:], g[:, q_:q_ + 1024], -1.0),
                      incs=[(s_dve, 1)])
                else:
                    E("vector", lambda e, q_=qb: e.tensor_tensor(
                        sp1[:], sp1[:], g[:, q_:q_ + 1024], OP.max),
                      waits=[(s_act, M["gelu_marks"][ht])], incs=[(s_dve, 1)])
                    E("vector", lambda e, q_=qb: e.scalar_tensor_tensor(
                        sp2[:], g[:, q_:q_ + 1024], -1.0, sp2[:],
                        OP.mult, OP.max),
                      incs=[(s_dve, 1)])

        def emit_q2_round(c, M):
            roll["mm1_pe"] = M["mm1_marks"][HT - 1]
            roll["mm1war2"] = [M["pre_marks"][HT - 2], M["pre_marks"][HT - 1]]
            E("vector", lambda e: e.tensor_tensor(
                sp1[:, 0:512], sp1[:, 0:512], sp1[:, 512:1024], OP.max))
            E("vector", lambda e: e.tensor_tensor(
                mrun[:], sp1[:, 0:256], sp1[:, 256:512], OP.max))
            E("vector", lambda e: e.tensor_tensor(
                sp2[:, 0:512], sp2[:, 0:512], sp2[:, 512:1024], OP.max))
            E("vector", lambda e: e.tensor_tensor(
                mnrun[:], sp2[:, 0:256], sp2[:, 256:512], OP.max),
              incs=[(s_dve, 1)])
            q2max_dve = s_dve.n
            q2rows_dve, _ = scale_rows(q2max_dve, 2, M["sp_ln_all"])
            E("tensor", lambda e: e.matmul(ps_b3[:, 0:LC], onr[:], RW(3)[:],
                                           start=True, stop=True),
              waits=[(s_dve, q2rows_dve), (s_act, M["c1s_act"])])
            E("tensor", lambda e: e.matmul(ps_b3[:, 256:512], onr[:], RW(4)[:],
                                           start=True, stop=True),
              incs=[(s_pe, 1)])
            bc32_pe = s_pe.n
            E("scalar", lambda e: e.activation(c2s[:], ps_b3[:, 256:512],
                                               AF.Identity),
              waits=[(s_pe, bc32_pe), (s_dve, s_dve.n)], incs=[(s_act, 1)])
            E("scalar", lambda e: e.activation(scs[:], ps_b3[:, 0:LC], AF.Identity),
              waits=[(s_dve, M["xq_dve"])], incs=[(s_act, 1)])
            c2s_act = s_act.n
            roll["c2s_act"] = c2s_act
            M["c2s_act"] = c2s_act
            round_marks = [0] * HT
            for q_ in range(HT // 4):
                E("vector", lambda e, qq=q_: e.tensor_tensor(
                    sp1[:].rearrange("p (r f) -> p r f", r=4),
                    g[:, qq * 1024:(qq + 1) * 1024].rearrange(
                        "p (r f) -> p r f", r=4),
                    screp4, OP.mult),
                  waits=[(s_act, c2s_act), (s_pe, roll["outtr_pe"])])
                E("vector", lambda e, qq=q_: e.tensor_scalar(
                    hq[:, qq * 1024:(qq + 1) * 1024], sp1[:],
                    MAGIC, MAGIC, OP.add, OP.subtract),
                  incs=[(s_dve, 1)])
                for j in range(4):
                    round_marks[q_ * 4 + j] = s_dve.n
            M["round_marks"] = round_marks
            roll["round_all"] = s_dve.n

        def _mm2_group(c, M, dt_, war):
            m2ps = (ps_m1 if dt_ % 2 == 0 else ps_m2)[:, 256:256 + LC]
            for ht in range(HT):
                E("tensor", lambda e, d_=dt_, h_=ht, pp=m2ps: e.matmul(
                    pp,
                    wq2[:, h_ * 1024 + d_ * 128:h_ * 1024 + (d_ + 1) * 128],
                    hq[:, h_ * LC:(h_ + 1) * LC],
                    start=(h_ == 0), stop=(h_ == HT - 1)),
                  waits=[(s_dve, M["round_marks"][ht]), (s_dve, war[dt_ % 2])],
                  incs=[(s_pe, 1)] if ht == HT - 1 else ())
            M["mm2_pe"][dt_] = s_pe.n

        def emit_mm2(c, M):
            M["mm2_pe"] = [0] * DT
            M["m2war"] = list(roll.get("m2war2", [0, 0]))
            _mm2_group(c, M, 0, M["m2war"])
            _mm2_group(c, M, 1, M["m2war"])

        def make_tailcb(c, M):
            state = {"dt": 0}
            M["e3_marks"] = [0] * DT
            war = M["m2war"]

            def cb():
                dt_ = state["dt"]
                if dt_ >= DT:
                    return
                state["dt"] += 1
                m2ps = (ps_m1 if dt_ % 2 == 0 else ps_m2)[:, 256:256 + LC]
                eo = (atp if dt_ < 4 else btp)
                E("vector", lambda e, pp=m2ps: e.tensor_tensor(
                    f1[:], pp, c2s[:], OP.mult),
                  waits=[(s_pe, M["mm2_pe"][dt_]), (s_act, M["c2s_act"])],
                  incs=[(s_dve, 1)])
                war[dt_ % 2] = s_dve.n
                E("vector", lambda e, d_=dt_, b_=eo: e.tensor_tensor(
                    bass.AP(b_.tensor, b_.offset + (d_ % 4) * LC,
                            [list(b_.ap[0]), [1, LC]]),
                    f1[:], ya[:, d_ * LC:(d_ + 1) * LC], OP.add),
                  waits=[(s_pool, M["mamba_pool"])],
                  incs=[(s_dve, 1)])
                M["e3_marks"][dt_] = s_dve.n
                if dt_ + 2 < DT:
                    _mm2_group(c, M, dt_ + 2, war)
                if dt_ == DT - 1:
                    roll["m2war2"] = war

            return cb

        def emit_out(c, M):
            for k in range(TB):
                E("sync", lambda e, kk=k, c2=c: e.dma_start(
                    out=utok[:],
                    in_=bass.AP(u_d, (c2 * LC + kk * 128) * D, [[D, 128], [1, D]])),
                  waits=[(s_pe, s_pe.n), (s_dve, roll["tailevac_dve"])],
                  incs=[(s_du, 16)])
                duv = s_du.n
                for half in range(2):
                    for j in range(4):
                        dt_ = half * 4 + j
                        src = (atp if dt_ < 4 else btp)
                        E("tensor", lambda e, d_=dt_, jj=j, kk=k, ss=src:
                          e.transpose(
                            ps_tr[:, jj * 128:(jj + 1) * 128],
                            bass.AP(ss.tensor,
                                    ss.offset + (d_ % 4) * LC + kk * 128,
                                    [list(ss.ap[0]), [1, 128]]),
                            eye[:]),
                          waits=[(s_dve, M["e3_marks"][dt_]), (s_dve, s_dve.n),
                                 (s_act, s_act.n)],
                          incs=[(s_pe, 1)] if j == 3 else ())
                    pev = s_pe.n
                    E("vector", lambda e, hh=half: e.tensor_tensor(
                        outst[:, hh * 512:(hh + 1) * 512], ps_tr[:, 0:512],
                        utok[:, hh * 512:(hh + 1) * 512], OP.add),
                      waits=[(s_pe, pev), (s_du, duv), (s_do, roll["outdma"])],
                      incs=[(s_dve, 1)])
                roll["tailevac_dve"] = s_dve.n
                roll["outtr_pe"] = s_pe.n
                dvv = s_dve.n
                E("sync", lambda e, kk=k, c2=c: e.dma_start(
                    out=bass.AP(o_d, (c2 * LC + kk * 128) * D, [[D, 128], [1, D]]),
                    in_=outst[:]),
                  waits=[(s_dve, dvv)], incs=[(s_do, 16)])
                roll["outdma"] = s_do.n
            E("vector", lambda e: e.memset(
                bass.AP(atp.tensor, atp.offset, [list(atp.ap[0]), [257, 4]]), 0.0),
              waits=[(s_pe, roll["outtr_pe"])], incs=[(s_dve, 1)])
            roll["azero_dve"] = s_dve.n

        def emit_yainit(c, M):
            M["yainit_act"] = [0] * DT
            for dt_ in range(DT):
                E("scalar", lambda e, d_=dt_: e.activation(
                    ya[:, d_ * LC:(d_ + 1) * LC], xnT[:, d_ * LC:(d_ + 1) * LC],
                    AF.Identity, bias=b2c[:, d_:d_ + 1], scale=Dg[:, d_:d_ + 1]),
                  waits=[(s_dve, max(M["ya_free_dve"][dt_], M["xq_dve"]))],
                  incs=[(s_act, 1)])
                M["yainit_act"][dt_] = s_act.n

        def emit_main_rest(c, M, start_pr=0, delay=0, lag=1, hoist=None):
            M["mm1_marks"] = [0] * HT
            M["pre_marks"] = [0] * HT
            M["gelu_marks"] = [0] * HT
            hsteps = dict(zip((12, 20, 28, 36), hoist)) if hoist else {}
            npds = (8 - start_pr) * DT
            doneA = 0
            doneB = 0
            i = 0
            for pr in range(start_pr, 8):
                for db in range(DT):
                    emit_pd(c, M, pr, db)
                    i += 1
                    if i in hsteps:
                        hsteps[i]()
                    if i > delay:
                        want = min(HT, ((i - delay) * HT) // (npds - delay - 4) + 1)
                        while doneA < want:
                            while doneB < doneA - 1:
                                emit_stage2B(c, M, doneB)
                                doneB += 1
                            emit_stage2A(c, M, doneA)
                            doneA += 1
                        while doneB < max(0, doneA - lag):
                            emit_stage2B(c, M, doneB)
                            doneB += 1
                if pr + 2 < 8:
                    M["bc_dma"](pr + 2)
            while doneA < HT:
                emit_stage2A(c, M, doneA)
                doneA += 1
            while doneB < HT:
                emit_stage2B(c, M, doneB)
                doneB += 1
            M["mamba_pool"] = s_pool.n

        A, asteps = prepA_steps(0)
        for fn in asteps:
            fn()
        emit_quant(am1_done)
        wq_done_dve = s_dve.n
        wq1_written = s_dwq.n
        M = emit_prep(0, A)
        M["ya_free_dve"] = [0] * DT
        emit_yainit(0, M)
        An, asteps = prepA_steps(1)
        emit_main_rest(0, M, 0, hoist=asteps)
        for c in range(NCH):
            emit_q2_round(c, M)
            emit_mm2(c, M)
            cb = make_tailcb(c, M)
            if c + 1 < NCH:
                Mn = emit_prep(c + 1, An, tailcb=cb)
                while M["e3_marks"][DT - 1] == 0:
                    cb()
                emit_out(c, M)
                Mn["ya_free_dve"] = list(M["e3_marks"])
                emit_yainit(c + 1, Mn)
                if c + 2 < NCH:
                    An, asteps = prepA_steps(c + 2)
                else:
                    asteps = None
                emit_main_rest(c + 1, Mn, 0, hoist=asteps)
                M = Mn
            else:
                for _ in range(DT):
                    cb()
                emit_out(c, M)

        E("gpsimd", lambda e: e.engine_nop(), waits=[(s_do, s_do.n)])

        import os
        if os.environ.get("BASS_PRINT_SBUF"):
            print("sbuf remaining/partition:", nc.sbuf_bytes_remaining,
                  "psum banks remaining:", nc.psum_banks_remaining)
        with nc.Block() as block:
            em.flush(block, max_seq=max_seq)

    return nc


def kernel(L0_attn_u, L0_attn_W_x, L0_attn_W_dt, L0_attn_b_dt, L0_attn_log_A,
           L0_attn_D_param, L0_ffn_w1, L0_ffn_b1, L0_ffn_w2, L0_ffn_b2,
           L0_ffn_gamma, L0_ffn_beta):
    u = np.asarray(L0_attn_u, dtype=np.float32)
    wx = np.asarray(L0_attn_W_x, dtype=np.float32)
    wdt = np.asarray(L0_attn_W_dt, dtype=np.float32)
    bdt = np.asarray(L0_attn_b_dt, dtype=np.float32)
    logA = np.asarray(L0_attn_log_A, dtype=np.float32)
    Dp = np.asarray(L0_attn_D_param, dtype=np.float32)
    w1 = np.asarray(L0_ffn_w1, dtype=np.float32)
    b1 = np.asarray(L0_ffn_b1, dtype=np.float32)
    w2 = np.asarray(L0_ffn_w2, dtype=np.float32)
    b2 = np.asarray(L0_ffn_b2, dtype=np.float32)
    gam = np.asarray(L0_ffn_gamma, dtype=np.float32)
    bet = np.asarray(L0_ffn_beta, dtype=np.float32)

    wxT = np.ascontiguousarray(wx.T)
    wdtTb = np.ascontiguousarray(np.concatenate([wdt.T, bdt[None, :]], axis=0))
    logA_r = np.ascontiguousarray(
        logA.reshape(DT, 128, N).transpose(1, 0, 2).reshape(128, DT * N))
    r8 = lambda v: np.ascontiguousarray(v.reshape(DT, 128).T)
    base = dict(
        wxT=wxT, wdtTb=wdtTb, logA_r=logA_r,
        D_r=r8(Dp), gamma_r=r8(gam), beta_r=r8(bet), b2_r=r8(b2),
        b1_r=np.ascontiguousarray(b1.reshape(HT, 128).T),
        w1T=np.ascontiguousarray(w1.T), w2T=np.ascontiguousarray(w2.T),
        eye=np.eye(128, dtype=np.float32),
        ones_row=np.ones((1, 128), dtype=np.float32),
        ones_col=np.ones((128, 1), dtype=np.float32),
        sgnm=np.full((128, 1), np.int32(-2147483648), dtype=np.int32),
        cst=np.broadcast_to(np.array([MAGIC, EPS], dtype=np.float32),
                            (128, 2)).copy(),
    )
    nc = build_nc()
    in_maps = [dict(base, u=np.ascontiguousarray(u[i])) for i in range(8)]
    res = run_bass_kernel_spmd(nc, in_maps, core_ids=list(range(8)))
    return np.stack([res.results[i]["o"] for i in range(8)], axis=0)


# version 42
# speedup vs baseline: 1.2064x; 1.0292x over previous
import numpy as np
import concourse.bass as bass
import concourse.mybir as mybir
from concourse.bass_utils import run_bass_kernel_spmd

F32 = mybir.dt.float32
BF16 = mybir.dt.bfloat16
I32 = mybir.dt.int32
AF = mybir.ActivationFunctionType
OP = mybir.AluOpType
AX = mybir.AxisListType

T, D, N, H, R = 4096, 1024, 16, 4096, 64
LC = 256
NCH = T // LC
TB = LC // 128
DT = D // 128
HT = H // 128
NU = N * DT
EPS = 1e-5
MAGIC = 12582912.0
INV_D = 1.0 / D


class Sem:
    def __init__(self, nc, es, name):
        self.h = es.enter_context(nc.semaphore(name))
        self.n = 0


class Emitter:
    ENGS = ("sync", "tensor", "vector", "scalar", "gpsimd")

    def __init__(self, self_sync_sems, use_drain=False):
        self.ops = {e: [] for e in self.ENGS}
        self.waited = {e: {} for e in self.ENGS}
        self.seq = 0
        self.self_sync = self_sync_sems
        self.use_drain = use_drain

    def op(self, eng, body, waits=(), incs=(), self_sync=True):
        own = self.self_sync.get(eng) if self_sync else None
        wl = []
        drain = False
        if own is not None and own.n > 0:
            if self.use_drain and eng != "tensor":
                drain = True
            else:
                wl.append((own.h, own.n))
                self.waited[eng][id(own)] = max(
                    self.waited[eng].get(id(own), 0), own.n)
        for s, v in waits:
            if v is None or v <= 0:
                continue
            key = id(s)
            if v > self.waited[eng].get(key, 0):
                wl.append((s.h, v))
                self.waited[eng][key] = v
        il = []
        own_inced = False
        for s, k in incs:
            s.n += k
            il.append((s.h, k))
            if s is own:
                own_inced = True
        if own is not None and not own_inced:
            own.n += 1
            il.append((own.h, 1))
        self.ops[eng].append((wl, body, il, self.seq, drain))
        self.seq += 1

    def flush(self, block, max_seq=None):
        def mk(lst):
            def run(eng):
                for wl, body, il, sq_, drain in lst:
                    if max_seq is not None and sq_ > max_seq:
                        continue
                    if drain:
                        eng.drain()
                    for h, v in wl:
                        eng.wait_ge(h, v)
                    ins = body(eng)
                    for h, k in il:
                        ins.then_inc(h, k)
            return run
        block.sync(mk(self.ops["sync"]))
        block.tensor(mk(self.ops["tensor"]))
        block.vector(mk(self.ops["vector"]))
        block.scalar(mk(self.ops["scalar"]))
        block.gpsimd(mk(self.ops["gpsimd"]))


def build_nc(full_sync=False, nch=NCH, max_seq=None):
    nc = bass.Bass("TRN2", target_bir_lowering=False,
                   detect_race_conditions=full_sync)

    u_d = nc.dram_tensor("u", [T, D], F32, kind="ExternalInput")
    o_d = nc.dram_tensor("o", [T, D], F32, kind="ExternalOutput")
    wxT_d = nc.dram_tensor("wxT", [D, 96], F32, kind="ExternalInput")
    wdtTb_d = nc.dram_tensor("wdtTb", [R + 1, D], F32, kind="ExternalInput")
    logA_d = nc.dram_tensor("logA_r", [128, DT * N], F32, kind="ExternalInput")
    Dp_d = nc.dram_tensor("D_r", [128, DT], F32, kind="ExternalInput")
    gam_d = nc.dram_tensor("gamma_r", [128, DT], F32, kind="ExternalInput")
    bet_d = nc.dram_tensor("beta_r", [128, DT], F32, kind="ExternalInput")
    b2_d = nc.dram_tensor("b2_r", [128, DT], F32, kind="ExternalInput")
    b1_d = nc.dram_tensor("b1_r", [128, HT], F32, kind="ExternalInput")
    w1T_d = nc.dram_tensor("w1T", [D, H], F32, kind="ExternalInput")
    w2T_d = nc.dram_tensor("w2T", [H, D], F32, kind="ExternalInput")
    eye_d = nc.dram_tensor("eye", [128, 128], F32, kind="ExternalInput")
    onr_d = nc.dram_tensor("ones_row", [1, 128], F32, kind="ExternalInput")
    onc_d = nc.dram_tensor("ones_col", [128, 1], F32, kind="ExternalInput")
    sgn_d = nc.dram_tensor("sgnm", [128, 1], I32, kind="ExternalInput")
    cst_d = nc.dram_tensor("cst", [128, 2], F32, kind="ExternalInput")

    wq1_d = nc.dram_tensor("wq1d", [HT * 128 * 1024], BF16, kind="Internal")
    bc_d = nc.dram_tensor("bcd", [2 * 32 * LC], F32, kind="Internal")
    scl_d = nc.dram_tensor("scld", [8], F32, kind="Internal")

    from contextlib import ExitStack
    es = ExitStack()
    with es:
        sb = nc.sbuf_tensor
        wq2 = es.enter_context(sb("wq2", [128, HT * 1024], BF16))
        g = es.enter_context(sb("g", [128, HT * LC], F32))
        uT = es.enter_context(sb("uT", [128, DT * LC], F32))
        xnT = es.enter_context(sb("xnT", [128, DT * LC], F32))
        dl = es.enter_context(sb("dl", [128, DT * LC], F32))
        wt = es.enter_context(sb("wt", [128, DT * LC], F32))
        ya = es.enter_context(sb("ya", [128, DT * LC], F32))
        xqT = es.enter_context(sb("xqT", [128, DT * LC], BF16))
        utok = es.enter_context(sb("utok", [128, D], F32))
        outst = es.enter_context(sb("outst", [128, D], F32))
        wq1s = es.enter_context(sb("wq1s", [128, 2 * 1024], BF16))
        bcB = es.enter_context(sb("bcB", [128, 2 * 512], F32))
        bcC = es.enter_context(sb("bcC", [128, 2 * 512], F32))
        a_t = es.enter_context(sb("a_t", [128, 2 * 514], F32))
        b_t = es.enter_context(sb("b_t", [128, 2 * 514], F32))
        h_t = es.enter_context(sb("h_t", [128, 514], F32))
        ptree = es.enter_context(sb("ptree", [128, 256], F32))
        dsb = es.enter_context(sb("dsb", [128, LC], F32))
        bcsb = es.enter_context(sb("bcsb", [32, LC], F32))
        sq = es.enter_context(sb("sq", [128, 2 * LC], F32))
        sp1 = es.enter_context(sb("sp1", [128, 1024], F32))
        sp2 = es.enter_context(sb("sp2", [128, 1024], F32))
        pre = es.enter_context(sb("pre", [128, 2 * LC], F32))
        mrun = es.enter_context(sb("mrun", [128, LC], F32))
        mnrun = es.enter_context(sb("mnrun", [128, LC], F32))
        c1s = es.enter_context(sb("c1s", [128, LC], F32))
        scs = es.enter_context(sb("scs", [128, LC], F32))
        c2s = es.enter_context(sb("c2s", [128, LC], F32))
        f1 = es.enter_context(sb("f1", [128, LC], F32))
        f2 = es.enter_context(sb("f2", [128, LC], F32))
        stt_sb = es.enter_context(sb("stt", [128, NU], F32))
        wxT = es.enter_context(sb("wxTs", [128, DT * 96], F32))
        wdt = es.enter_context(sb("wdts", [R + 1, D], F32))
        A_sb = es.enter_context(sb("A_sb", [128, DT * N], F32))
        Dg = es.enter_context(sb("Dg", [128, DT], F32))
        gam = es.enter_context(sb("gam", [128, DT], F32))
        bet = es.enter_context(sb("bet", [128, DT], F32))
        b2c = es.enter_context(sb("b2c", [128, DT], F32))
        b1c = es.enter_context(sb("b1c", [128, HT], F32))
        eye = es.enter_context(sb("eyes", [128, 128], F32))
        onr = es.enter_context(sb("onr", [1, 128], F32))
        onc = es.enter_context(sb("onc", [128, 1], F32))
        sgn = es.enter_context(sb("sgn", [128, 1], I32))
        mwc = es.enter_context(sb("mwc", [128, 4], F32))
        acc = es.enter_context(sb("acc", [128, 4], F32))
        cst = es.enter_context(sb("cst_s", [128, 2], F32))
        rows = es.enter_context(sb("rows", [1, 5 * LC], F32))
        RW = lambda i: rows[0:1, i * LC:(i + 1) * LC]

        ps = nc.psum_tensor
        ps_tr = es.enter_context(ps("ps_tr", [128, 512], F32))
        ps_xd = es.enter_context(ps("ps_xd", [128, 512], F32))
        ps_d = es.enter_context(ps("ps_d", [128, 512], F32))
        ps_b1 = es.enter_context(ps("ps_b1", [128, 512], F32))
        ps_b2 = es.enter_context(ps("ps_b2", [128, 512], F32))
        ps_b3 = es.enter_context(ps("ps_b3", [128, 512], F32))
        ps_m1 = es.enter_context(ps("ps_m1", [128, 512], F32))
        ps_m2 = es.enter_context(ps("ps_m2", [128, 512], F32))

        sems = {k: Sem(nc, es, "s_" + k) for k in
                ("ld", "du", "dw", "dwq", "dbc", "dbw", "dsc", "do",
                 "pe", "act", "dve", "pool", "dbc1", "dw1", "am0", "am1")}
        s_am0, s_am1 = sems["am0"], sems["am1"]
        s_ld, s_du, s_dw, s_dwq = (sems[k] for k in ("ld", "du", "dw", "dwq"))
        s_dbc, s_dbw, s_dsc, s_do = (sems[k] for k in ("dbc", "dbw", "dsc", "do"))
        s_dbc1, s_dw1 = sems["dbc1"], sems["dw1"]
        s_pe, s_act, s_dve, s_pool = (sems[k] for k in ("pe", "act", "dve", "pool"))

        nc.const_aps.register(F32, MAGIC, cst[:, 0:1])
        nc.const_aps.register(F32, EPS, cst[:, 1:2])

        self_sync = {"vector": s_dve, "scalar": s_act, "gpsimd": s_pool}
        if full_sync:
            self_sync["tensor"] = s_pe
        em = Emitter(self_sync, use_drain=not full_sync)
        E = em.op

        uT3 = uT[:].rearrange("p (dt l) -> p dt l", dt=DT)
        tr3 = ps_tr[:].rearrange("p (j l) -> p j l", j=4)

        for dst, src in ((wxT[:], bass.AP(wxT_d, 0, [[96, 128], [96 * 128, DT], [1, 96]])),
                         (wdt[:], wdtTb_d.ap()),
                         (A_sb[:], logA_d.ap()),
                         (Dg[:], Dp_d.ap()), (gam[:], gam_d.ap()),
                         (bet[:], bet_d.ap()), (b2c[:], b2_d.ap()),
                         (b1c[:], b1_d.ap()), (eye[:], eye_d.ap()),
                         (onr[:], onr_d.ap()), (onc[:], onc_d.ap()),
                         (sgn[:], sgn_d.ap()),
                         (cst[:], cst_d.ap())):
            E("sync", lambda e, d_=dst, s_=src: e.dma_start(out=d_, in_=s_),
              incs=[(s_ld, 16)])
        ld_all = s_ld.n

        E("scalar", lambda e: e.activation(sp1[:, 0:DT * N], A_sb[:], AF.Exp),
          waits=[(s_ld, ld_all)], incs=[(s_act, 1)])
        E("vector", lambda e: e.tensor_scalar_mul(A_sb[:], sp1[:, 0:DT * N], -1.0),
          waits=[(s_act, s_act.n)], incs=[(s_dve, 1)])
        E("vector", lambda e: e.memset(dsb[64:65, :], 1.0), incs=[(s_dve, 1)])
        E("vector", lambda e: e.memset(a_t[:], 0.0), incs=[(s_dve, 1)])
        E("vector", lambda e: e.memset(stt_sb[:], 0.0), incs=[(s_dve, 1)])

        stg = g[:, 4096:8192]
        stg2 = g[:, 0:4096]
        wq_bf = g[:].bitcast(BF16)[0:128, 8192:12288]

        def wblk_ap(is_w1, blk):
            if is_w1:
                return bass.AP(w1T_d, blk * 512, [[H, 128], [H * 128, DT], [1, 512]])
            return bass.AP(w2T_d, blk * 4 * 128 * D, [[D, 128], [D * 128, 4], [1, D]])

        def w_half_ap(is_w1, hb):
            if is_w1:
                return bass.AP(w1T_d,
                               (hb // 2) * 512 + (hb % 2) * (H * 128 * 4),
                               [[H, 128], [H * 128, 4], [1, 512]])
            return bass.AP(w2T_d, hb * (D * 128 * 2),
                           [[D, 128], [D * 128, 2], [1, D]])

        def w_absmean_dual():
            am1 = [0, 0]
            am2 = [0, 0]
            sems1 = (s_dw, s_dw1)
            sems2 = (s_am0, s_am1)
            for hb in range(16):
                sl = hb % 2
                b1 = stg[:, sl * 2048:(sl + 1) * 2048]
                E("sync", lambda e, a=w_half_ap(True, hb), b_=b1: e.dma_start(
                    out=b_, in_=a),
                  waits=[(s_dve, am1[sl])], incs=[(sems1[sl], 16)])
                d1 = sems1[sl].n
                E("vector", lambda e, b_=b1: e.tensor_reduce(
                    acc[:, 1:2], b_, AX.X, OP.add, apply_absolute_value=True),
                  waits=[(sems1[sl], d1)])
                if hb == 0:
                    E("vector", lambda e: e.tensor_copy(acc[:, 0:1],
                                                        acc[:, 1:2]),
                      incs=[(s_dve, 1)])
                else:
                    E("vector", lambda e: e.tensor_tensor(
                        acc[:, 0:1], acc[:, 0:1], acc[:, 1:2], OP.add),
                      incs=[(s_dve, 1)])
                am1[sl] = s_dve.n
                b2_ = uT[:] if sl == 0 else xnT[:]
                E("gpsimd", lambda e, a=w_half_ap(False, hb), b_=b2_:
                  e.dma_start(out=b_, in_=a),
                  waits=[(s_dve, am2[sl])],
                  incs=[(sems2[sl], 16)], self_sync=False)
                d2 = sems2[sl].n
                E("vector", lambda e, b_=b2_: e.tensor_reduce(
                    acc[:, 3:4], b_, AX.X, OP.add, apply_absolute_value=True),
                  waits=[(sems2[sl], d2)])
                if hb == 0:
                    E("vector", lambda e: e.tensor_copy(acc[:, 2:3],
                                                        acc[:, 3:4]),
                      incs=[(s_dve, 1)])
                else:
                    E("vector", lambda e: e.tensor_tensor(
                        acc[:, 2:3], acc[:, 2:3], acc[:, 3:4], OP.add),
                      incs=[(s_dve, 1)])
                am2[sl] = s_dve.n
            return am1, max(am2)

        def w_scale_finish(col, acol):
            E("tensor", lambda e, ac=acol: e.matmul(
                ps_tr[0:1, 0:1], onc[:], acc[:, ac:ac + 1],
                start=True, stop=True),
              waits=[(s_dve, s_dve.n), (s_ld, ld_all)], incs=[(s_pe, 1)])
            pev = s_pe.n
            E("vector", lambda e: e.tensor_scalar(
                RW(3)[:, 0:1], ps_tr[0:1, 0:1], 1.0 / (D * H), EPS, OP.mult, OP.max),
              waits=[(s_pe, pev), (s_dsc, s_dsc.n)])
            E("vector", lambda e: e.reciprocal(RW(3)[:, 1:2], RW(3)[:, 0:1]),
              incs=[(s_dve, 1)])
            dvv = s_dve.n
            E("sync", lambda e, c=col: e.dma_start(
                out=bass.AP(scl_d, 2 * c, [[1, 1], [1, 2]]), in_=RW(3)[0:1, 0:2]),
              waits=[(s_dve, dvv), (s_dsc, s_dsc.n)], incs=[(s_dsc, 16)])
            dscv = s_dsc.n
            E("sync", lambda e, c=col: e.dma_start(
                out=mwc[:, 2 * c:2 * c + 2],
                in_=bass.AP(scl_d, 2 * c, [[0, 128], [1, 2]])),
              waits=[(s_dsc, dscv)], incs=[(s_dsc, 16)])

        def emit_quant(am1_slots):
            sw1_col = mwc[:, 1:2]
            sw2_col = mwc[:, 3:4]
            mwrdy = s_dsc.n
            yabf = ya[:].bitcast(BF16)
            qsems = (s_dw, s_dw1)
            actm = [0, 0]
            dvem = [0, 0]
            inm = list(am1_slots)
            dwqm = s_dwq.n
            for idx in range(32):
                is_w1 = idx < 16
                hb = idx % 16
                blk, h2 = hb // 2, hb % 2
                sl = idx % 2
                sin = stg[:, sl * 2048:(sl + 1) * 2048]
                ssc = stg2[:, sl * 2048:(sl + 1) * 2048]
                E("sync", lambda e, a=w_half_ap(is_w1, hb), b_=sin:
                  e.dma_start(out=b_, in_=a),
                  waits=[(s_act, actm[sl]), (s_dve, inm[sl])],
                  incs=[(qsems[sl], 16)])
                dw = qsems[sl].n
                E("scalar", lambda e, i_=sin, o_=ssc,
                  sc=(sw1_col if is_w1 else sw2_col): e.activation(
                    o_, i_, AF.Identity, bias=MAGIC, scale=sc),
                  waits=[(qsems[sl], dw), (s_dve, dvem[sl]), (s_dsc, mwrdy)],
                  incs=[(s_act, 1)])
                actm[sl] = s_act.n
                E("vector", lambda e, o_=ssc: e.tensor_scalar(
                    o_, o_, MAGIC, 1.0, OP.subtract, OP.min),
                  waits=[(s_act, actm[sl])])
                if is_w1:
                    out = bass.AP(yabf.tensor, yabf.offset + h2 * 512,
                                  [[4096, 128], [128, 4], [1024, 4], [1, 128]])
                    E("vector", lambda e, o_=out, i_=ssc:
                      e.tensor_single_scalar(o_, i_, -1.0, OP.max),
                      waits=[(s_dwq, dwqm)], incs=[(s_dve, 1)])
                else:
                    o2 = wq2[:, blk * 4096 + h2 * 2048:
                             blk * 4096 + h2 * 2048 + 2048]
                    E("vector", lambda e, o_=o2, i_=ssc:
                      e.tensor_single_scalar(o_, i_, -1.0, OP.max),
                      incs=[(s_dve, 1)])
                dvem[sl] = s_dve.n
                if is_w1 and h2 == 1:
                    E("sync", lambda e, b=blk, dv=dvem[sl]: e.dma_start(
                        out=bass.AP(wq1_d, b * 4 * 131072,
                                    [[1024, 128], [131072, 4], [1, 1024]]),
                        in_=yabf.rearrange("p (hb f) -> p hb f", hb=4)),
                      waits=[(s_dve, dvem[sl])], incs=[(s_dwq, 16)])
                    dwqm = s_dwq.n

        am1_done, am2_done = w_absmean_dual()
        w_scale_finish(0, 0)
        w_scale_finish(1, 2)
        mw_ready = s_dsc.n
        wq_done_dve = 0
        wq1_written = 0

        NUP = 64
        unit_dve = [0] * NUP
        unit_pool = [0] * NUP

        def stp_slice(stp, i_):
            return bass.AP(stp.tensor, stp.offset + i_, [list(stp.ap[0]), [1, 2]])

        atp = a_t[:]
        btp = b_t[:]
        htp = h_t[:]
        wtp = wt[:]
        bBp = bcB[:]
        bCp = bcC[:]
        stp = stt_sb[:]
        sqa = sq[:]
        rep8_0 = bass.AP(sqa.tensor, sqa.offset, [list(sqa.ap[0]), [0, 8], [1, LC]])
        rep8_1 = bass.AP(sqa.tensor, sqa.offset + LC,
                         [list(sqa.ap[0]), [0, 8], [1, LC]])
        xnT3d = xnT[:].rearrange("p (r f) -> p r f", r=8)
        uT3d = uT[:].rearrange("p (r f) -> p r f", r=8)
        scsa = scs[:]
        screp8 = bass.AP(scsa.tensor, scsa.offset, [list(scsa.ap[0]), [0, 8], [1, LC]])
        screp4 = bass.AP(scsa.tensor, scsa.offset, [list(scsa.ap[0]), [0, 4], [1, LC]])
        hq = g[:].bitcast(BF16)
        ZB = 4096

        roll = {
            "outdma": 0, "bcw": (s_dbw, 0), "wln_dve": wq_done_dve,
            "mm1_pe": 0,
            "sq_pe": 0, "c2s_act": 0, "evxd_act": 0, "pre2": [0, 0],
            "psd2": [0, 0], "ln2m": [0, 0], "round_all": 0, "outtr_pe": 0,
            "uevac_act": 0, "tailevac_dve": 0, "mm1war2": [0, 0],
            "prep": None, "xq_dve": 0, "stage2_done": 0,
            "z_dve": 0, "xn_dve": am2_done, "sp2run_pool": 0,
        }

        def scale_rows(src_dve_mark, mw_off, sp_ln_all):
            dvv_ = src_dve_mark
            for k in range(TB):
                E("tensor", lambda e, kk=k: e.transpose(
                    ps_tr[:, kk * 128:(kk + 1) * 128],
                    mrun[:, kk * 128:(kk + 1) * 128], eye[:]),
                  waits=[(s_dve, dvv_),
                         (s_act, max(sp_ln_all, roll["uevac_act"]))])
                E("tensor", lambda e, kk=k: e.transpose(
                    ps_tr[:, 256 + kk * 128:256 + (kk + 1) * 128],
                    mnrun[:, kk * 128:(kk + 1) * 128], eye[:]),
                  incs=[(s_pe, 1)] if k == TB - 1 else ())
            for k in range(TB):
                E("vector", lambda e, kk=k: e.tensor_reduce(
                    acc[:, 1:2], ps_tr[:, kk * 128:(kk + 1) * 128], AX.X, OP.max),
                  waits=[(s_pe, s_pe.n)])
                E("vector", lambda e, kk=k: e.tensor_reduce(
                    acc[:, 0:1], ps_tr[:, 256 + kk * 128:256 + (kk + 1) * 128],
                    AX.X, OP.max))
                E("vector", lambda e: e.tensor_tensor(
                    acc[:, 1:2], acc[:, 1:2], acc[:, 0:1], OP.max))
                E("vector", lambda e: e.tensor_scalar_max(acc[:, 1:2], acc[:, 1:2],
                                                          EPS))
                E("vector", lambda e: e.reciprocal(acc[:, 0:1], acc[:, 1:2]))
                E("vector", lambda e: e.tensor_scalar_mul(acc[:, 2:3], acc[:, 0:1],
                                                          127.0))
                E("vector", lambda e, mo=mw_off: e.tensor_scalar(
                    acc[:, 3:4], acc[:, 1:2], mwc[:, mo:mo + 1], 1.0 / 127.0,
                    OP.mult, OP.mult),
                  waits=[(s_dsc, mw_ready)], incs=[(s_dve, 1)])
                dv2 = s_dve.n
                E("tensor", lambda e, kk=k: e.transpose(
                    ps_tr[0:1, kk * 128:(kk + 1) * 128], acc[:, 2:3], eye[:]),
                  waits=[(s_dve, dv2)])
                E("tensor", lambda e, kk=k: e.transpose(
                    ps_tr[0:1, 256 + kk * 128:256 + (kk + 1) * 128],
                    acc[:, 3:4], eye[:]),
                  incs=[(s_pe, 1)])
            pe_mark = s_pe.n
            E("vector", lambda e: e.tensor_copy(RW(3)[:], ps_tr[0:1, 0:LC]),
              waits=[(s_pe, pe_mark)])
            E("vector", lambda e: e.tensor_copy(RW(4)[:], ps_tr[0:1, 256:512]),
              incs=[(s_dve, 1)])
            return s_dve.n, pe_mark

        def prepA_steps(c):
            A = {"uevac_act": 0, "stats1_pe": 0, "pev": 0,
                 "sqm": [roll["sq_pe"], roll["sq_pe"]]}

            def mk_k(k):
                def fn():
                    E("sync", lambda e, kk=k, cc=c: e.dma_start(
                        out=utok[:],
                        in_=bass.AP(u_d, (cc * LC + kk * 128) * D,
                                    [[D, 128], [1, D]])),
                      waits=[(s_pe, A["pev"]),
                             (s_dve, roll["tailevac_dve"])],
                      incs=[(s_du, 16)])
                    duv = s_du.n
                    for half in range(2):
                        for j in range(4):
                            dt_ = half * 4 + j
                            E("tensor", lambda e, d_=dt_, jj=j: e.transpose(
                                ps_tr[:, jj * 128:(jj + 1) * 128],
                                utok[:, d_ * 128:(d_ + 1) * 128], eye[:]),
                              waits=[(s_du, duv), (s_ld, ld_all),
                                     (s_act, s_act.n), (s_dve, s_dve.n)],
                              incs=[(s_pe, 1)] if j == 3 else ())
                        pev = s_pe.n
                        E("scalar", lambda e, kk=k, hh=half: e.activation(
                            uT3[:, hh * 4:(hh + 1) * 4, kk * 128:(kk + 1) * 128],
                            tr3[:, :, :], AF.Identity),
                          waits=[(s_pe, pev), (s_dve, roll["xn_dve"])],
                          incs=[(s_act, 1)])
                    A["pev"] = s_pe.n
                    A["uevac_act"] = s_act.n
                    roll["uevac_act"] = s_act.n
                return fn

            def mk_stats(dlo, dhi):
                def fn():
                    sqm = A["sqm"]
                    for dt_ in range(dlo, dhi):
                        E("scalar", lambda e, d_=dt_: e.activation(
                            sq[:, (d_ % 2) * LC:(d_ % 2 + 1) * LC],
                            uT3[:, d_:d_ + 1, :], AF.Square),
                          waits=[(s_act, A["uevac_act"]), (s_pe, sqm[dt_ % 2]),
                                 (s_dve, roll["z_dve"])],
                          incs=[(s_act, 1)])
                        acv = s_act.n
                        E("tensor", lambda e, d_=dt_: e.matmul(
                            ps_b2[0:1, 0:LC], onc[:], uT3[:, d_:d_ + 1, :],
                            start=(d_ == 0), stop=(d_ == DT - 1)),
                          waits=[(s_act, A["uevac_act"]),
                                 (s_dve, roll["wln_dve"])])
                        E("tensor", lambda e, d_=dt_: e.matmul(
                            ps_b2[32:33, 0:LC], onc[:],
                            sq[:, (d_ % 2) * LC:(d_ % 2 + 1) * LC],
                            start=(d_ == 0), stop=(d_ == DT - 1)),
                          waits=[(s_act, acv)], incs=[(s_pe, 1)])
                        sqm[dt_ % 2] = s_pe.n
                    A["stats1_pe"] = s_pe.n
                return fn

            return A, [mk_k(0), mk_k(1), mk_stats(0, 4), mk_stats(4, DT)]

        def emit_prep(c, A, tailcb=None):
            cmod = c % 2
            M = {}
            cb = tailcb if tailcb is not None else (lambda: None)
            ut_evac_act = A["uevac_act"]
            M["uevac_act"] = ut_evac_act
            stats1_pe = A["stats1_pe"]
            cb()
            E("vector", lambda e: e.tensor_scalar_mul(RW(0)[:], ps_b2[0:1, 0:LC],
                                                      INV_D),
              waits=[(s_pe, stats1_pe)], incs=[(s_dve, 1)])
            rw0_dve = s_dve.n
            E("scalar", lambda e: e.activation(RW(3)[:], RW(0)[:], AF.Square),
              waits=[(s_pe, stats1_pe), (s_dve, rw0_dve if c == 0 else s_dve.n),
                     (s_dsc, mw_ready if c == 0 else 0)],
              incs=[(s_act, 1)])
            acv = s_act.n
            E("vector", lambda e: e.scalar_tensor_tensor(
                RW(3)[:], ps_b2[32:33, 0:LC], INV_D, RW(3)[:], OP.mult, OP.subtract),
              waits=[(s_act, acv)], incs=[(s_dve, 1)])
            dvv = s_dve.n
            E("scalar", lambda e: e.activation(RW(3)[:], RW(3)[:], AF.Ln, bias=EPS),
              waits=[(s_dve, dvv)])
            E("scalar", lambda e: e.activation(RW(1)[:], RW(3)[:], AF.Exp, scale=-0.5),
              incs=[(s_act, 1)])
            acv = s_act.n
            E("vector", lambda e: e.scalar_tensor_tensor(
                RW(2)[:], RW(0)[:], -1.0, RW(1)[:], OP.mult, OP.mult),
              waits=[(s_act, acv)], incs=[(s_dve, 1)])
            rows1_dve = s_dve.n
            E("tensor", lambda e: e.matmul(ps_b1[:, 0:LC], onr[:], RW(1)[:],
                                           start=True, stop=True),
              waits=[(s_dve, rows1_dve)])
            E("tensor", lambda e: e.matmul(ps_b1[:, 256:512], onr[:], RW(2)[:],
                                           start=True, stop=True),
              incs=[(s_pe, 1)])
            bc1_pe = s_pe.n
            cb()
            E("scalar", lambda e: e.activation(sq[:, 0:LC], ps_b1[:, 0:LC],
                                               AF.Identity),
              waits=[(s_pe, bc1_pe), (s_dve, 0 if c == 0 else s_dve.n)])
            E("scalar", lambda e: e.activation(sq[:, LC:2 * LC], ps_b1[:, 256:512],
                                               AF.Identity),
              incs=[(s_act, 1)])
            acv = s_act.n
            E("vector", lambda e: e.tensor_tensor(xnT3d, uT3d, rep8_0, OP.mult),
              waits=[(s_act, acv), (s_act, ut_evac_act)])
            E("vector", lambda e: e.tensor_tensor(xnT3d, xnT3d, rep8_1, OP.add),
              incs=[(s_dve, 1)])
            xn_dve = s_dve.n
            roll["xn_dve"] = xn_dve
            cb()
            for dt_ in range(DT):
                E("tensor", lambda e, d_=dt_: e.matmul(
                    ps_xd[0:96, 0:LC], wxT[:, d_ * 96:(d_ + 1) * 96],
                    xnT[:, d_ * LC:(d_ + 1) * LC],
                    start=(d_ == 0), stop=(d_ == DT - 1)),
                  waits=[(s_dve, xn_dve), (s_act, roll["evxd_act"])],
                  incs=[(s_pe, 1)] if dt_ == DT - 1 else ())
            xd_pe = s_pe.n
            E("scalar", lambda e: e.activation(dsb[0:64, :], ps_xd[0:64, 0:LC],
                                               AF.Identity),
              waits=[(s_pe, xd_pe), (s_dve, 0 if c == 0 else s_dve.n)],
              incs=[(s_act, 1)])
            E("scalar", lambda e: e.activation(bcsb[0:32, :], ps_xd[64:96, 0:LC],
                                               AF.Identity),
              waits=[roll["bcw"]], incs=[(s_act, 1)])
            evac_xd_act = s_act.n
            roll["evxd_act"] = evac_xd_act
            bwsem = s_am0 if c == 0 else s_dbw
            E("gpsimd" if c == 0 else "sync", lambda e, cm=cmod: e.dma_start(
                out=bass.AP(bc_d, cm * 32 * LC, [[LC, 32], [1, LC]]), in_=bcsb[:]),
              waits=[(s_act, evac_xd_act), (s_dbc, s_dbc.n), (s_dbc1, s_dbc1.n)],
              incs=[(bwsem, 16)], self_sync=False)
            roll["bcw"] = (bwsem, bwsem.n)
            M["bcw_now"] = (bwsem, bwsem.n)
            cb()
            psd2 = list(roll["psd2"])
            ln2m = list(roll["ln2m"])
            for dt_ in range(DT):
                dps = ps_d[:, 0:LC] if dt_ % 2 == 0 else ps_xd[:, 256:256 + LC]
                so = (dt_ % 2) * LC
                E("tensor", lambda e, d_=dt_, pp=dps: e.matmul(
                    pp, wdt[:, d_ * 128:(d_ + 1) * 128],
                    dsb[0:65, :], start=True, stop=True),
                  waits=[(s_act, evac_xd_act), (s_dve, psd2[dt_ % 2])],
                  incs=[(s_pe, 1)])
                pev = s_pe.n
                E("vector", lambda e, pp=dps, s_=so: e.tensor_single_scalar(
                    sp1[:, s_:s_ + LC].bitcast(I32), pp.bitcast(I32),
                    sgn[:, 0:1], OP.bitwise_or),
                  waits=[(s_pe, pev), (s_act, ln2m[dt_ % 2])],
                  incs=[(s_dve, 1)])
                dvv = s_dve.n
                E("scalar", lambda e, s_=so: e.activation(
                    sp2[:, s_:s_ + LC], sp1[:, s_:s_ + LC], AF.Exp),
                  waits=[(s_dve, dvv)])
                E("scalar", lambda e, s_=so: e.activation(
                    sp1[:, s_:s_ + LC], sp2[:, s_:s_ + LC], AF.Ln, bias=1.0),
                  incs=[(s_act, 1)])
                ln2m[dt_ % 2] = s_act.n
                acv = s_act.n
                E("vector", lambda e, d_=dt_, pp=dps, s_=so: e.scalar_tensor_tensor(
                    dl[:, d_ * LC:(d_ + 1) * LC], pp, 0.0,
                    sp1[:, s_:s_ + LC], OP.max, OP.add),
                  waits=[(s_act, acv)], incs=[(s_dve, 1)])
                psd2[dt_ % 2] = s_dve.n
                if dt_ % 2 == 1:
                    cb()
            roll["psd2"] = psd2
            roll["ln2m"] = ln2m
            M["sp_ln_all"] = max(ln2m)
            M["psd2"] = list(psd2)
            E("vector", lambda e: e.tensor_tensor(wt[:], dl[:], xnT[:], OP.mult),
              incs=[(s_dve, 1)])
            M["dl_dve"] = s_dve.n
            M["wt_dve"] = s_dve.n
            sqm2 = [stats1_pe, stats1_pe]
            for dt_ in range(DT):
                E("scalar", lambda e, d_=dt_: e.activation(
                    sq[:, (d_ % 2) * LC:(d_ % 2 + 1) * LC],
                    xnT[:, d_ * LC:(d_ + 1) * LC], AF.Square),
                  waits=[(s_pe, sqm2[dt_ % 2]), (s_dve, xn_dve)],
                  incs=[(s_act, 1)])
                acv = s_act.n
                E("tensor", lambda e, d_=dt_: e.matmul(
                    ps_b2[64:65, 0:LC], onc[:], xnT[:, d_ * LC:(d_ + 1) * LC],
                    start=(d_ == 0), stop=(d_ == DT - 1)),
                  waits=[(s_dve, xn_dve)])
                E("tensor", lambda e, d_=dt_: e.matmul(
                    ps_b2[0:1, 256:512], onc[:],
                    sq[:, (d_ % 2) * LC:(d_ % 2 + 1) * LC],
                    start=(d_ == 0), stop=(d_ == DT - 1)),
                  waits=[(s_act, acv)], incs=[(s_pe, 1)])
                sqm2[dt_ % 2] = s_pe.n
            stats2_pe = s_pe.n
            roll["sq_pe"] = stats2_pe
            E("vector", lambda e: e.tensor_scalar_mul(RW(0)[:], ps_b2[64:65, 0:LC],
                                                      INV_D),
              waits=[(s_pe, stats2_pe)])
            E("scalar", lambda e: e.activation(RW(3)[:], RW(0)[:], AF.Square),
              waits=[(s_pe, stats2_pe), (s_dve, s_dve.n)], incs=[(s_act, 1)])
            acv = s_act.n
            E("vector", lambda e: e.scalar_tensor_tensor(
                RW(3)[:], ps_b2[0:1, 256:512], INV_D, RW(3)[:], OP.mult, OP.subtract),
              waits=[(s_act, acv)], incs=[(s_dve, 1)])
            dvv = s_dve.n
            E("scalar", lambda e: e.activation(RW(3)[:], RW(3)[:], AF.Ln, bias=EPS),
              waits=[(s_dve, dvv)])
            E("scalar", lambda e: e.activation(RW(1)[:], RW(3)[:], AF.Exp, scale=-0.5),
              waits=[(s_pe, bc1_pe)], incs=[(s_act, 1)])
            acv = s_act.n
            E("vector", lambda e: e.tensor_tensor(RW(2)[:], RW(0)[:], RW(1)[:],
                                                  OP.mult),
              waits=[(s_act, acv), (s_pe, bc1_pe)], incs=[(s_dve, 1)])
            rows2_dve = s_dve.n
            E("tensor", lambda e: e.matmul(ps_b2[:, 0:LC], onr[:], RW(1)[:],
                                           start=True, stop=True),
              waits=[(s_dve, rows2_dve)])
            E("tensor", lambda e: e.matmul(ps_b2[:, 256:512], onr[:], RW(2)[:],
                                           start=True, stop=True),
              incs=[(s_pe, 1)])
            bc2_pe = s_pe.n
            E("scalar", lambda e: e.activation(sq[:, 0:LC], ps_b2[:, 0:LC],
                                               AF.Identity),
              waits=[(s_pe, bc2_pe), (s_dve, s_dve.n)])
            E("scalar", lambda e: e.activation(sq[:, LC:2 * LC], ps_b2[:, 256:512],
                                               AF.Identity),
              incs=[(s_act, 1)])
            acv = s_act.n
            zb = g[:, ZB:ZB + 2048].rearrange("p (r f) -> p r f", r=8)
            E("vector", lambda e: e.tensor_tensor(zb, xnT3d, rep8_0, OP.mult),
              waits=[(s_act, acv), (s_dve, wq_done_dve if c == 0 else 0),
                     (s_dwq, wq1_written if c == 0 else 0)])
            E("vector", lambda e: e.tensor_tensor(zb, zb, rep8_1, OP.subtract),
              incs=[(s_dve, 1)])
            z_dve = s_dve.n
            roll["z_dve"] = z_dve
            wlb = ZB + 2048
            for dt_ in range(DT):
                E("scalar", lambda e, d_=dt_: e.activation(
                    g[:, wlb + d_ * LC:wlb + (d_ + 1) * LC],
                    g[:, ZB + d_ * LC:ZB + (d_ + 1) * LC], AF.Identity,
                    bias=bet[:, d_:d_ + 1], scale=gam[:, d_:d_ + 1]),
                  waits=[(s_dve, z_dve)],
                  incs=[(s_act, 1)])
            wlnact = s_act.n
            E("vector", lambda e: e.tensor_tensor(
                sp1[:], g[:, wlb:wlb + 1024], g[:, wlb + 1024:wlb + 2048], OP.max),
              waits=[(s_act, wlnact)])
            E("vector", lambda e: e.tensor_tensor(
                sp1[:, 0:512], sp1[:, 0:512], sp1[:, 512:1024], OP.max))
            E("vector", lambda e: e.tensor_tensor(
                mrun[:], sp1[:, 0:256], sp1[:, 256:512], OP.max))
            E("vector", lambda e: e.tensor_tensor(
                sp2[:], g[:, wlb:wlb + 1024], g[:, wlb + 1024:wlb + 2048], OP.min))
            E("vector", lambda e: e.tensor_tensor(
                sp2[:, 0:512], sp2[:, 0:512], sp2[:, 512:1024], OP.min))
            E("vector", lambda e: e.tensor_tensor(
                sp2[:, 0:256], sp2[:, 0:256], sp2[:, 256:512], OP.min))
            E("vector", lambda e: e.tensor_scalar_mul(mnrun[:], sp2[:, 0:256], -1.0),
              incs=[(s_dve, 1)])
            wln_dve = s_dve.n
            roll["wln_dve"] = wln_dve
            q1rows_dve, _ = scale_rows(wln_dve, 0, M["sp_ln_all"])
            E("tensor", lambda e: e.matmul(ps_b3[:, 0:LC], onr[:], RW(3)[:],
                                           start=True, stop=True),
              waits=[(s_dve, q1rows_dve), (s_act, roll["c2s_act"])])
            E("tensor", lambda e: e.matmul(ps_b3[:, 256:512], onr[:], RW(4)[:],
                                           start=True, stop=True),
              incs=[(s_pe, 1)])
            bc3_pe = s_pe.n
            E("scalar", lambda e: e.activation(c1s[:], ps_b3[:, 256:512],
                                               AF.Identity),
              waits=[(s_pe, bc3_pe), (s_dve, max(roll["mm1war2"]))],
              incs=[(s_act, 1)])
            E("scalar", lambda e: e.activation(scs[:], ps_b3[:, 0:LC], AF.Identity),
              waits=[(s_dve, roll["round_all"])], incs=[(s_act, 1)])
            c1s_act = s_act.n
            M["c1s_act"] = c1s_act
            E("vector", lambda e: e.tensor_tensor(
                g[:, ZB:ZB + 2048].rearrange("p (r f) -> p r f", r=8),
                g[:, wlb:wlb + 2048].rearrange("p (r f) -> p r f", r=8),
                screp8, OP.mult),
              waits=[(s_act, c1s_act), (s_pe, roll["mm1_pe"])])
            E("vector", lambda e: e.tensor_scalar(
                xqT[:], g[:, ZB:ZB + 2048], MAGIC, MAGIC, OP.add, OP.subtract),
              incs=[(s_dve, 1)])
            M["xq_dve"] = s_dve.n
            M["bc_done"] = {}

            def bc_dma(pr):
                war = 0
                un = ((pr - 2) % 8) * DT + DT - 1
                if c > 0 or pr >= 2:
                    war = unit_dve[un]
                if c == 0 and pr < 2:
                    sem = s_am0 if pr == 0 else s_am1
                else:
                    sem = s_dbc if pr % 2 == 0 else s_dbc1
                for which, buf in ((0, bcB), (1, bcC)):
                    off = cmod * 32 * LC + (which * 16 + pr * 2) * LC
                    E("gpsimd" if (c == 0 and pr < 2) else "sync",
                      lambda e, o_=off, b_=buf, p_=pr: e.dma_start(
                        out=b_[:, (p_ % 2) * 512:(p_ % 2) * 512 + 512],
                        in_=bass.AP(bc_d, o_, [[0, 128], [LC, 2], [1, LC]])),
                      waits=[M["bcw_now"], (s_dve, war)],
                      incs=[(sem, 16)], self_sync=False)
                M["bc_done"][pr] = (sem, sem.n)

            M["bc_dma"] = bc_dma
            bc_dma(0)
            bc_dma(1)
            return M

        def emit_pd(c, M, pr, db):
            pd = pr * DT + db
            sl = pd % 2
            so = sl * 514
            bo = (pr % 2) * 512
            idx = db * N + pr * 2
            aw = unit_dve[(pd - 2) % NUP] if (c > 0 or pd >= 2) else 0
            aw = max(aw, roll.get("azero_dve", 0))
            pw = unit_pool[(pd - 2) % NUP] if (c > 0 or pd >= 2) else 0
            for j in range(2):
                E("scalar", lambda e, d_=db, i_=idx, j_=j, o_=so: e.activation(
                    bass.AP(atp.tensor, atp.offset + o_ + j_ * 257 + 1,
                            [list(atp.ap[0]), [1, 256]]),
                    dl[:, d_ * LC:(d_ + 1) * LC],
                    AF.Exp, scale=A_sb[:, i_ + j_:i_ + j_ + 1]),
                  waits=[(s_dve, max(aw, M["dl_dve"]))],
                  incs=[(s_act, 1)] if j == 1 else ())
            E("scalar", lambda e, o_=so, i_=idx: e.activation(
                bass.AP(btp.tensor, btp.offset + o_, [list(btp.ap[0]), [257, 2]]),
                stp_slice(stp, i_), AF.Identity),
              waits=[(s_pool, pw), (s_pe, roll["outtr_pe"]),
                     (s_dve, aw if (c > 0 or pd >= 2) else M["dl_dve"])],
              incs=[(s_act, 1)])
            a_act = s_act.n
            E("vector", lambda e, d_=db, o_=so, b_=bo: e.tensor_tensor(
                bass.AP(btp.tensor, btp.offset + o_ + 1,
                        [list(btp.ap[0]), [257, 2], [1, 256]]),
                bass.AP(wtp.tensor, wtp.offset + d_ * LC,
                        [list(wtp.ap[0]), [0, 2], [1, 256]]),
                bass.AP(bBp.tensor, bBp.offset + b_,
                        [list(bBp.ap[0]), [256, 2], [1, 256]]),
                OP.mult),
              waits=[M["bc_done"][pr], (s_pool, pw), (s_dve, M["wt_dve"]),
                     (s_pe, roll["outtr_pe"])])
            E("vector", lambda e, o_=so: e.tensor_tensor_scan(
                htp[0:128, 0:514],
                bass.AP(atp.tensor, atp.offset + o_, [list(atp.ap[0]), [1, 514]]),
                bass.AP(btp.tensor, btp.offset + o_, [list(btp.ap[0]), [1, 514]]),
                0.0, OP.mult, OP.add),
              waits=[(s_act, a_act)])
            E("vector", lambda e, i_=idx: e.tensor_copy(
                stp_slice(stp, i_),
                bass.AP(htp.tensor, htp.offset + 256, [list(htp.ap[0]), [257, 2]])))
            E("vector", lambda e, o_=so, b_=bo: e.tensor_tensor(
                bass.AP(btp.tensor, btp.offset + o_ + 1,
                        [list(btp.ap[0]), [257, 2], [1, 256]]),
                bass.AP(htp.tensor, htp.offset + 1,
                        [list(htp.ap[0]), [257, 2], [1, 256]]),
                bass.AP(bCp.tensor, bCp.offset + b_,
                        [list(bCp.ap[0]), [256, 2], [1, 256]]),
                OP.mult),
              incs=[(s_dve, 1)])
            unit_dve[pd] = s_dve.n
            E("gpsimd", lambda e, o_=so: e.tensor_tensor(
                ptree[:],
                bass.AP(btp.tensor, btp.offset + o_ + 1, [list(btp.ap[0]), [1, 256]]),
                bass.AP(btp.tensor, btp.offset + o_ + 258,
                        [list(btp.ap[0]), [1, 256]]),
                OP.add),
              waits=[(s_dve, unit_dve[pd]),
                     (s_act, M["yainit_act"][db]) if pr == 0 else (s_act, 0)])
            E("gpsimd", lambda e, d_=db: e.tensor_tensor(
                ya[:, d_ * LC:(d_ + 1) * LC],
                ya[:, d_ * LC:(d_ + 1) * LC], ptree[:], OP.add),
              incs=[(s_pool, 1)])
            unit_pool[pd] = s_pool.n

        def emit_stage2A(c, M, ht):
            sl = ht % 2
            wsem = s_dw if ht % 2 == 0 else s_dw1
            E("sync", lambda e, h_=ht, s_=sl: e.dma_start(
                out=wq1s[:, s_ * 1024:(s_ + 1) * 1024],
                in_=bass.AP(wq1_d, h_ * 131072, [[1024, 128], [1, 1024]])),
              waits=[(s_dwq, wq1_written),
                     (s_dve, wq_done_dve if (c == 0 and ht < 2) else 0),
                     (s_pe, M["mm1_marks"][ht - 2] if ht >= 2 else roll["mm1_pe"])],
              incs=[(wsem, 16)])
            dwv = (wsem, wsem.n)
            m1ps = (ps_m1 if ht % 2 == 0 else ps_m2)[:, 0:LC]
            for dt_ in range(DT):
                E("tensor", lambda e, d_=dt_, s_=sl, pp=m1ps: e.matmul(
                    pp,
                    wq1s[:, s_ * 1024 + d_ * 128:s_ * 1024 + (d_ + 1) * 128],
                    xqT[:, d_ * LC:(d_ + 1) * LC],
                    start=(d_ == 0), stop=(d_ == DT - 1)),
                  waits=[dwv, (s_dve, M["xq_dve"]),
                         (s_dve, M["pre_marks"][ht - 2] if ht >= 2
                          else roll["mm1war2"][sl])],
                  incs=[(s_pe, 1)] if dt_ == DT - 1 else ())
            M["mm1_marks"][ht] = s_pe.n
            E("vector", lambda e, s_=sl, pp=m1ps: e.tensor_tensor(
                pre[:, s_ * LC:(s_ + 1) * LC], pp, c1s[:], OP.mult),
              waits=[(s_pe, M["mm1_marks"][ht]), (s_act, M["c1s_act"]),
                     (s_act, M["gelu_marks"][ht - 2] if ht >= 2 else 0)],
              incs=[(s_dve, 1)])
            M["pre_marks"][ht] = s_dve.n

        def emit_stage2B(c, M, ht):
            sl = ht % 2
            E("scalar", lambda e, h_=ht, s_=sl: e.activation(
                g[:, h_ * LC:(h_ + 1) * LC], pre[:, s_ * LC:(s_ + 1) * LC],
                AF.Gelu, bias=b1c[:, h_:h_ + 1]),
              waits=[(s_dve, M["pre_marks"][ht])],
              incs=[(s_act, 1)])
            M["gelu_marks"][ht] = s_act.n
            if ht % 4 == 3:
                qb = (ht - 3) * LC
                if ht == 3:
                    E("vector", lambda e, q_=qb: e.tensor_copy(
                        sp1[:], g[:, q_:q_ + 1024]),
                      waits=[(s_act, M["gelu_marks"][ht]),
                             (s_pe, roll["outtr_pe"])],
                      incs=[(s_dve, 1)])
                    E("vector", lambda e, q_=qb: e.tensor_scalar_mul(
                        sp2[:], g[:, q_:q_ + 1024], -1.0),
                      incs=[(s_dve, 1)])
                else:
                    E("vector", lambda e, q_=qb: e.tensor_tensor(
                        sp1[:], sp1[:], g[:, q_:q_ + 1024], OP.max),
                      waits=[(s_act, M["gelu_marks"][ht])], incs=[(s_dve, 1)])
                    E("vector", lambda e, q_=qb: e.scalar_tensor_tensor(
                        sp2[:], g[:, q_:q_ + 1024], -1.0, sp2[:],
                        OP.mult, OP.max),
                      incs=[(s_dve, 1)])

        def emit_q2_round(c, M):
            roll["mm1_pe"] = M["mm1_marks"][HT - 1]
            roll["mm1war2"] = [M["pre_marks"][HT - 2], M["pre_marks"][HT - 1]]
            E("vector", lambda e: e.tensor_tensor(
                sp1[:, 0:512], sp1[:, 0:512], sp1[:, 512:1024], OP.max))
            E("vector", lambda e: e.tensor_tensor(
                mrun[:], sp1[:, 0:256], sp1[:, 256:512], OP.max))
            E("vector", lambda e: e.tensor_tensor(
                sp2[:, 0:512], sp2[:, 0:512], sp2[:, 512:1024], OP.max))
            E("vector", lambda e: e.tensor_tensor(
                mnrun[:], sp2[:, 0:256], sp2[:, 256:512], OP.max),
              incs=[(s_dve, 1)])
            q2max_dve = s_dve.n
            q2rows_dve, _ = scale_rows(q2max_dve, 2, M["sp_ln_all"])
            E("tensor", lambda e: e.matmul(ps_b3[:, 0:LC], onr[:], RW(3)[:],
                                           start=True, stop=True),
              waits=[(s_dve, q2rows_dve), (s_act, M["c1s_act"])])
            E("tensor", lambda e: e.matmul(ps_b3[:, 256:512], onr[:], RW(4)[:],
                                           start=True, stop=True),
              incs=[(s_pe, 1)])
            bc32_pe = s_pe.n
            E("scalar", lambda e: e.activation(c2s[:], ps_b3[:, 256:512],
                                               AF.Identity),
              waits=[(s_pe, bc32_pe), (s_dve, s_dve.n)], incs=[(s_act, 1)])
            E("scalar", lambda e: e.activation(scs[:], ps_b3[:, 0:LC], AF.Identity),
              waits=[(s_dve, M["xq_dve"])], incs=[(s_act, 1)])
            c2s_act = s_act.n
            roll["c2s_act"] = c2s_act
            M["c2s_act"] = c2s_act
            round_marks = [0] * HT
            for q_ in range(HT // 4):
                E("vector", lambda e, qq=q_: e.tensor_tensor(
                    sp1[:].rearrange("p (r f) -> p r f", r=4),
                    g[:, qq * 1024:(qq + 1) * 1024].rearrange(
                        "p (r f) -> p r f", r=4),
                    screp4, OP.mult),
                  waits=[(s_act, c2s_act), (s_pe, roll["outtr_pe"])])
                E("vector", lambda e, qq=q_: e.tensor_scalar(
                    hq[:, qq * 1024:(qq + 1) * 1024], sp1[:],
                    MAGIC, MAGIC, OP.add, OP.subtract),
                  incs=[(s_dve, 1)])
                for j in range(4):
                    round_marks[q_ * 4 + j] = s_dve.n
            M["round_marks"] = round_marks
            roll["round_all"] = s_dve.n

        def _mm2_group(c, M, dt_, war):
            m2ps = (ps_m1 if dt_ % 2 == 0 else ps_m2)[:, 256:256 + LC]
            for ht in range(HT):
                E("tensor", lambda e, d_=dt_, h_=ht, pp=m2ps: e.matmul(
                    pp,
                    wq2[:, h_ * 1024 + d_ * 128:h_ * 1024 + (d_ + 1) * 128],
                    hq[:, h_ * LC:(h_ + 1) * LC],
                    start=(h_ == 0), stop=(h_ == HT - 1)),
                  waits=[(s_dve, M["round_marks"][ht]), (s_dve, war[dt_ % 2])],
                  incs=[(s_pe, 1)] if ht == HT - 1 else ())
            M["mm2_pe"][dt_] = s_pe.n

        def emit_mm2(c, M):
            M["mm2_pe"] = [0] * DT
            M["m2war"] = list(roll.get("m2war2", [0, 0]))
            _mm2_group(c, M, 0, M["m2war"])
            _mm2_group(c, M, 1, M["m2war"])

        def make_tailcb(c, M):
            state = {"dt": 0}
            M["e3_marks"] = [0] * DT
            war = M["m2war"]

            def cb():
                dt_ = state["dt"]
                if dt_ >= DT:
                    return
                state["dt"] += 1
                m2ps = (ps_m1 if dt_ % 2 == 0 else ps_m2)[:, 256:256 + LC]
                eo = (atp if dt_ < 4 else btp)
                E("vector", lambda e, pp=m2ps: e.tensor_tensor(
                    f1[:], pp, c2s[:], OP.mult),
                  waits=[(s_pe, M["mm2_pe"][dt_]), (s_act, M["c2s_act"])],
                  incs=[(s_dve, 1)])
                war[dt_ % 2] = s_dve.n
                E("vector", lambda e, d_=dt_, b_=eo: e.tensor_tensor(
                    bass.AP(b_.tensor, b_.offset + (d_ % 4) * LC,
                            [list(b_.ap[0]), [1, LC]]),
                    f1[:], ya[:, d_ * LC:(d_ + 1) * LC], OP.add),
                  waits=[(s_pool, M["mamba_pool"])],
                  incs=[(s_dve, 1)])
                M["e3_marks"][dt_] = s_dve.n
                if dt_ + 2 < DT:
                    _mm2_group(c, M, dt_ + 2, war)
                if dt_ == DT - 1:
                    roll["m2war2"] = war

            return cb

        def emit_out(c, M):
            for k in range(TB):
                E("sync", lambda e, kk=k, c2=c: e.dma_start(
                    out=utok[:],
                    in_=bass.AP(u_d, (c2 * LC + kk * 128) * D, [[D, 128], [1, D]])),
                  waits=[(s_pe, s_pe.n), (s_dve, roll["tailevac_dve"])],
                  incs=[(s_du, 16)])
                duv = s_du.n
                for half in range(2):
                    for j in range(4):
                        dt_ = half * 4 + j
                        src = (atp if dt_ < 4 else btp)
                        E("tensor", lambda e, d_=dt_, jj=j, kk=k, ss=src:
                          e.transpose(
                            ps_tr[:, jj * 128:(jj + 1) * 128],
                            bass.AP(ss.tensor,
                                    ss.offset + (d_ % 4) * LC + kk * 128,
                                    [list(ss.ap[0]), [1, 128]]),
                            eye[:]),
                          waits=[(s_dve, M["e3_marks"][dt_]), (s_dve, s_dve.n),
                                 (s_act, s_act.n)],
                          incs=[(s_pe, 1)] if j == 3 else ())
                    pev = s_pe.n
                    E("vector", lambda e, hh=half: e.tensor_tensor(
                        outst[:, hh * 512:(hh + 1) * 512], ps_tr[:, 0:512],
                        utok[:, hh * 512:(hh + 1) * 512], OP.add),
                      waits=[(s_pe, pev), (s_du, duv), (s_do, roll["outdma"])],
                      incs=[(s_dve, 1)])
                roll["tailevac_dve"] = s_dve.n
                roll["outtr_pe"] = s_pe.n
                dvv = s_dve.n
                E("sync", lambda e, kk=k, c2=c: e.dma_start(
                    out=bass.AP(o_d, (c2 * LC + kk * 128) * D, [[D, 128], [1, D]]),
                    in_=outst[:]),
                  waits=[(s_dve, dvv)], incs=[(s_do, 16)])
                roll["outdma"] = s_do.n
            E("vector", lambda e: e.memset(
                bass.AP(atp.tensor, atp.offset, [list(atp.ap[0]), [257, 4]]), 0.0),
              waits=[(s_pe, roll["outtr_pe"])], incs=[(s_dve, 1)])
            roll["azero_dve"] = s_dve.n

        def emit_yainit(c, M):
            M["yainit_act"] = [0] * DT
            for dt_ in range(DT):
                E("scalar", lambda e, d_=dt_: e.activation(
                    ya[:, d_ * LC:(d_ + 1) * LC], xnT[:, d_ * LC:(d_ + 1) * LC],
                    AF.Identity, bias=b2c[:, d_:d_ + 1], scale=Dg[:, d_:d_ + 1]),
                  waits=[(s_dve, max(M["ya_free_dve"][dt_], M["xq_dve"]))],
                  incs=[(s_act, 1)])
                M["yainit_act"][dt_] = s_act.n

        def emit_main_rest(c, M, start_pr=0, delay=0, lag=1, hoist=None):
            M["mm1_marks"] = [0] * HT
            M["pre_marks"] = [0] * HT
            M["gelu_marks"] = [0] * HT
            hsteps = dict(zip((16, 24, 32, 40), hoist)) if hoist else {}
            npds = (8 - start_pr) * DT
            doneA = 0
            doneB = 0
            i = 0
            for pr in range(start_pr, 8):
                for db in range(DT):
                    emit_pd(c, M, pr, db)
                    i += 1
                    if i in hsteps:
                        hsteps[i]()
                    if i > delay:
                        want = min(HT, ((i - delay) * HT) // (npds - delay - 4) + 1)
                        while doneA < want:
                            while doneB < doneA - 1:
                                emit_stage2B(c, M, doneB)
                                doneB += 1
                            emit_stage2A(c, M, doneA)
                            doneA += 1
                        while doneB < max(0, doneA - lag):
                            emit_stage2B(c, M, doneB)
                            doneB += 1
                if pr + 2 < 8:
                    M["bc_dma"](pr + 2)
            while doneA < HT:
                emit_stage2A(c, M, doneA)
                doneA += 1
            while doneB < HT:
                emit_stage2B(c, M, doneB)
                doneB += 1
            M["mamba_pool"] = s_pool.n

        A, asteps = prepA_steps(0)
        for fn in asteps:
            fn()
        emit_quant(am1_done)
        wq_done_dve = s_dve.n
        wq1_written = s_dwq.n
        M = emit_prep(0, A)
        M["ya_free_dve"] = [0] * DT
        emit_yainit(0, M)
        An, asteps = prepA_steps(1)
        emit_main_rest(0, M, 0, hoist=asteps)
        for c in range(NCH):
            emit_q2_round(c, M)
            emit_mm2(c, M)
            cb = make_tailcb(c, M)
            if c + 1 < NCH:
                Mn = emit_prep(c + 1, An, tailcb=cb)
                while M["e3_marks"][DT - 1] == 0:
                    cb()
                emit_out(c, M)
                Mn["ya_free_dve"] = list(M["e3_marks"])
                emit_yainit(c + 1, Mn)
                if c + 2 < NCH:
                    An, asteps = prepA_steps(c + 2)
                else:
                    asteps = None
                emit_main_rest(c + 1, Mn, 0, hoist=asteps)
                M = Mn
            else:
                for _ in range(DT):
                    cb()
                emit_out(c, M)

        E("gpsimd", lambda e: e.engine_nop(), waits=[(s_do, s_do.n)])

        import os
        if os.environ.get("BASS_PRINT_SBUF"):
            print("sbuf remaining/partition:", nc.sbuf_bytes_remaining,
                  "psum banks remaining:", nc.psum_banks_remaining)
        with nc.Block() as block:
            em.flush(block, max_seq=max_seq)

    return nc


def kernel(L0_attn_u, L0_attn_W_x, L0_attn_W_dt, L0_attn_b_dt, L0_attn_log_A,
           L0_attn_D_param, L0_ffn_w1, L0_ffn_b1, L0_ffn_w2, L0_ffn_b2,
           L0_ffn_gamma, L0_ffn_beta):
    u = np.asarray(L0_attn_u, dtype=np.float32)
    wx = np.asarray(L0_attn_W_x, dtype=np.float32)
    wdt = np.asarray(L0_attn_W_dt, dtype=np.float32)
    bdt = np.asarray(L0_attn_b_dt, dtype=np.float32)
    logA = np.asarray(L0_attn_log_A, dtype=np.float32)
    Dp = np.asarray(L0_attn_D_param, dtype=np.float32)
    w1 = np.asarray(L0_ffn_w1, dtype=np.float32)
    b1 = np.asarray(L0_ffn_b1, dtype=np.float32)
    w2 = np.asarray(L0_ffn_w2, dtype=np.float32)
    b2 = np.asarray(L0_ffn_b2, dtype=np.float32)
    gam = np.asarray(L0_ffn_gamma, dtype=np.float32)
    bet = np.asarray(L0_ffn_beta, dtype=np.float32)

    wxT = np.ascontiguousarray(wx.T)
    wdtTb = np.ascontiguousarray(np.concatenate([wdt.T, bdt[None, :]], axis=0))
    logA_r = np.ascontiguousarray(
        logA.reshape(DT, 128, N).transpose(1, 0, 2).reshape(128, DT * N))
    r8 = lambda v: np.ascontiguousarray(v.reshape(DT, 128).T)
    base = dict(
        wxT=wxT, wdtTb=wdtTb, logA_r=logA_r,
        D_r=r8(Dp), gamma_r=r8(gam), beta_r=r8(bet), b2_r=r8(b2),
        b1_r=np.ascontiguousarray(b1.reshape(HT, 128).T),
        w1T=np.ascontiguousarray(w1.T), w2T=np.ascontiguousarray(w2.T),
        eye=np.eye(128, dtype=np.float32),
        ones_row=np.ones((1, 128), dtype=np.float32),
        ones_col=np.ones((128, 1), dtype=np.float32),
        sgnm=np.full((128, 1), np.int32(-2147483648), dtype=np.int32),
        cst=np.broadcast_to(np.array([MAGIC, EPS], dtype=np.float32),
                            (128, 2)).copy(),
    )
    nc = build_nc()
    in_maps = [dict(base, u=np.ascontiguousarray(u[i])) for i in range(8)]
    res = run_bass_kernel_spmd(nc, in_maps, core_ids=list(range(8)))
    return np.stack([res.results[i]["o"] for i in range(8)], axis=0)


# version 43
# speedup vs baseline: 1.2539x; 1.0393x over previous
import numpy as np
import concourse.bass as bass
import concourse.mybir as mybir
from concourse.bass_utils import run_bass_kernel_spmd

F32 = mybir.dt.float32
BF16 = mybir.dt.bfloat16
I32 = mybir.dt.int32
AF = mybir.ActivationFunctionType
OP = mybir.AluOpType
AX = mybir.AxisListType

T, D, N, H, R = 4096, 1024, 16, 4096, 64
LC = 256
NCH = T // LC
TB = LC // 128
DT = D // 128
HT = H // 128
NU = N * DT
EPS = 1e-5
MAGIC = 12582912.0
INV_D = 1.0 / D


class Sem:
    def __init__(self, nc, es, name):
        self.h = es.enter_context(nc.semaphore(name))
        self.n = 0


class Emitter:
    ENGS = ("sync", "tensor", "vector", "scalar", "gpsimd")

    def __init__(self, self_sync_sems, use_drain=False):
        self.ops = {e: [] for e in self.ENGS}
        self.waited = {e: {} for e in self.ENGS}
        self.seq = 0
        self.self_sync = self_sync_sems
        self.use_drain = use_drain

    def op(self, eng, body, waits=(), incs=(), self_sync=True):
        own = self.self_sync.get(eng) if self_sync else None
        wl = []
        drain = False
        if own is not None and own.n > 0:
            if self.use_drain and eng != "tensor":
                drain = True
            else:
                wl.append((own.h, own.n))
                self.waited[eng][id(own)] = max(
                    self.waited[eng].get(id(own), 0), own.n)
        for s, v in waits:
            if v is None or v <= 0:
                continue
            key = id(s)
            if v > self.waited[eng].get(key, 0):
                wl.append((s.h, v))
                self.waited[eng][key] = v
        il = []
        own_inced = False
        for s, k in incs:
            s.n += k
            il.append((s.h, k))
            if s is own:
                own_inced = True
        if own is not None and not own_inced:
            own.n += 1
            il.append((own.h, 1))
        self.ops[eng].append((wl, body, il, self.seq, drain))
        self.seq += 1

    def flush(self, block, max_seq=None):
        def mk(lst):
            def run(eng):
                for wl, body, il, sq_, drain in lst:
                    if max_seq is not None and sq_ > max_seq:
                        continue
                    if drain:
                        eng.drain()
                    for h, v in wl:
                        eng.wait_ge(h, v)
                    ins = body(eng)
                    for h, k in il:
                        ins.then_inc(h, k)
            return run
        block.sync(mk(self.ops["sync"]))
        block.tensor(mk(self.ops["tensor"]))
        block.vector(mk(self.ops["vector"]))
        block.scalar(mk(self.ops["scalar"]))
        block.gpsimd(mk(self.ops["gpsimd"]))


def build_nc(full_sync=False, nch=NCH, max_seq=None):
    nc = bass.Bass("TRN2", target_bir_lowering=False,
                   detect_race_conditions=full_sync)

    u_d = nc.dram_tensor("u", [T, D], F32, kind="ExternalInput")
    o_d = nc.dram_tensor("o", [T, D], F32, kind="ExternalOutput")
    wxT_d = nc.dram_tensor("wxT", [D, 96], F32, kind="ExternalInput")
    wdtTb_d = nc.dram_tensor("wdtTb", [R + 1, D], F32, kind="ExternalInput")
    logA_d = nc.dram_tensor("logA_r", [128, DT * N], F32, kind="ExternalInput")
    Dp_d = nc.dram_tensor("D_r", [128, DT], F32, kind="ExternalInput")
    gam_d = nc.dram_tensor("gamma_r", [128, DT], F32, kind="ExternalInput")
    bet_d = nc.dram_tensor("beta_r", [128, DT], F32, kind="ExternalInput")
    b2_d = nc.dram_tensor("b2_r", [128, DT], F32, kind="ExternalInput")
    b1_d = nc.dram_tensor("b1_r", [128, HT], F32, kind="ExternalInput")
    w1T_d = nc.dram_tensor("w1T", [D, H], F32, kind="ExternalInput")
    w2T_d = nc.dram_tensor("w2T", [H, D], F32, kind="ExternalInput")
    eye_d = nc.dram_tensor("eye", [128, 128], F32, kind="ExternalInput")
    onr_d = nc.dram_tensor("ones_row", [1, 128], F32, kind="ExternalInput")
    onc_d = nc.dram_tensor("ones_col", [128, 1], F32, kind="ExternalInput")
    sgn_d = nc.dram_tensor("sgnm", [128, 1], I32, kind="ExternalInput")
    cst_d = nc.dram_tensor("cst", [128, 2], F32, kind="ExternalInput")

    wq1_d = nc.dram_tensor("wq1d", [HT * 128 * 1024], BF16, kind="Internal")
    bc_d = nc.dram_tensor("bcd", [2 * 32 * LC], F32, kind="Internal")
    scl_d = nc.dram_tensor("scld", [8], F32, kind="Internal")

    from contextlib import ExitStack
    es = ExitStack()
    with es:
        sb = nc.sbuf_tensor
        wq2 = es.enter_context(sb("wq2", [128, HT * 1024], BF16))
        g = es.enter_context(sb("g", [128, HT * LC], F32))
        uT = es.enter_context(sb("uT", [128, DT * LC], F32))
        xnT = es.enter_context(sb("xnT", [128, DT * LC], F32))
        dl = es.enter_context(sb("dl", [128, DT * LC], F32))
        wt = es.enter_context(sb("wt", [128, DT * LC], F32))
        ya = es.enter_context(sb("ya", [128, DT * LC], F32))
        xqT = es.enter_context(sb("xqT", [128, DT * LC], BF16))
        utok = es.enter_context(sb("utok", [128, D], F32))
        outst = es.enter_context(sb("outst", [128, D], F32))
        wq1s = es.enter_context(sb("wq1s", [128, 2 * 1024], BF16))
        bcB = es.enter_context(sb("bcB", [128, 2 * 512], F32))
        bcC = es.enter_context(sb("bcC", [128, 2 * 512], F32))
        a_t = es.enter_context(sb("a_t", [128, 2 * 514], F32))
        b_t = es.enter_context(sb("b_t", [128, 2 * 514], F32))
        h_t = es.enter_context(sb("h_t", [128, 514], F32))
        ptree = es.enter_context(sb("ptree", [128, 256], F32))
        dsb = es.enter_context(sb("dsb", [128, LC], F32))
        bcsb = es.enter_context(sb("bcsb", [32, LC], F32))
        sq = es.enter_context(sb("sq", [128, 2 * LC], F32))
        sp1 = es.enter_context(sb("sp1", [128, 1024], F32))
        sp2 = es.enter_context(sb("sp2", [128, 1024], F32))
        pre = es.enter_context(sb("pre", [128, 2 * LC], F32))
        mrun = es.enter_context(sb("mrun", [128, LC], F32))
        mnrun = es.enter_context(sb("mnrun", [128, LC], F32))
        c1s = es.enter_context(sb("c1s", [128, LC], F32))
        scs = es.enter_context(sb("scs", [128, LC], F32))
        c2s = es.enter_context(sb("c2s", [128, LC], F32))
        f1 = es.enter_context(sb("f1", [128, LC], F32))
        f2 = es.enter_context(sb("f2", [128, LC], F32))
        stt_sb = es.enter_context(sb("stt", [128, NU], F32))
        wxT = es.enter_context(sb("wxTs", [128, DT * 96], F32))
        wdt = es.enter_context(sb("wdts", [R + 1, D], F32))
        A_sb = es.enter_context(sb("A_sb", [128, DT * N], F32))
        Dg = es.enter_context(sb("Dg", [128, DT], F32))
        gam = es.enter_context(sb("gam", [128, DT], F32))
        bet = es.enter_context(sb("bet", [128, DT], F32))
        b2c = es.enter_context(sb("b2c", [128, DT], F32))
        b1c = es.enter_context(sb("b1c", [128, HT], F32))
        eye = es.enter_context(sb("eyes", [128, 128], F32))
        onr = es.enter_context(sb("onr", [1, 128], F32))
        onc = es.enter_context(sb("onc", [128, 1], F32))
        sgn = es.enter_context(sb("sgn", [128, 1], I32))
        mwc = es.enter_context(sb("mwc", [128, 4], F32))
        acc = es.enter_context(sb("acc", [128, 4], F32))
        cst = es.enter_context(sb("cst_s", [128, 2], F32))
        rows = es.enter_context(sb("rows", [1, 5 * LC], F32))
        RW = lambda i: rows[0:1, i * LC:(i + 1) * LC]

        ps = nc.psum_tensor
        ps_tr = es.enter_context(ps("ps_tr", [128, 512], F32))
        ps_xd = es.enter_context(ps("ps_xd", [128, 512], F32))
        ps_d = es.enter_context(ps("ps_d", [128, 512], F32))
        ps_b1 = es.enter_context(ps("ps_b1", [128, 512], F32))
        ps_b2 = es.enter_context(ps("ps_b2", [128, 512], F32))
        ps_b3 = es.enter_context(ps("ps_b3", [128, 512], F32))
        ps_m1 = es.enter_context(ps("ps_m1", [128, 512], F32))
        ps_m2 = es.enter_context(ps("ps_m2", [128, 512], F32))

        sems = {k: Sem(nc, es, "s_" + k) for k in
                ("ld", "du", "dw", "dwq", "dbc", "dbw", "dsc", "do",
                 "pe", "act", "dve", "pool", "dbc1", "dw1", "am0", "am1")}
        s_am0, s_am1 = sems["am0"], sems["am1"]
        s_ld, s_du, s_dw, s_dwq = (sems[k] for k in ("ld", "du", "dw", "dwq"))
        s_dbc, s_dbw, s_dsc, s_do = (sems[k] for k in ("dbc", "dbw", "dsc", "do"))
        s_dbc1, s_dw1 = sems["dbc1"], sems["dw1"]
        s_pe, s_act, s_dve, s_pool = (sems[k] for k in ("pe", "act", "dve", "pool"))

        nc.const_aps.register(F32, MAGIC, cst[:, 0:1])
        nc.const_aps.register(F32, EPS, cst[:, 1:2])

        self_sync = {"vector": s_dve, "scalar": s_act, "gpsimd": s_pool}
        if full_sync:
            self_sync["tensor"] = s_pe
        em = Emitter(self_sync, use_drain=not full_sync)
        E = em.op

        uT3 = uT[:].rearrange("p (dt l) -> p dt l", dt=DT)
        tr3 = ps_tr[:].rearrange("p (j l) -> p j l", j=4)

        for dst, src in ((wxT[:], bass.AP(wxT_d, 0, [[96, 128], [96 * 128, DT], [1, 96]])),
                         (wdt[:], wdtTb_d.ap()),
                         (A_sb[:], logA_d.ap()),
                         (Dg[:], Dp_d.ap()), (gam[:], gam_d.ap()),
                         (bet[:], bet_d.ap()), (b2c[:], b2_d.ap()),
                         (b1c[:], b1_d.ap()), (eye[:], eye_d.ap()),
                         (onr[:], onr_d.ap()), (onc[:], onc_d.ap()),
                         (sgn[:], sgn_d.ap()),
                         (cst[:], cst_d.ap())):
            E("sync", lambda e, d_=dst, s_=src: e.dma_start(out=d_, in_=s_),
              incs=[(s_ld, 16)])
        ld_all = s_ld.n

        E("scalar", lambda e: e.activation(sp1[:, 0:DT * N], A_sb[:], AF.Exp),
          waits=[(s_ld, ld_all)], incs=[(s_act, 1)])
        E("vector", lambda e: e.tensor_scalar_mul(A_sb[:], sp1[:, 0:DT * N], -1.0),
          waits=[(s_act, s_act.n)], incs=[(s_dve, 1)])
        E("vector", lambda e: e.memset(dsb[64:65, :], 1.0), incs=[(s_dve, 1)])
        E("vector", lambda e: e.memset(a_t[:], 0.0), incs=[(s_dve, 1)])
        E("vector", lambda e: e.memset(stt_sb[:], 0.0), incs=[(s_dve, 1)])

        stg = g[:, 4096:8192]
        stg2 = g[:, 0:4096]
        wq_bf = g[:].bitcast(BF16)[0:128, 8192:12288]

        def wblk_ap(is_w1, blk):
            if is_w1:
                return bass.AP(w1T_d, blk * 512, [[H, 128], [H * 128, DT], [1, 512]])
            return bass.AP(w2T_d, blk * 4 * 128 * D, [[D, 128], [D * 128, 4], [1, D]])

        def w_half_ap(is_w1, hb):
            if is_w1:
                return bass.AP(w1T_d,
                               (hb // 2) * 512 + (hb % 2) * (H * 128 * 4),
                               [[H, 128], [H * 128, 4], [1, 512]])
            return bass.AP(w2T_d, hb * (D * 128 * 2),
                           [[D, 128], [D * 128, 2], [1, D]])

        def w_absmean_dual():
            am1 = [0, 0]
            am2 = [0, 0]
            sems1 = (s_dw, s_dw1)
            sems2 = (s_am0, s_am1)
            for hb in range(16):
                sl = hb % 2
                b1 = stg[:, sl * 2048:(sl + 1) * 2048]
                E("sync", lambda e, a=w_half_ap(True, hb), b_=b1: e.dma_start(
                    out=b_, in_=a),
                  waits=[(s_dve, am1[sl])], incs=[(sems1[sl], 16)])
                d1 = sems1[sl].n
                E("vector", lambda e, b_=b1: e.tensor_reduce(
                    acc[:, 1:2], b_, AX.X, OP.add, apply_absolute_value=True),
                  waits=[(sems1[sl], d1)])
                if hb == 0:
                    E("vector", lambda e: e.tensor_copy(acc[:, 0:1],
                                                        acc[:, 1:2]),
                      incs=[(s_dve, 1)])
                else:
                    E("vector", lambda e: e.tensor_tensor(
                        acc[:, 0:1], acc[:, 0:1], acc[:, 1:2], OP.add),
                      incs=[(s_dve, 1)])
                am1[sl] = s_dve.n
                b2_ = uT[:] if sl == 0 else xnT[:]
                E("gpsimd", lambda e, a=w_half_ap(False, hb), b_=b2_:
                  e.dma_start(out=b_, in_=a),
                  waits=[(s_dve, am2[sl])],
                  incs=[(sems2[sl], 16)], self_sync=False)
                d2 = sems2[sl].n
                E("vector", lambda e, b_=b2_: e.tensor_reduce(
                    acc[:, 3:4], b_, AX.X, OP.add, apply_absolute_value=True),
                  waits=[(sems2[sl], d2)])
                if hb == 0:
                    E("vector", lambda e: e.tensor_copy(acc[:, 2:3],
                                                        acc[:, 3:4]),
                      incs=[(s_dve, 1)])
                else:
                    E("vector", lambda e: e.tensor_tensor(
                        acc[:, 2:3], acc[:, 2:3], acc[:, 3:4], OP.add),
                      incs=[(s_dve, 1)])
                am2[sl] = s_dve.n
            return am1, max(am2)

        def w_scale_finish(col, acol):
            E("tensor", lambda e, ac=acol: e.matmul(
                ps_tr[0:1, 0:1], onc[:], acc[:, ac:ac + 1],
                start=True, stop=True),
              waits=[(s_dve, s_dve.n), (s_ld, ld_all)], incs=[(s_pe, 1)])
            pev = s_pe.n
            E("vector", lambda e: e.tensor_scalar(
                RW(3)[:, 0:1], ps_tr[0:1, 0:1], 1.0 / (D * H), EPS, OP.mult, OP.max),
              waits=[(s_pe, pev), (s_dsc, s_dsc.n)])
            E("vector", lambda e: e.reciprocal(RW(3)[:, 1:2], RW(3)[:, 0:1]),
              incs=[(s_dve, 1)])
            dvv = s_dve.n
            E("sync", lambda e, c=col: e.dma_start(
                out=bass.AP(scl_d, 2 * c, [[1, 1], [1, 2]]), in_=RW(3)[0:1, 0:2]),
              waits=[(s_dve, dvv), (s_dsc, s_dsc.n)], incs=[(s_dsc, 16)])
            dscv = s_dsc.n
            E("sync", lambda e, c=col: e.dma_start(
                out=mwc[:, 2 * c:2 * c + 2],
                in_=bass.AP(scl_d, 2 * c, [[0, 128], [1, 2]])),
              waits=[(s_dsc, dscv)], incs=[(s_dsc, 16)])

        def emit_quant(am1_slots):
            sw1_col = mwc[:, 1:2]
            sw2_col = mwc[:, 3:4]
            mwrdy = s_dsc.n
            yabf = ya[:].bitcast(BF16)
            qsems = (s_dw, s_dw1)
            actm = [0, 0]
            dvem = [0, 0]
            inm = list(am1_slots)
            dwqm = s_dwq.n
            for idx in range(32):
                is_w1 = idx < 16
                hb = idx % 16
                blk, h2 = hb // 2, hb % 2
                sl = idx % 2
                sin = stg[:, sl * 2048:(sl + 1) * 2048]
                ssc = stg2[:, sl * 2048:(sl + 1) * 2048]
                E("sync", lambda e, a=w_half_ap(is_w1, hb), b_=sin:
                  e.dma_start(out=b_, in_=a),
                  waits=[(s_act, actm[sl]), (s_dve, inm[sl])],
                  incs=[(qsems[sl], 16)])
                dw = qsems[sl].n
                E("scalar", lambda e, i_=sin, o_=ssc,
                  sc=(sw1_col if is_w1 else sw2_col): e.activation(
                    o_, i_, AF.Identity, bias=MAGIC, scale=sc),
                  waits=[(qsems[sl], dw), (s_dve, dvem[sl]), (s_dsc, mwrdy)],
                  incs=[(s_act, 1)])
                actm[sl] = s_act.n
                E("vector", lambda e, o_=ssc: e.tensor_scalar(
                    o_, o_, MAGIC, 1.0, OP.subtract, OP.min),
                  waits=[(s_act, actm[sl])])
                if is_w1:
                    out = bass.AP(yabf.tensor, yabf.offset + h2 * 512,
                                  [[4096, 128], [128, 4], [1024, 4], [1, 128]])
                    E("vector", lambda e, o_=out, i_=ssc:
                      e.tensor_single_scalar(o_, i_, -1.0, OP.max),
                      waits=[(s_dwq, dwqm)], incs=[(s_dve, 1)])
                else:
                    o2 = wq2[:, blk * 4096 + h2 * 2048:
                             blk * 4096 + h2 * 2048 + 2048]
                    E("vector", lambda e, o_=o2, i_=ssc:
                      e.tensor_single_scalar(o_, i_, -1.0, OP.max),
                      incs=[(s_dve, 1)])
                dvem[sl] = s_dve.n
                if is_w1 and h2 == 1:
                    E("sync", lambda e, b=blk, dv=dvem[sl]: e.dma_start(
                        out=bass.AP(wq1_d, b * 4 * 131072,
                                    [[1024, 128], [131072, 4], [1, 1024]]),
                        in_=yabf.rearrange("p (hb f) -> p hb f", hb=4)),
                      waits=[(s_dve, dvem[sl])], incs=[(s_dwq, 16)])
                    dwqm = s_dwq.n

        am1_done, am2_done = w_absmean_dual()
        w_scale_finish(0, 0)
        w_scale_finish(1, 2)
        mw_ready = s_dsc.n
        wq_done_dve = 0
        wq1_written = 0

        NUP = 64
        unit_dve = [0] * NUP
        unit_pool = [0] * NUP

        def stp_slice(stp, i_):
            return bass.AP(stp.tensor, stp.offset + i_, [list(stp.ap[0]), [1, 2]])

        atp = a_t[:]
        btp = b_t[:]
        htp = h_t[:]
        wtp = wt[:]
        bBp = bcB[:]
        bCp = bcC[:]
        stp = stt_sb[:]
        sqa = sq[:]
        rep8_0 = bass.AP(sqa.tensor, sqa.offset, [list(sqa.ap[0]), [0, 8], [1, LC]])
        rep8_1 = bass.AP(sqa.tensor, sqa.offset + LC,
                         [list(sqa.ap[0]), [0, 8], [1, LC]])
        xnT3d = xnT[:].rearrange("p (r f) -> p r f", r=8)
        uT3d = uT[:].rearrange("p (r f) -> p r f", r=8)
        scsa = scs[:]
        screp8 = bass.AP(scsa.tensor, scsa.offset, [list(scsa.ap[0]), [0, 8], [1, LC]])
        screp4 = bass.AP(scsa.tensor, scsa.offset, [list(scsa.ap[0]), [0, 4], [1, LC]])
        hq = g[:].bitcast(BF16)
        ZB = 4096

        roll = {
            "outdma": 0, "bcw": (s_dbw, 0), "wln_dve": wq_done_dve,
            "mm1_pe": 0,
            "sq_pe": 0, "c2s_act": 0, "evxd_act": 0, "pre2": [0, 0],
            "psd2": [0, 0], "ln2m": [0, 0], "round_all": 0, "outtr_pe": 0,
            "uevac_act": 0, "tailevac_dve": 0, "mm1war2": [0, 0],
            "prep": None, "xq_dve": 0, "stage2_done": 0,
            "z_dve": 0, "xn_dve": am2_done, "sp2run_pool": 0,
        }

        def scale_rows(src_dve_mark, mw_off, sp_ln_all):
            dvv_ = src_dve_mark
            for k in range(TB):
                E("tensor", lambda e, kk=k: e.transpose(
                    ps_tr[:, kk * 128:(kk + 1) * 128],
                    mrun[:, kk * 128:(kk + 1) * 128], eye[:]),
                  waits=[(s_dve, dvv_),
                         (s_act, max(sp_ln_all, roll["uevac_act"]))])
                E("tensor", lambda e, kk=k: e.transpose(
                    ps_tr[:, 256 + kk * 128:256 + (kk + 1) * 128],
                    mnrun[:, kk * 128:(kk + 1) * 128], eye[:]),
                  incs=[(s_pe, 1)] if k == TB - 1 else ())
            for k in range(TB):
                E("vector", lambda e, kk=k: e.tensor_reduce(
                    acc[:, 1:2], ps_tr[:, kk * 128:(kk + 1) * 128], AX.X, OP.max),
                  waits=[(s_pe, s_pe.n)])
                E("vector", lambda e, kk=k: e.tensor_reduce(
                    acc[:, 0:1], ps_tr[:, 256 + kk * 128:256 + (kk + 1) * 128],
                    AX.X, OP.max))
                E("vector", lambda e: e.tensor_tensor(
                    acc[:, 1:2], acc[:, 1:2], acc[:, 0:1], OP.max))
                E("vector", lambda e: e.tensor_scalar_max(acc[:, 1:2], acc[:, 1:2],
                                                          EPS))
                E("vector", lambda e: e.reciprocal(acc[:, 0:1], acc[:, 1:2]))
                E("vector", lambda e: e.tensor_scalar_mul(acc[:, 2:3], acc[:, 0:1],
                                                          127.0))
                E("vector", lambda e, mo=mw_off: e.tensor_scalar(
                    acc[:, 3:4], acc[:, 1:2], mwc[:, mo:mo + 1], 1.0 / 127.0,
                    OP.mult, OP.mult),
                  waits=[(s_dsc, mw_ready)], incs=[(s_dve, 1)])
                dv2 = s_dve.n
                E("tensor", lambda e, kk=k: e.transpose(
                    ps_tr[0:1, kk * 128:(kk + 1) * 128], acc[:, 2:3], eye[:]),
                  waits=[(s_dve, dv2)])
                E("tensor", lambda e, kk=k: e.transpose(
                    ps_tr[0:1, 256 + kk * 128:256 + (kk + 1) * 128],
                    acc[:, 3:4], eye[:]),
                  incs=[(s_pe, 1)])
            pe_mark = s_pe.n
            E("vector", lambda e: e.tensor_copy(RW(3)[:], ps_tr[0:1, 0:LC]),
              waits=[(s_pe, pe_mark)])
            E("vector", lambda e: e.tensor_copy(RW(4)[:], ps_tr[0:1, 256:512]),
              incs=[(s_dve, 1)])
            return s_dve.n, pe_mark

        def prepA_steps(c):
            A = {"uevac_act": 0, "stats1_pe": 0, "pev": 0,
                 "sqm": [roll["sq_pe"], roll["sq_pe"]]}

            def mk_k(k):
                def fn():
                    E("sync", lambda e, kk=k, cc=c: e.dma_start(
                        out=utok[:],
                        in_=bass.AP(u_d, (cc * LC + kk * 128) * D,
                                    [[D, 128], [1, D]])),
                      waits=[(s_pe, A["pev"]),
                             (s_dve, roll["tailevac_dve"])],
                      incs=[(s_du, 16)])
                    duv = s_du.n
                    for half in range(2):
                        for j in range(4):
                            dt_ = half * 4 + j
                            E("tensor", lambda e, d_=dt_, jj=j: e.transpose(
                                ps_tr[:, jj * 128:(jj + 1) * 128],
                                utok[:, d_ * 128:(d_ + 1) * 128], eye[:]),
                              waits=[(s_du, duv), (s_ld, ld_all),
                                     (s_act, s_act.n), (s_dve, s_dve.n)],
                              incs=[(s_pe, 1)] if j == 3 else ())
                        pev = s_pe.n
                        E("scalar", lambda e, kk=k, hh=half: e.activation(
                            uT3[:, hh * 4:(hh + 1) * 4, kk * 128:(kk + 1) * 128],
                            tr3[:, :, :], AF.Identity),
                          waits=[(s_pe, pev), (s_dve, roll["xn_dve"])],
                          incs=[(s_act, 1)])
                    A["pev"] = s_pe.n
                    A["uevac_act"] = s_act.n
                    roll["uevac_act"] = s_act.n
                return fn

            def mk_stats(dlo, dhi):
                def fn():
                    sqm = A["sqm"]
                    for dt_ in range(dlo, dhi):
                        E("scalar", lambda e, d_=dt_: e.activation(
                            sq[:, (d_ % 2) * LC:(d_ % 2 + 1) * LC],
                            uT3[:, d_:d_ + 1, :], AF.Square),
                          waits=[(s_act, A["uevac_act"]), (s_pe, sqm[dt_ % 2]),
                                 (s_dve, roll["z_dve"])],
                          incs=[(s_act, 1)])
                        acv = s_act.n
                        E("tensor", lambda e, d_=dt_: e.matmul(
                            ps_b2[0:1, 0:LC], onc[:], uT3[:, d_:d_ + 1, :],
                            start=(d_ == 0), stop=(d_ == DT - 1)),
                          waits=[(s_act, A["uevac_act"]),
                                 (s_dve, roll["wln_dve"])])
                        E("tensor", lambda e, d_=dt_: e.matmul(
                            ps_b2[32:33, 0:LC], onc[:],
                            sq[:, (d_ % 2) * LC:(d_ % 2 + 1) * LC],
                            start=(d_ == 0), stop=(d_ == DT - 1)),
                          waits=[(s_act, acv)], incs=[(s_pe, 1)])
                        sqm[dt_ % 2] = s_pe.n
                    A["stats1_pe"] = s_pe.n
                return fn

            return A, [mk_k(0), mk_k(1), mk_stats(0, 4), mk_stats(4, DT)]

        def emit_prep(c, A, tailcb=None):
            cmod = c % 2
            M = {}
            cb = tailcb if tailcb is not None else (lambda: None)
            ut_evac_act = A["uevac_act"]
            M["uevac_act"] = ut_evac_act
            stats1_pe = A["stats1_pe"]
            cb()
            E("vector", lambda e: e.tensor_scalar_mul(RW(0)[:], ps_b2[0:1, 0:LC],
                                                      INV_D),
              waits=[(s_pe, stats1_pe)], incs=[(s_dve, 1)])
            rw0_dve = s_dve.n
            E("scalar", lambda e: e.activation(RW(3)[:], RW(0)[:], AF.Square),
              waits=[(s_pe, stats1_pe), (s_dve, rw0_dve if c == 0 else s_dve.n),
                     (s_dsc, mw_ready if c == 0 else 0)],
              incs=[(s_act, 1)])
            acv = s_act.n
            E("vector", lambda e: e.scalar_tensor_tensor(
                RW(3)[:], ps_b2[32:33, 0:LC], INV_D, RW(3)[:], OP.mult, OP.subtract),
              waits=[(s_act, acv)], incs=[(s_dve, 1)])
            dvv = s_dve.n
            E("scalar", lambda e: e.activation(RW(3)[:], RW(3)[:], AF.Ln, bias=EPS),
              waits=[(s_dve, dvv)])
            E("scalar", lambda e: e.activation(RW(1)[:], RW(3)[:], AF.Exp, scale=-0.5),
              incs=[(s_act, 1)])
            acv = s_act.n
            E("vector", lambda e: e.scalar_tensor_tensor(
                RW(2)[:], RW(0)[:], -1.0, RW(1)[:], OP.mult, OP.mult),
              waits=[(s_act, acv)], incs=[(s_dve, 1)])
            rows1_dve = s_dve.n
            E("tensor", lambda e: e.matmul(ps_b1[:, 0:LC], onr[:], RW(1)[:],
                                           start=True, stop=True),
              waits=[(s_dve, rows1_dve)])
            E("tensor", lambda e: e.matmul(ps_b1[:, 256:512], onr[:], RW(2)[:],
                                           start=True, stop=True),
              incs=[(s_pe, 1)])
            bc1_pe = s_pe.n
            cb()
            E("scalar", lambda e: e.activation(sq[:, 0:LC], ps_b1[:, 0:LC],
                                               AF.Identity),
              waits=[(s_pe, bc1_pe), (s_dve, 0 if c == 0 else s_dve.n)])
            E("scalar", lambda e: e.activation(sq[:, LC:2 * LC], ps_b1[:, 256:512],
                                               AF.Identity),
              incs=[(s_act, 1)])
            acv = s_act.n
            E("vector", lambda e: e.tensor_tensor(xnT3d, uT3d, rep8_0, OP.mult),
              waits=[(s_act, acv), (s_act, ut_evac_act)])
            E("vector", lambda e: e.tensor_tensor(xnT3d, xnT3d, rep8_1, OP.add),
              incs=[(s_dve, 1)])
            xn_dve = s_dve.n
            roll["xn_dve"] = xn_dve
            cb()
            for dt_ in range(DT):
                E("tensor", lambda e, d_=dt_: e.matmul(
                    ps_xd[0:96, 0:LC], wxT[:, d_ * 96:(d_ + 1) * 96],
                    xnT[:, d_ * LC:(d_ + 1) * LC],
                    start=(d_ == 0), stop=(d_ == DT - 1)),
                  waits=[(s_dve, xn_dve), (s_act, roll["evxd_act"])],
                  incs=[(s_pe, 1)] if dt_ == DT - 1 else ())
            xd_pe = s_pe.n
            E("scalar", lambda e: e.activation(dsb[0:64, :], ps_xd[0:64, 0:LC],
                                               AF.Identity),
              waits=[(s_pe, xd_pe), (s_dve, 0 if c == 0 else s_dve.n)],
              incs=[(s_act, 1)])
            E("scalar", lambda e: e.activation(bcsb[0:32, :], ps_xd[64:96, 0:LC],
                                               AF.Identity),
              waits=[roll["bcw"]], incs=[(s_act, 1)])
            evac_xd_act = s_act.n
            roll["evxd_act"] = evac_xd_act
            bwsem = s_am0 if c == 0 else s_dbw
            E("gpsimd" if c == 0 else "sync", lambda e, cm=cmod: e.dma_start(
                out=bass.AP(bc_d, cm * 32 * LC, [[LC, 32], [1, LC]]), in_=bcsb[:]),
              waits=[(s_act, evac_xd_act), (s_dbc, s_dbc.n), (s_dbc1, s_dbc1.n)],
              incs=[(bwsem, 16)], self_sync=False)
            roll["bcw"] = (bwsem, bwsem.n)
            M["bcw_now"] = (bwsem, bwsem.n)
            cb()
            psd2 = list(roll["psd2"])
            ln2m = list(roll["ln2m"])
            for dt_ in range(DT):
                dps = ps_d[:, 0:LC] if dt_ % 2 == 0 else ps_xd[:, 256:256 + LC]
                so = (dt_ % 2) * LC
                E("tensor", lambda e, d_=dt_, pp=dps: e.matmul(
                    pp, wdt[:, d_ * 128:(d_ + 1) * 128],
                    dsb[0:65, :], start=True, stop=True),
                  waits=[(s_act, evac_xd_act), (s_dve, psd2[dt_ % 2])],
                  incs=[(s_pe, 1)])
                pev = s_pe.n
                E("vector", lambda e, pp=dps, s_=so: e.tensor_single_scalar(
                    sp1[:, s_:s_ + LC].bitcast(I32), pp.bitcast(I32),
                    sgn[:, 0:1], OP.bitwise_or),
                  waits=[(s_pe, pev), (s_act, ln2m[dt_ % 2])],
                  incs=[(s_dve, 1)])
                dvv = s_dve.n
                E("scalar", lambda e, s_=so: e.activation(
                    sp2[:, s_:s_ + LC], sp1[:, s_:s_ + LC], AF.Exp),
                  waits=[(s_dve, dvv)])
                E("scalar", lambda e, s_=so: e.activation(
                    sp1[:, s_:s_ + LC], sp2[:, s_:s_ + LC], AF.Ln, bias=1.0),
                  incs=[(s_act, 1)])
                ln2m[dt_ % 2] = s_act.n
                acv = s_act.n
                E("vector", lambda e, d_=dt_, pp=dps, s_=so: e.scalar_tensor_tensor(
                    dl[:, d_ * LC:(d_ + 1) * LC], pp, 0.0,
                    sp1[:, s_:s_ + LC], OP.max, OP.add),
                  waits=[(s_act, acv)], incs=[(s_dve, 1)])
                psd2[dt_ % 2] = s_dve.n
                if dt_ % 2 == 1:
                    cb()
            roll["psd2"] = psd2
            roll["ln2m"] = ln2m
            M["sp_ln_all"] = max(ln2m)
            M["psd2"] = list(psd2)
            E("vector", lambda e: e.tensor_tensor(wt[:], dl[:], xnT[:], OP.mult),
              incs=[(s_dve, 1)])
            M["dl_dve"] = s_dve.n
            M["wt_dve"] = s_dve.n
            sqm2 = [stats1_pe, stats1_pe]
            for dt_ in range(DT):
                E("scalar", lambda e, d_=dt_: e.activation(
                    sq[:, (d_ % 2) * LC:(d_ % 2 + 1) * LC],
                    xnT[:, d_ * LC:(d_ + 1) * LC], AF.Square),
                  waits=[(s_pe, sqm2[dt_ % 2]), (s_dve, xn_dve)],
                  incs=[(s_act, 1)])
                acv = s_act.n
                E("tensor", lambda e, d_=dt_: e.matmul(
                    ps_b2[64:65, 0:LC], onc[:], xnT[:, d_ * LC:(d_ + 1) * LC],
                    start=(d_ == 0), stop=(d_ == DT - 1)),
                  waits=[(s_dve, xn_dve)])
                E("tensor", lambda e, d_=dt_: e.matmul(
                    ps_b2[0:1, 256:512], onc[:],
                    sq[:, (d_ % 2) * LC:(d_ % 2 + 1) * LC],
                    start=(d_ == 0), stop=(d_ == DT - 1)),
                  waits=[(s_act, acv)], incs=[(s_pe, 1)])
                sqm2[dt_ % 2] = s_pe.n
            stats2_pe = s_pe.n
            roll["sq_pe"] = stats2_pe
            E("vector", lambda e: e.tensor_scalar_mul(RW(0)[:], ps_b2[64:65, 0:LC],
                                                      INV_D),
              waits=[(s_pe, stats2_pe)])
            E("scalar", lambda e: e.activation(RW(3)[:], RW(0)[:], AF.Square),
              waits=[(s_pe, stats2_pe), (s_dve, s_dve.n)], incs=[(s_act, 1)])
            acv = s_act.n
            E("vector", lambda e: e.scalar_tensor_tensor(
                RW(3)[:], ps_b2[0:1, 256:512], INV_D, RW(3)[:], OP.mult, OP.subtract),
              waits=[(s_act, acv)], incs=[(s_dve, 1)])
            dvv = s_dve.n
            E("scalar", lambda e: e.activation(RW(3)[:], RW(3)[:], AF.Ln, bias=EPS),
              waits=[(s_dve, dvv)])
            E("scalar", lambda e: e.activation(RW(1)[:], RW(3)[:], AF.Exp, scale=-0.5),
              waits=[(s_pe, bc1_pe)], incs=[(s_act, 1)])
            acv = s_act.n
            E("vector", lambda e: e.tensor_tensor(RW(2)[:], RW(0)[:], RW(1)[:],
                                                  OP.mult),
              waits=[(s_act, acv), (s_pe, bc1_pe)], incs=[(s_dve, 1)])
            rows2_dve = s_dve.n
            E("tensor", lambda e: e.matmul(ps_b2[:, 0:LC], onr[:], RW(1)[:],
                                           start=True, stop=True),
              waits=[(s_dve, rows2_dve)])
            E("tensor", lambda e: e.matmul(ps_b2[:, 256:512], onr[:], RW(2)[:],
                                           start=True, stop=True),
              incs=[(s_pe, 1)])
            bc2_pe = s_pe.n
            E("scalar", lambda e: e.activation(sq[:, 0:LC], ps_b2[:, 0:LC],
                                               AF.Identity),
              waits=[(s_pe, bc2_pe), (s_dve, s_dve.n)])
            E("scalar", lambda e: e.activation(sq[:, LC:2 * LC], ps_b2[:, 256:512],
                                               AF.Identity),
              incs=[(s_act, 1)])
            acv = s_act.n
            zb = g[:, ZB:ZB + 2048].rearrange("p (r f) -> p r f", r=8)
            E("vector", lambda e: e.tensor_tensor(zb, xnT3d, rep8_0, OP.mult),
              waits=[(s_act, acv), (s_dve, wq_done_dve if c == 0 else 0),
                     (s_dwq, wq1_written if c == 0 else 0)])
            E("vector", lambda e: e.tensor_tensor(zb, zb, rep8_1, OP.subtract),
              incs=[(s_dve, 1)])
            z_dve = s_dve.n
            roll["z_dve"] = z_dve
            wlb = ZB + 2048
            for dt_ in range(DT):
                E("scalar", lambda e, d_=dt_: e.activation(
                    g[:, wlb + d_ * LC:wlb + (d_ + 1) * LC],
                    g[:, ZB + d_ * LC:ZB + (d_ + 1) * LC], AF.Identity,
                    bias=bet[:, d_:d_ + 1], scale=gam[:, d_:d_ + 1]),
                  waits=[(s_dve, z_dve)],
                  incs=[(s_act, 1)])
            wlnact = s_act.n
            E("vector", lambda e: e.tensor_tensor(
                sp1[:], g[:, wlb:wlb + 1024], g[:, wlb + 1024:wlb + 2048], OP.max),
              waits=[(s_act, wlnact)])
            E("vector", lambda e: e.tensor_tensor(
                sp1[:, 0:512], sp1[:, 0:512], sp1[:, 512:1024], OP.max))
            E("vector", lambda e: e.tensor_tensor(
                mrun[:], sp1[:, 0:256], sp1[:, 256:512], OP.max))
            E("vector", lambda e: e.tensor_tensor(
                sp2[:], g[:, wlb:wlb + 1024], g[:, wlb + 1024:wlb + 2048], OP.min))
            E("vector", lambda e: e.tensor_tensor(
                sp2[:, 0:512], sp2[:, 0:512], sp2[:, 512:1024], OP.min))
            E("vector", lambda e: e.tensor_tensor(
                sp2[:, 0:256], sp2[:, 0:256], sp2[:, 256:512], OP.min))
            E("vector", lambda e: e.tensor_scalar_mul(mnrun[:], sp2[:, 0:256], -1.0),
              incs=[(s_dve, 1)])
            wln_dve = s_dve.n
            roll["wln_dve"] = wln_dve
            q1rows_dve, _ = scale_rows(wln_dve, 0, M["sp_ln_all"])
            E("tensor", lambda e: e.matmul(ps_b3[:, 0:LC], onr[:], RW(3)[:],
                                           start=True, stop=True),
              waits=[(s_dve, q1rows_dve), (s_act, roll["c2s_act"])])
            E("tensor", lambda e: e.matmul(ps_b3[:, 256:512], onr[:], RW(4)[:],
                                           start=True, stop=True),
              incs=[(s_pe, 1)])
            bc3_pe = s_pe.n
            E("scalar", lambda e: e.activation(c1s[:], ps_b3[:, 256:512],
                                               AF.Identity),
              waits=[(s_pe, bc3_pe), (s_dve, max(roll["mm1war2"]))],
              incs=[(s_act, 1)])
            E("scalar", lambda e: e.activation(scs[:], ps_b3[:, 0:LC], AF.Identity),
              waits=[(s_dve, roll["round_all"])], incs=[(s_act, 1)])
            c1s_act = s_act.n
            M["c1s_act"] = c1s_act
            E("vector", lambda e: e.tensor_tensor(
                g[:, ZB:ZB + 2048].rearrange("p (r f) -> p r f", r=8),
                g[:, wlb:wlb + 2048].rearrange("p (r f) -> p r f", r=8),
                screp8, OP.mult),
              waits=[(s_act, c1s_act), (s_pe, roll["mm1_pe"])])
            E("vector", lambda e: e.tensor_scalar(
                xqT[:], g[:, ZB:ZB + 2048], MAGIC, MAGIC, OP.add, OP.subtract),
              incs=[(s_dve, 1)])
            M["xq_dve"] = s_dve.n
            M["bc_done"] = {}

            def bc_dma(pr):
                war = 0
                un = ((pr - 2) % 8) * DT + DT - 1
                if c > 0 or pr >= 2:
                    war = unit_dve[un]
                if c == 0 and pr < 2:
                    sem = s_am0 if pr == 0 else s_am1
                else:
                    sem = s_dbc if pr % 2 == 0 else s_dbc1
                for which, buf in ((0, bcB), (1, bcC)):
                    off = cmod * 32 * LC + (which * 16 + pr * 2) * LC
                    E("gpsimd" if (c == 0 and pr < 2) else "sync",
                      lambda e, o_=off, b_=buf, p_=pr: e.dma_start(
                        out=b_[:, (p_ % 2) * 512:(p_ % 2) * 512 + 512],
                        in_=bass.AP(bc_d, o_, [[0, 128], [LC, 2], [1, LC]])),
                      waits=[M["bcw_now"], (s_dve, war)],
                      incs=[(sem, 16)], self_sync=False)
                M["bc_done"][pr] = (sem, sem.n)

            M["bc_dma"] = bc_dma
            bc_dma(0)
            bc_dma(1)
            return M

        def emit_pd(c, M, pr, db):
            pd = pr * DT + db
            sl = pd % 2
            so = sl * 514
            bo = (pr % 2) * 512
            idx = db * N + pr * 2
            aw = unit_dve[(pd - 2) % NUP] if (c > 0 or pd >= 2) else 0
            aw = max(aw, roll.get("azero_dve", 0))
            pw = unit_pool[(pd - 2) % NUP] if (c > 0 or pd >= 2) else 0
            for j in range(2):
                E("scalar", lambda e, d_=db, i_=idx, j_=j, o_=so: e.activation(
                    bass.AP(atp.tensor, atp.offset + o_ + j_ * 257 + 1,
                            [list(atp.ap[0]), [1, 256]]),
                    dl[:, d_ * LC:(d_ + 1) * LC],
                    AF.Exp, scale=A_sb[:, i_ + j_:i_ + j_ + 1]),
                  waits=[(s_dve, max(aw, M["dl_dve"]))],
                  incs=[(s_act, 1)] if j == 1 else ())
            E("scalar", lambda e, o_=so, i_=idx: e.activation(
                bass.AP(btp.tensor, btp.offset + o_, [list(btp.ap[0]), [257, 2]]),
                stp_slice(stp, i_), AF.Identity),
              waits=[(s_pool, pw), (s_pe, roll["outtr_pe"]),
                     (s_dve, aw if (c > 0 or pd >= 2) else M["dl_dve"])],
              incs=[(s_act, 1)])
            a_act = s_act.n
            E("vector", lambda e, d_=db, o_=so, b_=bo: e.tensor_tensor(
                bass.AP(btp.tensor, btp.offset + o_ + 1,
                        [list(btp.ap[0]), [257, 2], [1, 256]]),
                bass.AP(wtp.tensor, wtp.offset + d_ * LC,
                        [list(wtp.ap[0]), [0, 2], [1, 256]]),
                bass.AP(bBp.tensor, bBp.offset + b_,
                        [list(bBp.ap[0]), [256, 2], [1, 256]]),
                OP.mult),
              waits=[M["bc_done"][pr], (s_pool, pw), (s_dve, M["wt_dve"]),
                     (s_pe, roll["outtr_pe"])])
            E("vector", lambda e, o_=so: e.tensor_tensor_scan(
                htp[0:128, 0:514],
                bass.AP(atp.tensor, atp.offset + o_, [list(atp.ap[0]), [1, 514]]),
                bass.AP(btp.tensor, btp.offset + o_, [list(btp.ap[0]), [1, 514]]),
                0.0, OP.mult, OP.add),
              waits=[(s_act, a_act)])
            E("vector", lambda e, i_=idx: e.tensor_copy(
                stp_slice(stp, i_),
                bass.AP(htp.tensor, htp.offset + 256, [list(htp.ap[0]), [257, 2]])))
            E("vector", lambda e, o_=so, b_=bo: e.tensor_tensor(
                bass.AP(btp.tensor, btp.offset + o_ + 1,
                        [list(btp.ap[0]), [257, 2], [1, 256]]),
                bass.AP(htp.tensor, htp.offset + 1,
                        [list(htp.ap[0]), [257, 2], [1, 256]]),
                bass.AP(bCp.tensor, bCp.offset + b_,
                        [list(bCp.ap[0]), [256, 2], [1, 256]]),
                OP.mult),
              incs=[(s_dve, 1)])
            unit_dve[pd] = s_dve.n
            E("gpsimd", lambda e, o_=so: e.tensor_tensor(
                ptree[:],
                bass.AP(btp.tensor, btp.offset + o_ + 1, [list(btp.ap[0]), [1, 256]]),
                bass.AP(btp.tensor, btp.offset + o_ + 258,
                        [list(btp.ap[0]), [1, 256]]),
                OP.add),
              waits=[(s_dve, unit_dve[pd]),
                     (s_act, M["yainit_act"][db]) if pr == 0 else (s_act, 0)])
            E("gpsimd", lambda e, d_=db: e.tensor_tensor(
                ya[:, d_ * LC:(d_ + 1) * LC],
                ya[:, d_ * LC:(d_ + 1) * LC], ptree[:], OP.add),
              incs=[(s_pool, 1)])
            unit_pool[pd] = s_pool.n

        def emit_stage2A(c, M, ht):
            sl = ht % 2
            wsem = s_dw if ht % 2 == 0 else s_dw1
            E("sync", lambda e, h_=ht, s_=sl: e.dma_start(
                out=wq1s[:, s_ * 1024:(s_ + 1) * 1024],
                in_=bass.AP(wq1_d, h_ * 131072, [[1024, 128], [1, 1024]])),
              waits=[(s_dwq, wq1_written),
                     (s_dve, wq_done_dve if (c == 0 and ht < 2) else 0),
                     (s_pe, M["mm1_marks"][ht - 2] if ht >= 2 else roll["mm1_pe"])],
              incs=[(wsem, 16)])
            dwv = (wsem, wsem.n)
            m1ps = (ps_m1 if ht % 2 == 0 else ps_m2)[:, 0:LC]
            for dt_ in range(DT):
                E("tensor", lambda e, d_=dt_, s_=sl, pp=m1ps: e.matmul(
                    pp,
                    wq1s[:, s_ * 1024 + d_ * 128:s_ * 1024 + (d_ + 1) * 128],
                    xqT[:, d_ * LC:(d_ + 1) * LC],
                    start=(d_ == 0), stop=(d_ == DT - 1)),
                  waits=[dwv, (s_dve, M["xq_dve"]),
                         (s_dve, M["pre_marks"][ht - 2] if ht >= 2
                          else roll["mm1war2"][sl])],
                  incs=[(s_pe, 1)] if dt_ == DT - 1 else ())
            M["mm1_marks"][ht] = s_pe.n
            E("vector", lambda e, s_=sl, pp=m1ps: e.tensor_tensor(
                pre[:, s_ * LC:(s_ + 1) * LC], pp, c1s[:], OP.mult),
              waits=[(s_pe, M["mm1_marks"][ht]), (s_act, M["c1s_act"]),
                     (s_act, M["gelu_marks"][ht - 2] if ht >= 2 else 0)],
              incs=[(s_dve, 1)])
            M["pre_marks"][ht] = s_dve.n

        def emit_stage2B(c, M, ht):
            sl = ht % 2
            E("scalar", lambda e, h_=ht, s_=sl: e.activation(
                g[:, h_ * LC:(h_ + 1) * LC], pre[:, s_ * LC:(s_ + 1) * LC],
                AF.Gelu, bias=b1c[:, h_:h_ + 1]),
              waits=[(s_dve, M["pre_marks"][ht])],
              incs=[(s_act, 1)])
            M["gelu_marks"][ht] = s_act.n
            if ht % 4 == 3:
                qb = (ht - 3) * LC
                if ht == 3:
                    E("vector", lambda e, q_=qb: e.tensor_copy(
                        sp1[:], g[:, q_:q_ + 1024]),
                      waits=[(s_act, M["gelu_marks"][ht]),
                             (s_pe, roll["outtr_pe"])],
                      incs=[(s_dve, 1)])
                    E("vector", lambda e, q_=qb: e.tensor_scalar_mul(
                        sp2[:], g[:, q_:q_ + 1024], -1.0),
                      incs=[(s_dve, 1)])
                else:
                    E("vector", lambda e, q_=qb: e.tensor_tensor(
                        sp1[:], sp1[:], g[:, q_:q_ + 1024], OP.max),
                      waits=[(s_act, M["gelu_marks"][ht])], incs=[(s_dve, 1)])
                    E("vector", lambda e, q_=qb: e.scalar_tensor_tensor(
                        sp2[:], g[:, q_:q_ + 1024], -1.0, sp2[:],
                        OP.mult, OP.max),
                      incs=[(s_dve, 1)])

        def emit_q2_round(c, M):
            roll["mm1_pe"] = M["mm1_marks"][HT - 1]
            roll["mm1war2"] = [M["pre_marks"][HT - 2], M["pre_marks"][HT - 1]]
            E("vector", lambda e: e.tensor_tensor(
                sp1[:, 0:512], sp1[:, 0:512], sp1[:, 512:1024], OP.max))
            E("vector", lambda e: e.tensor_tensor(
                mrun[:], sp1[:, 0:256], sp1[:, 256:512], OP.max))
            E("vector", lambda e: e.tensor_tensor(
                sp2[:, 0:512], sp2[:, 0:512], sp2[:, 512:1024], OP.max))
            E("vector", lambda e: e.tensor_tensor(
                mnrun[:], sp2[:, 0:256], sp2[:, 256:512], OP.max),
              incs=[(s_dve, 1)])
            q2max_dve = s_dve.n
            q2rows_dve, _ = scale_rows(q2max_dve, 2, M["sp_ln_all"])
            E("tensor", lambda e: e.matmul(ps_b3[:, 0:LC], onr[:], RW(3)[:],
                                           start=True, stop=True),
              waits=[(s_dve, q2rows_dve), (s_act, M["c1s_act"])])
            E("tensor", lambda e: e.matmul(ps_b3[:, 256:512], onr[:], RW(4)[:],
                                           start=True, stop=True),
              incs=[(s_pe, 1)])
            bc32_pe = s_pe.n
            E("scalar", lambda e: e.activation(c2s[:], ps_b3[:, 256:512],
                                               AF.Identity),
              waits=[(s_pe, bc32_pe), (s_dve, s_dve.n)], incs=[(s_act, 1)])
            E("scalar", lambda e: e.activation(scs[:], ps_b3[:, 0:LC], AF.Identity),
              waits=[(s_dve, M["xq_dve"])], incs=[(s_act, 1)])
            c2s_act = s_act.n
            roll["c2s_act"] = c2s_act
            M["c2s_act"] = c2s_act
            round_marks = [0] * HT
            for q_ in range(HT // 4):
                E("vector", lambda e, qq=q_: e.tensor_tensor(
                    sp1[:].rearrange("p (r f) -> p r f", r=4),
                    g[:, qq * 1024:(qq + 1) * 1024].rearrange(
                        "p (r f) -> p r f", r=4),
                    screp4, OP.mult),
                  waits=[(s_act, c2s_act), (s_pe, roll["outtr_pe"])])
                E("vector", lambda e, qq=q_: e.tensor_scalar(
                    hq[:, qq * 1024:(qq + 1) * 1024], sp1[:],
                    MAGIC, MAGIC, OP.add, OP.subtract),
                  incs=[(s_dve, 1)])
                for j in range(4):
                    round_marks[q_ * 4 + j] = s_dve.n
            M["round_marks"] = round_marks
            roll["round_all"] = s_dve.n

        def _mm2_group(c, M, dt_, war):
            m2ps = (ps_m1 if dt_ % 2 == 0 else ps_m2)[:, 256:256 + LC]
            for ht in range(HT):
                E("tensor", lambda e, d_=dt_, h_=ht, pp=m2ps: e.matmul(
                    pp,
                    wq2[:, h_ * 1024 + d_ * 128:h_ * 1024 + (d_ + 1) * 128],
                    hq[:, h_ * LC:(h_ + 1) * LC],
                    start=(h_ == 0), stop=(h_ == HT - 1)),
                  waits=[(s_dve, M["round_marks"][ht]), (s_dve, war[dt_ % 2])],
                  incs=[(s_pe, 1)] if ht == HT - 1 else ())
            M["mm2_pe"][dt_] = s_pe.n

        def emit_mm2(c, M):
            M["mm2_pe"] = [0] * DT
            M["m2war"] = list(roll.get("m2war2", [0, 0]))
            _mm2_group(c, M, 0, M["m2war"])

        def make_tailcb(c, M):
            state = {"dt": 0}
            M["e3_marks"] = [0] * DT
            war = M["m2war"]

            def cb():
                dt_ = state["dt"]
                if dt_ >= DT:
                    return
                state["dt"] += 1
                m2ps = (ps_m1 if dt_ % 2 == 0 else ps_m2)[:, 256:256 + LC]
                eo = (atp if dt_ < 4 else btp)
                E("vector", lambda e, pp=m2ps: e.tensor_tensor(
                    f1[:], pp, c2s[:], OP.mult),
                  waits=[(s_pe, M["mm2_pe"][dt_]), (s_act, M["c2s_act"])],
                  incs=[(s_dve, 1)])
                war[dt_ % 2] = s_dve.n
                E("vector", lambda e, d_=dt_, b_=eo: e.tensor_tensor(
                    bass.AP(b_.tensor, b_.offset + (d_ % 4) * LC,
                            [list(b_.ap[0]), [1, LC]]),
                    f1[:], ya[:, d_ * LC:(d_ + 1) * LC], OP.add),
                  waits=[(s_pool, M["mamba_pool"])],
                  incs=[(s_dve, 1)])
                M["e3_marks"][dt_] = s_dve.n
                if dt_ + 1 < DT:
                    _mm2_group(c, M, dt_ + 1, war)
                if dt_ == DT - 1:
                    roll["m2war2"] = war

            return cb

        def emit_out(c, M):
            for k in range(TB):
                E("sync", lambda e, kk=k, c2=c: e.dma_start(
                    out=utok[:],
                    in_=bass.AP(u_d, (c2 * LC + kk * 128) * D, [[D, 128], [1, D]])),
                  waits=[(s_pe, s_pe.n), (s_dve, roll["tailevac_dve"])],
                  incs=[(s_du, 16)])
                duv = s_du.n
                for half in range(2):
                    for j in range(4):
                        dt_ = half * 4 + j
                        src = (atp if dt_ < 4 else btp)
                        E("tensor", lambda e, d_=dt_, jj=j, kk=k, ss=src:
                          e.transpose(
                            ps_tr[:, jj * 128:(jj + 1) * 128],
                            bass.AP(ss.tensor,
                                    ss.offset + (d_ % 4) * LC + kk * 128,
                                    [list(ss.ap[0]), [1, 128]]),
                            eye[:]),
                          waits=[(s_dve, M["e3_marks"][dt_]), (s_dve, s_dve.n),
                                 (s_act, s_act.n)],
                          incs=[(s_pe, 1)] if j == 3 else ())
                    pev = s_pe.n
                    E("vector", lambda e, hh=half: e.tensor_tensor(
                        outst[:, hh * 512:(hh + 1) * 512], ps_tr[:, 0:512],
                        utok[:, hh * 512:(hh + 1) * 512], OP.add),
                      waits=[(s_pe, pev), (s_du, duv), (s_do, roll["outdma"])],
                      incs=[(s_dve, 1)])
                roll["tailevac_dve"] = s_dve.n
                roll["outtr_pe"] = s_pe.n
                dvv = s_dve.n
                E("sync", lambda e, kk=k, c2=c: e.dma_start(
                    out=bass.AP(o_d, (c2 * LC + kk * 128) * D, [[D, 128], [1, D]]),
                    in_=outst[:]),
                  waits=[(s_dve, dvv)], incs=[(s_do, 16)])
                roll["outdma"] = s_do.n
            E("vector", lambda e: e.memset(
                bass.AP(atp.tensor, atp.offset, [list(atp.ap[0]), [257, 4]]), 0.0),
              waits=[(s_pe, roll["outtr_pe"])], incs=[(s_dve, 1)])
            roll["azero_dve"] = s_dve.n

        def emit_yainit(c, M):
            M["yainit_act"] = [0] * DT
            for dt_ in range(DT):
                E("scalar", lambda e, d_=dt_: e.activation(
                    ya[:, d_ * LC:(d_ + 1) * LC], xnT[:, d_ * LC:(d_ + 1) * LC],
                    AF.Identity, bias=b2c[:, d_:d_ + 1], scale=Dg[:, d_:d_ + 1]),
                  waits=[(s_dve, max(M["ya_free_dve"][dt_], M["xq_dve"]))],
                  incs=[(s_act, 1)])
                M["yainit_act"][dt_] = s_act.n

        def emit_main_rest(c, M, start_pr=0, delay=0, lag=1, hoist=None):
            M["mm1_marks"] = [0] * HT
            M["pre_marks"] = [0] * HT
            M["gelu_marks"] = [0] * HT
            hsteps = dict(zip((16, 24, 32, 40), hoist)) if hoist else {}
            npds = (8 - start_pr) * DT
            doneA = 0
            doneB = 0
            i = 0
            for pr in range(start_pr, 8):
                for db in range(DT):
                    emit_pd(c, M, pr, db)
                    i += 1
                    if i in hsteps:
                        hsteps[i]()
                    if i > delay:
                        want = min(HT, ((i - delay) * HT) // (npds - delay - 4) + 1)
                        while doneA < want:
                            while doneB < doneA - 1:
                                emit_stage2B(c, M, doneB)
                                doneB += 1
                            emit_stage2A(c, M, doneA)
                            doneA += 1
                        while doneB < max(0, doneA - lag):
                            emit_stage2B(c, M, doneB)
                            doneB += 1
                if pr + 2 < 8:
                    M["bc_dma"](pr + 2)
            while doneA < HT:
                emit_stage2A(c, M, doneA)
                doneA += 1
            while doneB < HT:
                emit_stage2B(c, M, doneB)
                doneB += 1
            M["mamba_pool"] = s_pool.n

        A, asteps = prepA_steps(0)
        for fn in asteps:
            fn()
        emit_quant(am1_done)
        wq_done_dve = s_dve.n
        wq1_written = s_dwq.n
        M = emit_prep(0, A)
        M["ya_free_dve"] = [0] * DT
        emit_yainit(0, M)
        An, asteps = prepA_steps(1)
        emit_main_rest(0, M, 0, hoist=asteps)
        for c in range(NCH):
            emit_q2_round(c, M)
            emit_mm2(c, M)
            cb = make_tailcb(c, M)
            if c + 1 < NCH:
                Mn = emit_prep(c + 1, An, tailcb=cb)
                while M["e3_marks"][DT - 1] == 0:
                    cb()
                emit_out(c, M)
                Mn["ya_free_dve"] = list(M["e3_marks"])
                emit_yainit(c + 1, Mn)
                if c + 2 < NCH:
                    An, asteps = prepA_steps(c + 2)
                else:
                    asteps = None
                emit_main_rest(c + 1, Mn, 0, hoist=asteps)
                M = Mn
            else:
                for _ in range(DT):
                    cb()
                emit_out(c, M)

        E("gpsimd", lambda e: e.engine_nop(), waits=[(s_do, s_do.n)])

        import os
        if os.environ.get("BASS_PRINT_SBUF"):
            print("sbuf remaining/partition:", nc.sbuf_bytes_remaining,
                  "psum banks remaining:", nc.psum_banks_remaining)
        with nc.Block() as block:
            em.flush(block, max_seq=max_seq)

    return nc


def kernel(L0_attn_u, L0_attn_W_x, L0_attn_W_dt, L0_attn_b_dt, L0_attn_log_A,
           L0_attn_D_param, L0_ffn_w1, L0_ffn_b1, L0_ffn_w2, L0_ffn_b2,
           L0_ffn_gamma, L0_ffn_beta):
    u = np.asarray(L0_attn_u, dtype=np.float32)
    wx = np.asarray(L0_attn_W_x, dtype=np.float32)
    wdt = np.asarray(L0_attn_W_dt, dtype=np.float32)
    bdt = np.asarray(L0_attn_b_dt, dtype=np.float32)
    logA = np.asarray(L0_attn_log_A, dtype=np.float32)
    Dp = np.asarray(L0_attn_D_param, dtype=np.float32)
    w1 = np.asarray(L0_ffn_w1, dtype=np.float32)
    b1 = np.asarray(L0_ffn_b1, dtype=np.float32)
    w2 = np.asarray(L0_ffn_w2, dtype=np.float32)
    b2 = np.asarray(L0_ffn_b2, dtype=np.float32)
    gam = np.asarray(L0_ffn_gamma, dtype=np.float32)
    bet = np.asarray(L0_ffn_beta, dtype=np.float32)

    wxT = np.ascontiguousarray(wx.T)
    wdtTb = np.ascontiguousarray(np.concatenate([wdt.T, bdt[None, :]], axis=0))
    logA_r = np.ascontiguousarray(
        logA.reshape(DT, 128, N).transpose(1, 0, 2).reshape(128, DT * N))
    r8 = lambda v: np.ascontiguousarray(v.reshape(DT, 128).T)
    base = dict(
        wxT=wxT, wdtTb=wdtTb, logA_r=logA_r,
        D_r=r8(Dp), gamma_r=r8(gam), beta_r=r8(bet), b2_r=r8(b2),
        b1_r=np.ascontiguousarray(b1.reshape(HT, 128).T),
        w1T=np.ascontiguousarray(w1.T), w2T=np.ascontiguousarray(w2.T),
        eye=np.eye(128, dtype=np.float32),
        ones_row=np.ones((1, 128), dtype=np.float32),
        ones_col=np.ones((128, 1), dtype=np.float32),
        sgnm=np.full((128, 1), np.int32(-2147483648), dtype=np.int32),
        cst=np.broadcast_to(np.array([MAGIC, EPS], dtype=np.float32),
                            (128, 2)).copy(),
    )
    nc = build_nc()
    in_maps = [dict(base, u=np.ascontiguousarray(u[i])) for i in range(8)]
    res = run_bass_kernel_spmd(nc, in_maps, core_ids=list(range(8)))
    return np.stack([res.results[i]["o"] for i in range(8)], axis=0)
